# version 04
# speedup vs baseline: 1.2741x; 1.0280x over previous
import math
import numpy as np

import concourse.bass as bass
import concourse.mybir as mybir
from concourse.bass_utils import run_bass_kernel_spmd

F32 = mybir.dt.float32
BF16 = mybir.dt.bfloat16
AF = mybir.ActivationFunctionType
ALU = mybir.AluOpType
AX = mybir.AxisListType

P = 128

ENGINE_INFO = {
    "pe": ("tensor", "tensor"),
    "act": ("scalar", "scalar"),
    "dve": ("vector", "vector"),
    "pool": ("gpsimd", "gpsimd"),
    "sp": ("sync", "sync"),
}

SAME_ENGINE_SYNC = True


class _Op:
    __slots__ = ("idx", "eng", "fn", "reads", "writes", "name", "is_dma",
                 "deps", "inc_val", "waits", "dma_slot", "dma_cycle")

    def __init__(self, idx, eng, fn, reads, writes, name, is_dma):
        self.idx = idx
        self.eng = eng
        self.fn = fn
        self.reads = reads
        self.writes = writes
        self.name = name
        self.is_dma = is_dma
        self.deps = []
        self.inc_val = None
        self.waits = []
        self.dma_slot = None
        self.dma_cycle = None


class Prog:
    def __init__(self, nc: bass.Bass):
        self.nc = nc
        self.ops = []
        self._lw = {}
        self._rd = {}
        self.sems = {}
        for e in ENGINE_INFO:
            self.sems[e] = nc.alloc_semaphore(f"s_{e}")
        self.DMA_SLOTS = {"sp": 12, "pool": 12, "act": 4}
        self.dsems = {q: [nc.alloc_semaphore(f"d_{q}{i}") for i in range(n)]
                      for q, n in self.DMA_SLOTS.items()}
        self._elast = {}
        self._qlast = {}
        self._qcount = {}

    def emit(self, eng, fn, reads=(), writes=(), name="", dma=False, extra_deps=()):
        i = len(self.ops)
        op = _Op(i, eng, fn, tuple(reads), tuple(writes), name, dma)
        deps = set(extra_deps)
        for k in op.reads:
            j = self._lw.get(k)
            if j is not None:
                deps.add(j)
        for k in op.writes:
            j = self._lw.get(k)
            if j is not None:
                deps.add(j)
            deps.update(self._rd.get(k, ()))
        deps.discard(i)
        op.deps = sorted(deps)
        for k in op.reads:
            self._rd.setdefault(k, set()).add(i)
        for k in op.writes:
            self._lw[k] = i
            self._rd[k] = set()
        self.ops.append(op)
        if dma:
            self._qlast.setdefault(eng, {})
            R = self.DMA_SLOTS[eng]
            self._qlast[eng][self._qcount.get(eng, 0) % R] = i
            self._qcount[eng] = self._qcount.get(eng, 0) + 1
        else:
            self._elast[eng] = i
        return i

    def fence(self):
        deps = set()
        for e, i in self._elast.items():
            deps.add(i)
        for q, slots in self._qlast.items():
            deps.update(slots.values())
        nc = self.nc
        eng_attr = {e: a for e, (a, _) in ENGINE_INFO.items()}
        for e in ENGINE_INFO:
            self.emit(e, lambda eng, a=eng_attr[e]: getattr(nc, a).nop(),
                      name=f"fence_{e}", extra_deps=sorted(deps))

    def dma(self, queue, out_ap, in_ap, reads=(), writes=(), name="dma", **kw):
        attr = ENGINE_INFO[queue][0]

        def fn(eng, nc=self.nc, out_ap=out_ap, in_ap=in_ap, kw=kw):
            return getattr(nc, attr).dma_start(out_ap, in_ap, **kw)

        return self.emit(queue, fn, reads, writes, name, dma=True)

    def finalize(self):
        nc = self.nc
        need_inc = set()
        filt_deps = {}
        for op in self.ops:
            fd = []
            for j in op.deps:
                dj = self.ops[j]
                if dj.eng == op.eng and not dj.is_dma and not op.is_dma:
                    if dj.eng == "pe":
                        continue
                    if not SAME_ENGINE_SYNC:
                        continue
                fd.append(j)
            filt_deps[op.idx] = fd
            need_inc.update(fd)
        eng_count = {e: 0 for e in ENGINE_INFO}
        dma_count = {q: 0 for q in self.dsems}
        for op in self.ops:
            if op.is_dma:
                i = dma_count[op.eng]
                dma_count[op.eng] = i + 1
                R = self.DMA_SLOTS[op.eng]
                op.dma_slot = i % R
                op.dma_cycle = i // R
                op.inc_val = (op.dma_cycle + 1) * 16
            elif op.idx in need_inc:
                eng_count[op.eng] += 1
                op.inc_val = eng_count[op.eng]
        waited = {e: {} for e in ENGINE_INFO}
        for op in self.ops:
            wl = {}

            def add_wait(sem, v):
                if wl.get(sem.name, (None, 0))[1] < v:
                    wl[sem.name] = (sem, v)

            if op.is_dma and op.dma_cycle > 0:
                add_wait(self.dsems[op.eng][op.dma_slot], op.dma_cycle * 16)
            for j in filt_deps[op.idx]:
                dj = self.ops[j]
                if dj.is_dma:
                    add_wait(self.dsems[dj.eng][dj.dma_slot], dj.inc_val)
                else:
                    add_wait(self.sems[dj.eng], dj.inc_val)
            final = []
            wmap = waited[op.eng]
            for semname, (sem, v) in wl.items():
                if wmap.get(semname, 0) >= v:
                    continue
                wmap[semname] = v
                final.append((sem, v))
            op.waits = final
        self._final_dma = {}
        for q, n in dma_count.items():
            R = self.DMA_SLOTS[q]
            self._final_dma[q] = [((n - 1 - s) // R + 1) * 16 if s < min(n, R) else 0
                                  for s in range(R)]
        self._final_eng = dict(eng_count)

    def replay(self):
        nc = self.nc
        per_eng = {e: [] for e in ENGINE_INFO}
        for op in self.ops:
            per_eng[op.eng].append(op)

        with nc.Block() as block:
            for ename, (attr, battr) in ENGINE_INFO.items():
                ops = per_eng[ename]

                def body(eng, ename=ename, ops=ops):
                    for op in ops:
                        for (sem, v) in op.waits:
                            eng.wait_ge(sem, v)
                        inst = op.fn(eng)
                        if op.inc_val is not None:
                            if inst is None:
                                raise RuntimeError(f"op {op.name} returned no inst")
                            if op.is_dma:
                                inst.then_inc(self.dsems[ename][op.dma_slot], 16)
                            else:
                                inst.then_inc(self.sems[ename], 1)
                    if ename in self.dsems:
                        for s, v in enumerate(self._final_dma.get(ename, [])):
                            if v:
                                eng.wait_ge(self.dsems[ename][s], v)

                getattr(block, battr)(body)

    def build(self):
        self.finalize()
        self.replay()


def ceil_div(a, b):
    return (a + b) // b if a % b else a // b



DM = 1024
NB = 2
SEQ = 2048
TOK = NB * SEQ
TC = 512
NCH = TOK // TC
CPB = SEQ // TC
DT = DM // P


class Ctx:
    def __init__(self, nc, prog):
        self.nc = nc
        self.p = prog
        self.ident = nc.alloc_sbuf_tensor("c_ident", [P, P], F32).ap()
        self.ident_bf = nc.alloc_sbuf_tensor("c_identb", [P, P], BF16).ap()
        self.ones_row = nc.alloc_sbuf_tensor("c_ones", [1, P], F32).ap()
        id_d = nc.inline_tensor(np.eye(P, dtype=np.float32), name="c_ident_d")
        prog.dma("sp", self.ident, id_d.ap(), writes=["c_ident"], name="ld_ident")
        prog.dma("pool", self.ident_bf, id_d.ap(), writes=["c_identb"], name="ld_identb")
        on_d = nc.inline_tensor(np.ones((1, P), dtype=np.float32), name="c_ones_d")
        prog.dma("sp", self.ones_row, on_d.ap(), writes=["c_ones"], name="ld_ones")
        self.ones_col_bf = nc.alloc_sbuf_tensor("c_onesb", [P, 1], BF16).ap()
        prog.emit("dve", lambda e: nc.vector.memset(self.ones_col_bf, 1.0),
                  writes=["c_onesb"], name="set_onesb")
        self.ones_col_f = nc.alloc_sbuf_tensor("c_onesf", [P, 1], F32).ap()
        prog.emit("dve", lambda e: nc.vector.memset(self.ones_col_f, 1.0),
                  writes=["c_onesf"], name="set_onesf")
        cm = np.where(np.arange(P)[None, :] <= np.arange(P)[:, None], 0.0, -3e38).astype(np.float32)
        cm_d = nc.inline_tensor(cm, name="c_cmask_d")
        self.cmask = nc.alloc_sbuf_tensor("c_cmask", [P, P], F32).ap()
        prog.dma("sp", self.cmask, cm_d.ap(), writes=["c_cmask"], name="ld_cmask")
        oh32_np = np.zeros((32, P * 32), dtype=np.float32)
        for n in range(32):
            oh32_np[n, n * P:(n + 1) * P] = 1.0
        oh32_d = nc.inline_tensor(oh32_np, name="c_oh32_d")
        self.oh32 = nc.alloc_sbuf_tensor("c_oh32", [32, P * 32], BF16).ap()
        prog.dma("pool", self.oh32, oh32_d.ap(), writes=["c_oh32"], name="ld_oh32")
        oh8_np = np.zeros((8, P * 8), dtype=np.float32)
        for n in range(8):
            oh8_np[n, n * P:(n + 1) * P] = 1.0
        oh8_d = nc.inline_tensor(oh8_np, name="c_oh8_d")
        self.oh8 = nc.alloc_sbuf_tensor("c_oh8", [8, P * 8], BF16).ap()
        prog.dma("pool", self.oh8, oh8_d.ap(), writes=["c_oh8"], name="ld_oh8")
        self.ps = [nc.alloc_psum_tensor(f"psf{i}", [P, 512], F32).ap() for i in range(6)]
        self.psb = [nc.alloc_psum_tensor(f"psb{i}", [P, 1024], BF16).ap() for i in range(2)]
        self._ps_rr = 0
        self._psb_rr = 0
        self._uniq = 0
        self.init_arena()

    def uname(self, base):
        self._uniq += 1
        return f"{base}_{self._uniq}"

    def psum(self):
        i = self._ps_rr % 4
        self._ps_rr += 1
        return self.ps[i], ("psf", i)

    def psum_bf(self):
        i = self._psb_rr % 2
        self._psb_rr += 1
        return self.psb[i], ("psb", i)

    def sbuf(self, name, shape, dtype=F32):
        return self.nc.alloc_sbuf_tensor(name, shape, dtype).ap()

    def init_arena(self):
        nc = self.nc
        avail = nc.sbuf_bytes_remaining - 256
        self._arena_res = nc.alloc_sbuf_tensor("ARENA", [P, avail], mybir.dt.uint8)
        self._arena_base = nc.lookup_mloc(self._arena_res).addr
        self._arena_size = avail
        self._arena_off = 0
        self._arena_gen = 0
        self._scratch = {}

    def arena_reset(self, fence=True):
        if fence:
            self.p.fence()
        self._arena_off = 0
        self._arena_gen += 1
        self._scratch = {}

    def arena_mark(self):
        return (set(self._scratch.keys()), self._arena_off)

    def arena_to(self, mark, fence=True):
        keepset, off = mark
        olds = {k: v for k, v in self._scratch.items() if k in keepset}
        self.arena_reset(fence=fence)
        self._scratch.update(olds)
        self._arena_off = off

    def psum_fixed(self, i):
        return self.ps[i], ("psf", i)

    def constcol(self, value):
        if not hasattr(self, "_cc"):
            self._cc = {}
        if value not in self._cc:
            i = len(self._cc)
            ap = self.nc.alloc_sbuf_tensor(f"c_cc{i}", [P, 1], F32).ap()
            self.p.emit("dve", lambda e, ap=ap, value=value: self.nc.vector.memset(ap, float(value)),
                        writes=[("c_cc", value)], name=f"set_cc{i}")
            self._cc[value] = (ap, ("c_cc", value))
        return self._cc[value]

    def scratch(self, tag, shape, dtype=F32):
        if not hasattr(self, "_scratch_global"):
            self._scratch_global = {}
        key = (tag, tuple(shape), dtype)
        if key not in self._scratch:
            nbytes = int(np.prod(shape[1:])) * mybir.dt.size(dtype)
            off = (self._arena_off + 31) & ~31
            assert off + nbytes <= self._arena_size, (
                f"arena overflow at {tag}: {off}+{nbytes} > {self._arena_size}")
            gkey = (tag, tuple(shape), dtype, off)
            if gkey in self._scratch_global:
                self._scratch[key] = self._scratch_global[gkey]
            else:
                t = self.nc.alloc_sbuf_tensor_at(
                    f"{tag}@g{self._arena_gen}", shape, dtype,
                    offset=self._arena_base + off)
                self._scratch[key] = t.ap()
                self._scratch_global[gkey] = self._scratch[key]
            self._arena_off = off + nbytes
        return self._scratch[key]

    def load_w_bf16(self, name, dram_ap, K, M):
        nt = K // P
        buf = self.scratch(name, [P, nt * M], BF16)
        keys = []
        for kt in range(nt):
            k = (name, kt)
            keys.append(k)
            self.p.dma("pool", buf[:, kt * M:(kt + 1) * M],
                       dram_ap[kt * P:(kt + 1) * P, :],
                       writes=[k], name=f"{name}_ld{kt}")

        def tile(kt, m0, m1):
            return buf[:, kt * M + m0: kt * M + m1]

        return buf, tile, keys

    def load_w_bf16_T(self, name, dram_ap, R, C):
        nc, p = self.nc, self.p
        rt_n, ct_n = ceil_div(R, P), ceil_div(C, P)
        buf = self.scratch(name, [P, ct_n * R], BF16)
        tmp = self.scratch(f"wTtmp{C}", [P, C], BF16)
        keys = []
        for rt in range(rt_n):
            rp = min(P, R - rt * P)
            tk = (f"wTtmp{C}",)
            p.dma("pool", tmp[:rp, :], dram_ap[rt * P:rt * P + rp, :],
                  writes=[tk], name=f"{name}_ldT{rt}")
            for ct in range(ct_n):
                cp = min(P, C - ct * P)
                pb, pbk = self.psum_bf()
                p.emit("pe", lambda e, tmp=tmp, rp=rp, cp=cp, ct=ct, pb=pb:
                       nc.tensor.transpose(pb[0:cp, 0:rp], tmp[:rp, ct * P:ct * P + cp], self.ident_bf[:rp, :rp]),
                       reads=[tk, "c_identb"], writes=[pbk], name=f"{name}_tr{rt}_{ct}")
                k = (name, ct, rt)
                keys.append(k)
                p.emit("act", lambda e, pb=pb, buf=buf, ct=ct, rt=rt, rp=rp, cp=cp:
                       nc.scalar.copy(buf[0:cp, ct * R + rt * P: ct * R + rt * P + rp], pb[0:cp, 0:rp]),
                       reads=[pbk], writes=[k], name=f"{name}_ev{rt}_{ct}")

        def tile(ct, r0, r1):
            return buf[:, ct * R + r0: ct * R + r1]

        return buf, tile, keys

    def vec_to_cols(self, name, dram_ap, D, scale=None, bias=None):
        nc, p = self.nc, self.p
        if D < P:
            nt = 1
            tmp = self.scratch(f"v2ctmp_s{D}", [1, D], F32)
            p.dma("sp", tmp, dram_ap.rearrange("(a b) -> a b", b=D),
                  reads=[("v2ctmp", D)], writes=[(name, "tmp"), ("v2ctmp", D)], name=f"{name}_ld")
            pb, pbk = self.psum()
            p.emit("pe", lambda e, pb=pb, tmp=tmp, D=D: nc.tensor.transpose(pb[0:D, 0:1], tmp, self.ident[:1, :1]),
                   reads=[(name, "tmp"), ("v2ctmp", D), "c_ident"], writes=[pbk], name=f"{name}_tr")
            cols = self.scratch(name, [D, 1], F32)
            sc = 1.0 if scale is None else scale
            bi = 0.0 if bias is None else bias
            p.emit("act", lambda e: nc.scalar.activation(cols, pb[0:D, 0:1], AF.Identity, bias=bi, scale=sc),
                   reads=[pbk], writes=[name], name=f"{name}_ev")
            return cols, name
        nt = D // P
        tmp = self.scratch(f"v2ctmp{nt}", [nt, P], F32)
        p.dma("sp", tmp, dram_ap.rearrange("(a b) -> a b", b=P),
              reads=[("v2ctmp", nt)], writes=[(name, "tmp"), ("v2ctmp", nt)], name=f"{name}_ld")
        pb, pbk = self.psum()
        p.emit("pe", lambda e, pb=pb, tmp=tmp, nt=nt: nc.tensor.transpose(pb[:, 0:nt], tmp, self.ident[:nt, :nt]),
               reads=[(name, "tmp"), ("v2ctmp", nt), "c_ident"], writes=[pbk], name=f"{name}_tr")
        cols = self.scratch(name, [P, nt], F32)
        if scale is None and bias is None:
            p.emit("act", lambda e: nc.scalar.copy(cols, pb[:, 0:nt]),
                   reads=[pbk], writes=[name], name=f"{name}_ev")
        else:
            sc = 1.0 if scale is None else scale
            bi = 0.0 if bias is None else bias
            p.emit("act", lambda e: nc.scalar.activation(cols, pb[:, 0:nt], AF.Identity, bias=bi, scale=sc),
                   reads=[pbk], writes=[name], name=f"{name}_ev")
        return cols, name

    def mm(self, psum_ap, psum_key, lhsT_list, rhs_list, name="mm", extra_reads=()):
        nc, p = self.nc, self.p
        n = len(lhsT_list)
        assert len(rhs_list) == n
        for i in range(n):
            la, lk = lhsT_list[i]
            ra, rk = rhs_list[i]
            p.emit("pe", lambda e, la=la, ra=ra, i=i:
                   nc.tensor.matmul(psum_ap, la, ra, start=(i == 0), stop=(i == n - 1)),
                   reads=[lk, rk] + list(extra_reads), writes=[psum_key],
                   name=f"{name}_{i}")

    def bcast_row(self, row_ap, row_key, n_free, name="bc", nrows=P):
        nc, p = self.nc, self.p
        pb, pbk = self.psum()
        p.emit("pe", lambda e: nc.tensor.matmul(pb[0:nrows, 0:n_free], self.ones_row[:, 0:nrows], row_ap, start=True, stop=True),
               reads=[row_key, "c_ones"], writes=[pbk], name=name)
        return pb[0:nrows, 0:n_free], pbk



def emit_pre_ln(ctx, x_tile, out_tile, T=TC, name="ln", out2_tile=None,
                gamma_beta=None, eps=1e-5):
    nc, p = ctx.nc, ctx.p
    un = name
    sq = ctx.scratch(un + "_sq", [P, DT * T], BF16)
    for j in range(DT):
        xa, xk = x_tile(j)
        p.emit("act", lambda e, xa=xa, j=j: nc.scalar.activation(sq[:, j * T:(j + 1) * T], xa, AF.Square),
               reads=[xk], writes=[(un, "sq", j)], name=f"{un}_sq{j}")
    pb, pbk = ctx.psum()
    pb2, pbk2 = ctx.psum()
    ctx.mm(pb[0:1, 0:T], pbk, [(ctx.ones_col_f, "c_onesf")] * DT,
           [x_tile(j) for j in range(DT)], name=f"{un}_sum")
    ctx.mm(pb2[0:1, 0:T], pbk2, [(ctx.ones_col_bf, "c_onesb")] * DT,
           [(sq[:, j * T:(j + 1) * T], (un, "sq", j)) for j in range(DT)], name=f"{un}_ssq")
    rows = ctx.scratch(un + "_rows", [1, 2 * T], F32)
    m_row = rows[:, 0:T]; v_row = rows[:, T:2 * T]; r_row = v_row
    p.emit("dve", lambda e: nc.vector.tensor_scalar_mul(m_row, pb[0:1, 0:T], 1.0 / DM),
           reads=[pbk], writes=[(un, "m")], name=f"{un}_m")
    p.emit("dve", lambda e: nc.vector.scalar_tensor_tensor(v_row, m_row, -1.0, m_row, ALU.mult, ALU.mult),
           reads=[(un, "m")], writes=[(un, "mm")], name=f"{un}_mm")
    p.emit("dve", lambda e: nc.vector.scalar_tensor_tensor(v_row, pb2[0:1, 0:T], 1.0 / DM, v_row, ALU.mult, ALU.add),
           reads=[pbk2, (un, "mm")], writes=[(un, "v")], name=f"{un}_v")
    epsc, epsk = ctx.constcol(float(eps))
    p.emit("act", lambda e: nc.scalar.activation(r_row, v_row, AF.Sqrt, bias=epsc[0:1, :]),
           reads=[(un, "v"), epsk], writes=[(un, "sd"), (un, "v")], name=f"{un}_sd")
    p.emit("dve", lambda e: nc.vector.reciprocal(r_row, r_row),
           reads=[(un, "sd")], writes=[(un, "r")], name=f"{un}_r")
    mb, mbk = ctx.bcast_row(m_row, (un, "m"), T, name=f"{un}_bm")
    rb, rbk = ctx.bcast_row(r_row, (un, "r"), T, name=f"{un}_br")
    for j in range(DT):
        xa, xk = x_tile(j)
        t1 = ctx.scratch(un + "_t1s", [P, T], F32)
        p.emit("dve", lambda e, xa=xa, t1=t1: nc.vector.tensor_sub(t1, xa, mb),
               reads=[xk, mbk], writes=[(un, "t1s")], name=f"{un}_sub{j}")
        d0, k0 = out_tile(j)
        if gamma_beta is None:
            p.emit("dve", lambda e, t1=t1, d0=d0: nc.vector.tensor_mul(d0, t1, rb),
                   reads=[(un, "t1s"), rbk], writes=[k0], name=f"{un}_mul{j}")
        else:
            g_cols, g_key, b_cols, b_key = gamma_beta
            t2 = ctx.scratch(un + "_t2s", [P, T], F32)
            p.emit("dve", lambda e, t1=t1, t2=t2: nc.vector.tensor_mul(t2, t1, rb),
                   reads=[(un, "t1s"), rbk], writes=[(un, "t2s")], name=f"{un}_mul{j}")
            p.emit("dve", lambda e, t2=t2, d0=d0, j=j:
                   nc.vector.tensor_scalar(d0, t2, g_cols[:, j:j + 1], b_cols[:, j:j + 1], ALU.mult, ALU.add),
                   reads=[(un, "t2s"), g_key, b_key], writes=[k0], name=f"{un}_gb{j}")
        if out2_tile is not None:
            d1, k1 = out2_tile(j)
            p.emit("act", lambda e, d0=d0, d1=d1: nc.scalar.copy(d1, d0),
                   reads=[k0], writes=[k1], name=f"{un}_cp{j}")



def emit_l2_ffn(ctx, xin_d, xout_d, W):
    nc, p = ctx.nc, ctx.p
    _, wr_t, _ = ctx.load_w_bf16_T("l2f_wr", W["W_r"], DM, DM)
    _, wk_t, _ = ctx.load_w_bf16_T("l2f_wk", W["W_k"], DM, DM)
    _, wv_t, _ = ctx.load_w_bf16_T("l2f_wv", W["W_v"], DM, DM)
    mur, mur_k = ctx.vec_to_cols("l2f_mur", W["mu_r"], DM)
    muk, muk_k = ctx.vec_to_cols("l2f_muk", W["mu_k"], DM)
    omr, omr_k = ctx.vec_to_cols("l2f_omr", W["mu_r"], DM, scale=-1.0, bias=1.0)
    omk, omk_k = ctx.vec_to_cols("l2f_omk", W["mu_k"], DM, scale=-1.0, bias=1.0)

    W1 = TC + 1
    bufs = []
    for par in range(2):
        bufs.append(dict(
            xb=ctx.scratch(f"l2f_xb{par}", [P, DT * TC], F32),
            xn=ctx.scratch(f"l2f_xn{par}", [P, DT * W1], BF16),
            xr=ctx.scratch(f"l2f_xr{par}", [P, DT * TC], BF16),
            xk=ctx.scratch(f"l2f_xk{par}", [P, DT * TC], BF16),
            ka=ctx.scratch(f"l2f_ka{par}", [P, DT * TC], BF16),
            r=ctx.scratch(f"l2f_r{par}", [P, DT * TC], BF16),
        ))

    for ch in range(NCH):
        b, bprev = bufs[ch % 2], bufs[(ch + 1) % 2]
        pre = f"l2f{ch}"
        t0 = ch * TC
        first = (ch % CPB == 0)
        for j in range(DT):
            p.dma("sp", b["xb"][:, j * TC:(j + 1) * TC],
                  xin_d[j * P:(j + 1) * P, t0:t0 + TC], writes=[("l2f_xb", ch % 2, j)], name=f"{pre}_ld{j}")
        xt = lambda j: (b["xb"][:, j * TC:(j + 1) * TC], ("l2f_xb", ch % 2, j))
        emit_pre_ln(ctx, xt,
                    lambda j: (b["xn"][:, j * W1 + 1:(j + 1) * W1], ("l2f_xn", ch % 2, j)),
                    name="l2fln")
        for j in range(DT):
            hk = ("l2f_xnh", ch % 2, j)
            if first:
                p.emit("dve", lambda e, j=j, b=b: nc.vector.memset(b["xn"][:, j * W1:j * W1 + 1], 0.0),
                       writes=[hk], name=f"{pre}_hz{j}")
            else:
                p.emit("dve", lambda e, j=j, b=b, bprev=bprev:
                       nc.vector.tensor_copy(b["xn"][:, j * W1:j * W1 + 1], bprev["xn"][:, (j + 1) * W1 - 1:(j + 1) * W1]),
                       reads=[("l2f_xn", (ch + 1) % 2, j)], writes=[hk], name=f"{pre}_hc{j}")
        for j in range(DT):
            v = b["xn"][:, j * W1 + 1:(j + 1) * W1]
            vp = b["xn"][:, j * W1:(j + 1) * W1 - 1]
            vk = [("l2f_xn", ch % 2, j), ("l2f_xnh", ch % 2, j)]
            tg1 = f"l2f_t1_{ch % 2}"
            t1 = ctx.scratch(tg1, [P, TC], F32)
            p.emit("dve", lambda e, vp=vp, t1=t1, j=j: nc.vector.tensor_scalar_mul(t1, vp, omr[:, j:j + 1]),
                   reads=vk + [omr_k], writes=[tg1], name=f"{pre}_mr{j}")
            p.emit("dve", lambda e, v=v, t1=t1, j=j, b=b:
                   nc.vector.scalar_tensor_tensor(b["xr"][:, j * TC:(j + 1) * TC], v, mur[:, j:j + 1], t1, ALU.mult, ALU.add),
                   reads=vk + [mur_k, tg1], writes=[("l2f_xr", ch % 2, j)], name=f"{pre}_xr{j}")
            tg2 = f"l2f_t2_{ch % 2}"
            t2 = ctx.scratch(tg2, [P, TC], F32)
            p.emit("dve", lambda e, vp=vp, t2=t2, j=j: nc.vector.tensor_scalar_mul(t2, vp, omk[:, j:j + 1]),
                   reads=vk + [omk_k], writes=[tg2], name=f"{pre}_mk{j}")
            p.emit("dve", lambda e, v=v, t2=t2, j=j, b=b:
                   nc.vector.scalar_tensor_tensor(b["xk"][:, j * TC:(j + 1) * TC], v, muk[:, j:j + 1], t2, ALU.mult, ALU.add),
                   reads=vk + [muk_k, tg2], writes=[("l2f_xk", ch % 2, j)], name=f"{pre}_xk{j}")
        xr_rhs = [(b["xr"][:, j * TC:(j + 1) * TC], ("l2f_xr", ch % 2, j)) for j in range(DT)]
        xk_rhs = [(b["xk"][:, j * TC:(j + 1) * TC], ("l2f_xk", ch % 2, j)) for j in range(DT)]
        for m in range(DT):
            pbr, pbrk = ctx.psum()
            ctx.mm(pbr, pbrk, [(wr_t(j, m * P, (m + 1) * P), ("l2f_wr", j, m)) for j in range(DT)],
                   xr_rhs, name=f"{pre}_mr{m}")
            p.emit("act", lambda e, pbr=pbr, m=m, b=b:
                   nc.scalar.activation(b["r"][:, m * TC:(m + 1) * TC], pbr, AF.Sigmoid),
                   reads=[pbrk], writes=[("l2f_r", ch % 2, m)], name=f"{pre}_sg{m}")
            pbK, pbKk = ctx.psum()
            ctx.mm(pbK, pbKk, [(wk_t(j, m * P, (m + 1) * P), ("l2f_wk", j, m)) for j in range(DT)],
                   xk_rhs, name=f"{pre}_mk{m}")
            tg3 = f"l2f_t3_{ch % 2}"
            t3 = ctx.scratch(tg3, [P, TC], F32)
            p.emit("dve", lambda e, pbK=pbK, t3=t3: nc.vector.tensor_scalar_max(t3, pbK, 0.0),
                   reads=[pbKk], writes=[tg3], name=f"{pre}_rl{m}")
            p.emit("act", lambda e, t3=t3, m=m, b=b:
                   nc.scalar.activation(b["ka"][:, m * TC:(m + 1) * TC], t3, AF.Square),
                   reads=[tg3], writes=[("l2f_ka", ch % 2, m)], name=f"{pre}_sqr{m}")
        ka_rhs = [(b["ka"][:, j * TC:(j + 1) * TC], ("l2f_ka", ch % 2, j)) for j in range(DT)]
        for m in range(DT):
            pbV, pbVk = ctx.psum()
            ctx.mm(pbV, pbVk, [(wv_t(j, m * P, (m + 1) * P), ("l2f_wv", j, m)) for j in range(DT)],
                   ka_rhs, name=f"{pre}_mv{m}")
            tg4 = f"l2f_t4_{ch % 2}"
            t4 = ctx.scratch(tg4, [P, TC], F32)
            p.emit("dve", lambda e, pbV=pbV, t4=t4, m=m, b=b:
                   nc.vector.tensor_mul(t4, b["r"][:, m * TC:(m + 1) * TC], pbV),
                   reads=[pbVk, ("l2f_r", ch % 2, m)], writes=[tg4], name=f"{pre}_g{m}")
            p.emit("dve", lambda e, t4=t4, m=m, b=b:
                   nc.vector.tensor_add(t4, t4, b["xb"][:, m * TC:(m + 1) * TC]),
                   reads=[tg4, ("l2f_xb", ch % 2, m)], writes=[tg4], name=f"{pre}_res{m}")
            p.dma("sp", xout_d[m * P:(m + 1) * P, t0:t0 + TC], t4,
                  reads=[tg4], writes=[("xo", ch, m)], name=f"{pre}_st{m}")



def emit_l0_attn(ctx, xin_d, xout_d, W):
    nc, p = ctx.nc, ctx.p
    DI = 2048
    NT = DI // P
    NS = 16
    KCONV = 4

    keep = []

    def keep_alloc(tag, shape, dtype):
        ap = ctx.scratch(tag, shape, dtype)
        keep.append((tag, tuple(shape), dtype))
        return ap

    bdt, bdt_k = ctx.vec_to_cols("l0a_bdt", W["b_dt"], DI)
    dcol, dcol_k = ctx.vec_to_cols("l0a_D", W["D"], DI)
    cb, cb_k = ctx.vec_to_cols("l0a_cb", W["conv_b"], DI)
    cwraw = ctx.scratch("l0a_cwr", [NT, P * KCONV], F32)
    p.dma("sp", cwraw, W["conv_w"].rearrange("(a b) k -> a (b k)", b=P), writes=[("l0a_cwr",)], name="l0a_cw_ld")
    cw = ctx.scratch("l0a_cw", [P, NT * KCONV], F32)
    for k in range(KCONV):
        pb, pbk = ctx.psum()
        ap_in = cwraw.rearrange("a (b k) -> a b k", k=KCONV)[:, :, k]
        p.emit("pe", lambda e, ap_in=ap_in, pb=pb: nc.tensor.transpose(pb[:, 0:NT], ap_in, ctx.ident[:NT, :NT]),
               reads=[("l0a_cwr",), "c_ident"], writes=[pbk], name=f"l0a_cwtr{k}")
        p.emit("act", lambda e, pb=pb, k=k: nc.scalar.copy(
            cw.rearrange("p (j q) -> p j q", q=KCONV)[:, :, k], pb[:, 0:NT]),
            reads=[pbk], writes=[("l0a_cw", k)], name=f"l0a_cwev{k}")
    cw_col = lambda j, k: cw[:, (j * KCONV + k):(j * KCONV + k) + 1]
    acol = ctx.scratch("l0a_acol", [P, NT * NS], F32)
    for j in range(NT):
        p.dma("sp", acol[:, j * NS:(j + 1) * NS], W["A_log"][j * P:(j + 1) * P, :],
              writes=[("l0a_acol_raw", j)], name=f"l0a_al{j}")
    p.emit("act", lambda e: nc.scalar.activation(acol, acol, AF.Exp),
           reads=[("l0a_acol_raw", j) for j in range(NT)], writes=[("l0a_acol_e",)], name="l0a_aexp")
    p.emit("dve", lambda e: nc.vector.tensor_scalar_mul(acol, acol, -1.0),
           reads=[("l0a_acol_e",)], writes=[("l0a_acol",)], name="l0a_aneg")
    state = ctx.scratch("l0a_state", [P, NT * NS], F32)
    bc_all = ctx.scratch("l0a_bc", [32, TOK], BF16)
    oh32 = ctx.oh32
    WA = 16 * 2080
    warena = ctx.scratch("l0a_warena", [P, WA], BF16)

    _mark = ctx.arena_mark()

    def phase_reset():
        ctx.arena_to(_mark)

    def load_warena(tag, dram_ap, K, M):
        nt = K // P
        keys = []
        for kt in range(nt):
            k = ("l0a_wa", tag, kt)
            keys.append(k)
            p.dma("pool", warena[:, kt * M:(kt + 1) * M], dram_ap[kt * P:(kt + 1) * P, :],
                  reads=[("l0a_wa_guard",)], writes=[k], name=f"l0a_w{tag}_{kt}")

        def tile(kt, m0, m1):
            return warena[:, kt * M + m0: kt * M + m1]
        return tile, keys

    def free_warena(keys):
        p.emit("dve", lambda e: nc.vector.engine_nop(),
               reads=[], writes=list(keys) + [("l0a_wa_guard",)], name="l0a_wfree")

    xc_d = nc.dram_tensor("l0a_xc", [DI, TOK], BF16, kind="Internal").ap()
    zs_d = nc.dram_tensor("l0a_zs", [DI, TOK], BF16, kind="Internal").ap()
    dr_d = nc.dram_tensor("l0a_dr", [DI, TOK], BF16, kind="Internal").ap()
    yg_d = nc.dram_tensor("l0a_yg", [DI, TOK], BF16, kind="Internal").ap()

    win_t, win_keys = load_warena("win", W["W_in"], DM, 2 * DI)
    HW = TC + 3
    xbuf = [ctx.scratch(f"l0a_xb{par}", [P, DT * TC], F32) for par in range(2)]
    xnb = [ctx.scratch(f"l0a_xn{par}", [P, DT * TC], BF16) for par in range(2)]
    xib = [ctx.scratch(f"l0a_xi{par}", [P, NT * HW], BF16) for par in range(2)]
    for ch in range(NCH):
        par = ch % 2
        pre = f"l0aA{ch}"
        t0 = ch * TC
        first = (ch % CPB == 0)
        for j in range(DT):
            p.dma("sp", xbuf[par][:, j * TC:(j + 1) * TC], xin_d[j * P:(j + 1) * P, t0:t0 + TC],
                  writes=[("l0a_xb", par, j)], name=f"{pre}_ld{j}")
        emit_pre_ln(ctx, lambda j: (xbuf[par][:, j * TC:(j + 1) * TC], ("l0a_xb", par, j)),
                    lambda j: (xnb[par][:, j * TC:(j + 1) * TC], ("l0a_xn", par, j)),
                    name="l0aln")
        xn_rhs = [(xnb[par][:, j * TC:(j + 1) * TC], ("l0a_xn", par, j)) for j in range(DT)]
        for m in range(NT):
            hk = ("l0a_xih", par, m)
            if first:
                p.emit("dve", lambda e, m=m, par=par: nc.vector.memset(xib[par][:, m * HW:m * HW + 3], 0.0),
                       writes=[hk], name=f"{pre}_hz{m}")
            else:
                q = (ch + 1) % 2
                p.emit("dve", lambda e, m=m, par=par, q=q:
                       nc.vector.tensor_copy(xib[par][:, m * HW:m * HW + 3], xib[q][:, (m + 1) * HW - 3:(m + 1) * HW]),
                       reads=[("l0a_xi", q, m)], writes=[hk], name=f"{pre}_hc{m}")
        for m in range(2 * NT):
            pb, pbk = ctx.psum()
            ctx.mm(pb, pbk, [(win_t(j, m * P, (m + 1) * P), ("l0a_wa", "win", j)) for j in range(DT)],
                   xn_rhs, name=f"{pre}_mm{m}")
            if m < NT:
                p.emit("act", lambda e, pb=pb, m=m, par=par:
                       nc.scalar.copy(xib[par][:, m * HW + 3:(m + 1) * HW], pb),
                       reads=[pbk], writes=[("l0a_xi", par, m)], name=f"{pre}_ev{m}")
            else:
                zt = ctx.scratch(f"l0a_z{par}_{m % 2}", [P, TC], BF16)
                p.emit("act", lambda e, pb=pb, zt=zt:
                       nc.scalar.activation(zt, pb, AF.Silu),
                       reads=[pbk], writes=[("l0a_z", par, m % 2)], name=f"{pre}_sz{m}")
                p.dma("sp", zs_d[(m - NT) * P:(m - NT + 1) * P, t0:t0 + TC], zt,
                      reads=[("l0a_z", par, m % 2)], writes=[("zs_d", ch, m - NT)], name=f"{pre}_stz{m}")
        for m in range(NT):
            mq = m % 2
            acc = ctx.scratch(f"l0a_cacc{par}_{mq}", [P, TC], F32)
            xi = xib[par]
            base = m * HW
            p.emit("dve", lambda e, xi=xi, base=base, acc=acc, m=m:
                   nc.vector.tensor_scalar(acc, xi[:, base + 3:base + 3 + TC], cw_col(m, 3), cb[:, m:m + 1], ALU.mult, ALU.add),
                   reads=[("l0a_xi", par, m), ("l0a_xih", par, m), ("l0a_cw", 3), cb_k],
                   writes=[("l0a_cacc", par, mq)], name=f"{pre}_cv3_{m}")
            for k in range(KCONV - 1):
                p.emit("dve", lambda e, xi=xi, base=base, acc=acc, m=m, k=k:
                       nc.vector.scalar_tensor_tensor(acc, xi[:, base + k:base + k + TC], cw_col(m, k), acc, ALU.mult, ALU.add),
                       reads=[("l0a_xi", par, m), ("l0a_xih", par, m), ("l0a_cw", k), ("l0a_cacc", par, mq)],
                       writes=[("l0a_cacc", par, mq)], name=f"{pre}_cv{k}_{m}")
            xct = ctx.scratch(f"l0a_xct{par}_{mq}", [P, TC], BF16)
            p.emit("act", lambda e, acc=acc, xct=xct: nc.scalar.activation(xct, acc, AF.Silu),
                   reads=[("l0a_cacc", par, mq)], writes=[("l0a_xct", par, mq)], name=f"{pre}_cs{m}")
            p.dma("sp", xc_d[m * P:(m + 1) * P, t0:t0 + TC], xct,
                  reads=[("l0a_xct", par, mq)], writes=[("xc_d", ch, m)], name=f"{pre}_stc{m}")
    free_warena(win_keys)
    phase_reset()

    wxp_t, wxp_keys = load_warena("wxp", W["W_xproj"], DI, DI + 2 * NS)
    xcb = [ctx.scratch(f"l0a_xcB{par}", [P, NT * TC], BF16) for par in range(2)]
    for ch in range(NCH):
        par = ch % 2
        pre = f"l0aB{ch}"
        t0 = ch * TC
        for j in range(NT):
            p.dma("sp", xcb[par][:, j * TC:(j + 1) * TC], xc_d[j * P:(j + 1) * P, t0:t0 + TC],
                  reads=[("xc_d", ch, j)], writes=[("l0a_xcB", par, j)], name=f"{pre}_ld{j}")
        xc_rhs = [(xcb[par][:, j * TC:(j + 1) * TC], ("l0a_xcB", par, j)) for j in range(NT)]
        for m in range(NT):
            pb, pbk = ctx.psum()
            ctx.mm(pb, pbk, [(wxp_t(j, m * P, (m + 1) * P), ("l0a_wa", "wxp", j)) for j in range(NT)],
                   xc_rhs, name=f"{pre}_mm{m}")
            drt = ctx.scratch(f"l0a_drt{par}_{m % 2}", [P, TC], BF16)
            p.emit("act", lambda e, pb=pb, drt=drt: nc.scalar.copy(drt, pb),
                   reads=[pbk], writes=[("l0a_drt", par, m % 2)], name=f"{pre}_ev{m}")
            p.dma("sp", dr_d[m * P:(m + 1) * P, t0:t0 + TC], drt,
                  reads=[("l0a_drt", par, m % 2)], writes=[("dr_d", ch, m)], name=f"{pre}_st{m}")
        pb, pbk = ctx.psum()
        ctx.mm(pb[0:32, 0:TC], pbk, [(wxp_t(j, NT * P, NT * P + 32), ("l0a_wa", "wxp", j)) for j in range(NT)],
               xc_rhs, name=f"{pre}_mmbc")
        p.emit("act", lambda e, pb=pb, t0=t0: nc.scalar.copy(bc_all[:, t0:t0 + TC], pb[0:32, 0:TC]),
               reads=[pbk], writes=[("l0a_bc", ch)], name=f"{pre}_evbc")
    free_warena(wxp_keys)
    phase_reset()

    wdt_t, wdt_keys = load_warena("wdt", W["W_dt"], DI, DI)
    delta = ctx.scratch("l0a_delta", [P, NT * TC], F32)
    xcj = [ctx.scratch(f"l0a_xcj{q}", [P, TC], BF16) for q in range(2)]
    zsj = [ctx.scratch(f"l0a_zsj{q}", [P, TC], BF16) for q in range(2)]
    drB = ctx.scratch("l0a_drB", [P, NT * TC], BF16)
    bbc_all = ctx.scratch("l0a_bbc", [P, NS * TC], BF16)
    cbc_all = ctx.scratch("l0a_cbc", [P, NS * TC], BF16)
    for ch in range(NCH):
        pre = f"l0aC{ch}"
        t0 = ch * TC
        first = (ch % CPB == 0)
        for j in range(NT):
            p.dma("sp", drB[:, j * TC:(j + 1) * TC], dr_d[j * P:(j + 1) * P, t0:t0 + TC],
                  reads=[("dr_d", ch, j)], writes=[("l0a_drB", j)], name=f"{pre}_lddr{j}")
        dr_rhs = [(drB[:, j * TC:(j + 1) * TC], ("l0a_drB", j)) for j in range(NT)]
        if first:
            p.emit("dve", lambda e: nc.vector.memset(state, 0.0),
                   reads=[], writes=[("l0a_state", n_, j_) for n_ in range(NS) for j_ in range(NT)],
                   name=f"{pre}_sz")
        for m in range(NT):
            q = m % 2
            pb, pbk = ctx.psum()
            ctx.mm(pb, pbk, [(wdt_t(j, m * P, (m + 1) * P), ("l0a_wa", "wdt", j)) for j in range(NT)],
                   dr_rhs, name=f"{pre}_mmdt{m}")
            raw = ctx.scratch(f"l0a_raw{q}", [P, TC], F32)
            na = ctx.scratch(f"l0a_na{q}", [P, TC], F32)
            dl = delta[:, m * TC:(m + 1) * TC]
            p.emit("dve", lambda e, pb=pb, raw=raw, m=m:
                   nc.vector.tensor_scalar_add(raw, pb, bdt[:, m:m + 1]),
                   reads=[pbk, bdt_k], writes=[("l0a_raw", q)], name=f"{pre}_bd{m}")
            p.emit("dve", lambda e, raw=raw, na=na:
                   nc.vector.scalar_tensor_tensor(na, raw, -1.0, raw, ALU.mult, ALU.min),
                   reads=[("l0a_raw", q)], writes=[("l0a_na", q)], name=f"{pre}_na{m}")
            p.emit("act", lambda e, na=na: nc.scalar.activation(na, na, AF.Exp),
                   reads=[("l0a_na", q)], writes=[("l0a_na", q)], name=f"{pre}_ex{m}")
            p.emit("act", lambda e, na=na: nc.scalar.activation(na, na, AF.Ln, bias=1.0),
                   reads=[("l0a_na", q)], writes=[("l0a_na", q)], name=f"{pre}_ln{m}")
            p.emit("dve", lambda e, raw=raw: nc.vector.tensor_scalar_max(raw, raw, 0.0),
                   reads=[("l0a_raw", q)], writes=[("l0a_raw", q)], name=f"{pre}_rl{m}")
            p.emit("dve", lambda e, raw=raw, na=na, dl=dl:
                   nc.vector.tensor_add(dl, raw, na),
                   reads=[("l0a_raw", q), ("l0a_na", q)], writes=[("l0a_delta", m)], name=f"{pre}_dl{m}")
        for n in range(NS):
            pb2, pbk2 = ctx.psum()
            p.emit("pe", lambda e, pb2=pb2, n=n, t0=t0:
                   nc.tensor.matmul(pb2, oh32[:, n * P:(n + 1) * P], bc_all[:, t0:t0 + TC], start=True, stop=True),
                   reads=["c_oh32", ("l0a_bc", ch)], writes=[pbk2], name=f"{pre}_bbc{n}")
            p.emit("act", lambda e, pb2=pb2, n=n: nc.scalar.copy(bbc_all[:, n * TC:(n + 1) * TC], pb2),
                   reads=[pbk2], writes=[("l0a_bbc", n)], name=f"{pre}_bev{n}")
            pb3, pbk3 = ctx.psum()
            p.emit("pe", lambda e, pb3=pb3, n=n, t0=t0:
                   nc.tensor.matmul(pb3, oh32[:, (16 + n) * P:(16 + n + 1) * P], bc_all[:, t0:t0 + TC], start=True, stop=True),
                   reads=["c_oh32", ("l0a_bc", ch)], writes=[pbk3], name=f"{pre}_cbc{n}")
            p.emit("act", lambda e, pb3=pb3, n=n: nc.scalar.copy(cbc_all[:, n * TC:(n + 1) * TC], pb3),
                   reads=[pbk3], writes=[("l0a_cbc", n)], name=f"{pre}_cev{n}")
        for j in range(NT):
            jq = j % 2
            ypb, ypbk = ctx.psum_fixed(4 + jq)
            xcjb, zsjb = xcj[jq], zsj[jq]
            p.dma("sp", xcjb, xc_d[j * P:(j + 1) * P, t0:t0 + TC],
                  reads=[("xc_d", ch, j)], writes=[("l0a_xcj", jq)], name=f"{pre}_ldxc{j}")
            p.dma("sp", zsjb, zs_d[j * P:(j + 1) * P, t0:t0 + TC],
                  reads=[("zs_d", ch, j)], writes=[("l0a_zsj", jq)], name=f"{pre}_ldzs{j}")
            for n in range(NS):
                q = n % 2
                a = ctx.scratch(f"l0a_a{q}", [P, TC], F32)
                u = ctx.scratch(f"l0a_u{q}", [P, TC], BF16)
                s = ctx.scratch(f"l0a_s{q}", [P, TC], F32)
                sq = ctx.scratch(f"l0a_sq2{q}", [P, TC], BF16)
                ueng = "pool" if q == 0 else "dve"
                uop = nc.gpsimd if q == 0 else nc.vector
                p.emit("act", lambda e, a=a, j=j, n=n:
                       nc.scalar.activation(a, delta[:, j * TC:(j + 1) * TC], AF.Exp, scale=acol[:, (j * NS + n):(j * NS + n) + 1]),
                       reads=[("l0a_delta", j), ("l0a_acol",)], writes=[("l0a_a", q)], name=f"{pre}_a{j}_{n}")
                p.emit(ueng, lambda e, u=u, j=j, n=n, xcjb=xcjb, uop=uop:
                       uop.tensor_mul(u, xcjb, bbc_all[:, n * TC:(n + 1) * TC]),
                       reads=[("l0a_xcj", jq), ("l0a_bbc", n)], writes=[("l0a_u", q)], name=f"{pre}_u{j}_{n}")
                st_col = state[:, (j * NS + n):(j * NS + n) + 1]
                p.emit("dve", lambda e, s=s, a=a, u=u, st_col=st_col:
                       nc.vector.tensor_tensor_scan(s, a, u, st_col, ALU.mult, ALU.add),
                       reads=[("l0a_a", q), ("l0a_u", q), ("l0a_state", n, j)], writes=[("l0a_s", q)], name=f"{pre}_sc{j}_{n}")
                p.emit("dve", lambda e, s=s, st_col=st_col:
                       nc.vector.tensor_copy(st_col, s[:, TC - 1:TC]),
                       reads=[("l0a_s", q)], writes=[("l0a_state", n, j)], name=f"{pre}_su{j}_{n}")
                p.emit("pool", lambda e, sq=sq, s=s, n=n:
                       nc.gpsimd.tensor_mul(sq, s, cbc_all[:, n * TC:(n + 1) * TC]),
                       reads=[("l0a_s", q), ("l0a_cbc", n)], writes=[("l0a_sq2", q)], name=f"{pre}_sp{j}_{n}")
                p.emit("pe", lambda e, ypb=ypb, sq=sq, n=n:
                       nc.tensor.matmul(ypb, ctx.ident_bf, sq, start=(n == 0), stop=(n == NS - 1)),
                       reads=[("l0a_sq2", q), "c_identb"], writes=[ypbk], name=f"{pre}_ym{j}_{n}")
            yg = ctx.scratch("l0a_ygs", [P, TC], F32)
            ygb = ctx.scratch(f"l0a_ygb{jq}", [P, TC], BF16)
            p.emit("dve", lambda e, yg=yg, ypb=ypb, j=j, xcjb=xcjb:
                   nc.vector.scalar_tensor_tensor(yg, xcjb, dcol[:, j:j + 1], ypb, ALU.mult, ALU.add),
                   reads=[ypbk, ("l0a_xcj", jq), dcol_k], writes=[("l0a_ygs",)], name=f"{pre}_yd{j}")
            p.emit("dve", lambda e, yg=yg, ygb=ygb, j=j, zsjb=zsjb:
                   nc.vector.tensor_mul(ygb, yg, zsjb),
                   reads=[("l0a_ygs",), ("l0a_zsj", jq)], writes=[("l0a_ygb", jq)], name=f"{pre}_yz{j}")
            p.dma("sp", yg_d[j * P:(j + 1) * P, t0:t0 + TC], ygb,
                  reads=[("l0a_ygb", jq)], writes=[("yg_d", ch, j)], name=f"{pre}_sty{j}")
    free_warena(wdt_keys)
    phase_reset()

    wout_t, wout_keys = load_warena("wout", W["W_out"], DI, DM)
    ygD = [ctx.scratch(f"l0a_ygD{par}", [P, NT * TC], BF16) for par in range(2)]
    xD = [ctx.scratch(f"l0a_xD{par}", [P, DT * TC], F32) for par in range(2)]
    for ch in range(NCH):
        par = ch % 2
        pre = f"l0aD{ch}"
        t0 = ch * TC
        for j in range(NT):
            p.dma("sp", ygD[par][:, j * TC:(j + 1) * TC], yg_d[j * P:(j + 1) * P, t0:t0 + TC],
                  reads=[("yg_d", ch, j)], writes=[("l0a_ygD", par, j)], name=f"{pre}_ldy{j}")
        for j in range(DT):
            p.dma("sp", xD[par][:, j * TC:(j + 1) * TC], xin_d[j * P:(j + 1) * P, t0:t0 + TC],
                  writes=[("l0a_xD", par, j)], name=f"{pre}_ldx{j}")
        yg_rhs = [(ygD[par][:, j * TC:(j + 1) * TC], ("l0a_ygD", par, j)) for j in range(NT)]
        for m in range(DT):
            pb, pbk = ctx.psum()
            ctx.mm(pb, pbk, [(wout_t(j, m * P, (m + 1) * P), ("l0a_wa", "wout", j)) for j in range(NT)],
                   yg_rhs, name=f"{pre}_mm{m}")
            xo = ctx.scratch(f"l0a_xo{par}_{m % 2}", [P, TC], F32)
            p.emit("dve", lambda e, pb=pb, xo=xo, m=m, par=par:
                   nc.vector.tensor_add(xo, pb, xD[par][:, m * TC:(m + 1) * TC]),
                   reads=[pbk, ("l0a_xD", par, m)], writes=[("l0a_xo", par, m % 2)], name=f"{pre}_res{m}")
            p.dma("sp", xout_d[m * P:(m + 1) * P, t0:t0 + TC], xo,
                  reads=[("l0a_xo", par, m % 2)], writes=[("xo", ch, m)], name=f"{pre}_st{m}")



def emit_l2_attn(ctx, xin_d, xout_d, W):
    nc, p = ctx.nc, ctx.p
    NS = 16
    KCONV = 4
    _, win_t, _ = ctx.load_w_bf16("l2a_win", W["W_in"], DM, 2048)
    _, wx_t, _ = ctx.load_w_bf16("l2a_wx", W["W_x"], DM, 32)
    _, wout_t, _ = ctx.load_w_bf16("l2a_wout", W["W_out"], DM, DM)
    bin_c, bin_k = ctx.vec_to_cols("l2a_bin", W["b_in"], 2048)
    bx_c, bx_k = ctx.vec_to_cols("l2a_bx", W["b_x"], 32)
    bout_c, bout_k = ctx.vec_to_cols("l2a_bout", W["b_out"], DM)
    dpr_c, dpr_k = ctx.vec_to_cols("l2a_dp", W["D_param"], DM)
    cb_c, cb_k = ctx.vec_to_cols("l2a_cb", W["conv_b"], DM)
    cwraw = ctx.scratch("l2a_cwr", [DT, P * KCONV], F32)
    p.dma("sp", cwraw, W["conv_w"].rearrange("(a b) k -> a (b k)", b=P), writes=[("l2a_cwr",)], name="l2a_cw_ld")
    cw = ctx.scratch("l2a_cw", [P, DT * KCONV], F32)
    for k in range(KCONV):
        pb, pbk = ctx.psum()
        ap_in = cwraw.rearrange("a (b k) -> a b k", k=KCONV)[:, :, k]
        p.emit("pe", lambda e, ap_in=ap_in, pb=pb: nc.tensor.transpose(pb[:, 0:DT], ap_in, ctx.ident[:DT, :DT]),
               reads=[("l2a_cwr",), "c_ident"], writes=[pbk], name=f"l2a_cwtr{k}")
        p.emit("act", lambda e, pb=pb, k=k: nc.scalar.copy(
            cw.rearrange("p (j q) -> p j q", q=KCONV)[:, :, k], pb[:, 0:DT]),
            reads=[pbk], writes=[("l2a_cw", k)], name=f"l2a_cwev{k}")
    cw_col = lambda j, k: cw[:, (j * KCONV + k):(j * KCONV + k) + 1]
    adc = ctx.scratch("l2a_adc", [P, DT * NS], F32)
    for j in range(DT):
        p.dma("sp", adc[:, j * NS:(j + 1) * NS], W["A_log"][j * P:(j + 1) * P, :],
              writes=[("l2a_adc_r", j)], name=f"l2a_al{j}")
    p.emit("act", lambda e: nc.scalar.activation(adc, adc, AF.Exp),
           reads=[("l2a_adc_r", j) for j in range(DT)], writes=[("l2a_adc_1",)], name="l2a_ae1")
    p.emit("act", lambda e: nc.scalar.activation(adc, adc, AF.Exp, scale=-1.0),
           reads=[("l2a_adc_1",)], writes=[("l2a_adc",)], name="l2a_ae2")
    state = ctx.scratch("l2a_state", [P, DT * NS], F32)

    HW = TC + 3
    xbuf = [ctx.scratch(f"l2a_xb{q}", [P, DT * TC], F32) for q in range(2)]
    xnb = [ctx.scratch(f"l2a_xn{q}", [P, DT * TC], BF16) for q in range(2)]
    xib = [ctx.scratch("l2a_xi", [P, DT * HW], BF16)] * 2
    gs = ctx.scratch("l2a_gs", [P, DT * TC], BF16)
    xc = ctx.scratch("l2a_xc", [P, DT * TC], BF16)
    ygb = ctx.scratch("l2a_ygb", [P, DT * TC], BF16)
    bcs = ctx.scratch("l2a_bcs", [32, TC], BF16)
    bbc = ctx.scratch("l2a_bbc", [P, NS * TC], BF16)
    cbc = ctx.scratch("l2a_cbc", [P, NS * TC], BF16)

    for ch in range(NCH):
        par = ch % 2
        pre = f"l2a{ch}"
        t0 = ch * TC
        first = (ch % CPB == 0)
        for j in range(DT):
            p.dma("sp", xbuf[par][:, j * TC:(j + 1) * TC], xin_d[j * P:(j + 1) * P, t0:t0 + TC],
                  writes=[("l2a_xb", par, j)], name=f"{pre}_ld{j}")
        emit_pre_ln(ctx, lambda j: (xbuf[par][:, j * TC:(j + 1) * TC], ("l2a_xb", par, j)),
                    lambda j: (xnb[par][:, j * TC:(j + 1) * TC], ("l2a_xn", par, j)),
                    name="l2aln")
        xn_rhs = [(xnb[par][:, j * TC:(j + 1) * TC], ("l2a_xn", par, j)) for j in range(DT)]
        for m in range(DT):
            hk = ("l2a_xih", m)
            if first:
                p.emit("dve", lambda e, m=m, par=par: nc.vector.memset(xib[par][:, m * HW:m * HW + 3], 0.0),
                       reads=[("l2a_xi", m)], writes=[hk], name=f"{pre}_hz{m}")
            else:
                p.emit("dve", lambda e, m=m, par=par:
                       nc.vector.tensor_copy(xib[par][:, m * HW:m * HW + 3], xib[par][:, (m + 1) * HW - 3:(m + 1) * HW]),
                       reads=[("l2a_xi", m)], writes=[hk], name=f"{pre}_hc{m}")
        for m in range(2 * DT):
            pb, pbk = ctx.psum()
            ctx.mm(pb, pbk, [(win_t(j, m * P, (m + 1) * P), ("l2a_win", j)) for j in range(DT)],
                   xn_rhs, name=f"{pre}_mm{m}")
            if m < DT:
                p.emit("act", lambda e, pb=pb, m=m, par=par:
                       nc.scalar.activation(xib[par][:, m * HW + 3:(m + 1) * HW], pb, AF.Identity, bias=bin_c[:, m:m + 1]),
                       reads=[pbk, bin_k, ("l2a_xih", m)], writes=[("l2a_xi", m)], name=f"{pre}_ev{m}")
            else:
                p.emit("act", lambda e, pb=pb, m=m:
                       nc.scalar.activation(gs[:, (m - DT) * TC:(m - DT + 1) * TC], pb, AF.Silu, bias=bin_c[:, m:m + 1]),
                       reads=[pbk, bin_k], writes=[("l2a_gs", m - DT)], name=f"{pre}_sg{m}")
        for m in range(DT):
            mq = 0
            acc = ctx.scratch("l2a_cacc", [P, TC], F32)
            xi = xib[par]
            base = m * HW
            p.emit("dve", lambda e, xi=xi, base=base, acc=acc, m=m:
                   nc.vector.tensor_scalar(acc, xi[:, base + 3:base + 3 + TC], cw_col(m, 3), cb_c[:, m:m + 1], ALU.mult, ALU.add),
                   reads=[("l2a_xi", m), ("l2a_xih", m), ("l2a_cw", 3), cb_k],
                   writes=[("l2a_cacc", mq)], name=f"{pre}_cv3_{m}")
            for k in range(KCONV - 1):
                p.emit("dve", lambda e, xi=xi, base=base, acc=acc, m=m, k=k:
                       nc.vector.scalar_tensor_tensor(acc, xi[:, base + k:base + k + TC], cw_col(m, k), acc, ALU.mult, ALU.add),
                       reads=[("l2a_xi", m), ("l2a_xih", m), ("l2a_cw", k), ("l2a_cacc", mq)],
                       writes=[("l2a_cacc", mq)], name=f"{pre}_cv{k}_{m}")
            p.emit("act", lambda e, acc=acc, m=m: nc.scalar.activation(xc[:, m * TC:(m + 1) * TC], acc, AF.Silu),
                   reads=[("l2a_cacc", mq)], writes=[("l2a_xc", m)], name=f"{pre}_cs{m}")
        xc_rhs = [(xc[:, j * TC:(j + 1) * TC], ("l2a_xc", j)) for j in range(DT)]
        pb, pbk = ctx.psum()
        ctx.mm(pb[0:32, 0:TC], pbk, [(wx_t(j, 0, 32), ("l2a_wx", j)) for j in range(DT)],
               xc_rhs, name=f"{pre}_mmbc")
        p.emit("dve", lambda e, pb=pb: nc.vector.tensor_scalar_add(bcs, pb[0:32, 0:TC], bx_c),
               reads=[pbk, bx_k], writes=[("l2a_bcs",)], name=f"{pre}_bcb")
        for n in range(NS):
            pb2, pbk2 = ctx.psum()
            p.emit("pe", lambda e, pb2=pb2, n=n:
                   nc.tensor.matmul(pb2, ctx.oh32[:, n * P:(n + 1) * P], bcs, start=True, stop=True),
                   reads=["c_oh32", ("l2a_bcs",)], writes=[pbk2], name=f"{pre}_bbc{n}")
            p.emit("act", lambda e, pb2=pb2, n=n: nc.scalar.copy(bbc[:, n * TC:(n + 1) * TC], pb2),
                   reads=[pbk2], writes=[("l2a_bbc", n)], name=f"{pre}_bev{n}")
            pb3, pbk3 = ctx.psum()
            p.emit("pe", lambda e, pb3=pb3, n=n:
                   nc.tensor.matmul(pb3, ctx.oh32[:, (16 + n) * P:(16 + n + 1) * P], bcs, start=True, stop=True),
                   reads=["c_oh32", ("l2a_bcs",)], writes=[pbk3], name=f"{pre}_cbc{n}")
            p.emit("act", lambda e, pb3=pb3, n=n: nc.scalar.copy(cbc[:, n * TC:(n + 1) * TC], pb3),
                   reads=[pbk3], writes=[("l2a_cbc", n)], name=f"{pre}_cev{n}")
        if first:
            p.emit("dve", lambda e: nc.vector.memset(state, 0.0),
                   writes=[("l2a_state", n_, j_) for n_ in range(NS) for j_ in range(DT)],
                   name=f"{pre}_sz")
        for j in range(DT):
            jq = j % 2
            ypb, ypbk = ctx.psum_fixed(4 + jq)
            for n in range(NS):
                q = n % 2
                u = ctx.scratch(f"l2a_u{q}", [P, TC], BF16)
                s = ctx.scratch(f"l2a_s{q}", [P, TC], F32)
                sq = ctx.scratch(f"l2a_sq2{q}", [P, TC], BF16)
                ueng = "pool" if q == 0 else "dve"
                uop = nc.gpsimd if q == 0 else nc.vector
                p.emit(ueng, lambda e, u=u, j=j, n=n, uop=uop:
                       uop.tensor_mul(u, xc[:, j * TC:(j + 1) * TC], bbc[:, n * TC:(n + 1) * TC]),
                       reads=[("l2a_xc", j), ("l2a_bbc", n)], writes=[("l2a_u", q)], name=f"{pre}_u{j}_{n}")
                st_col = state[:, (j * NS + n):(j * NS + n) + 1]
                ad_b = adc[:, (j * NS + n):(j * NS + n) + 1].to_broadcast([P, TC])
                p.emit("dve", lambda e, s=s, u=u, st_col=st_col, ad_b=ad_b:
                       nc.vector.tensor_tensor_scan(s, ad_b, u, st_col, ALU.mult, ALU.add),
                       reads=[("l2a_adc",), ("l2a_u", q), ("l2a_state", n, j)], writes=[("l2a_s", q)], name=f"{pre}_sc{j}_{n}")
                p.emit("dve", lambda e, s=s, st_col=st_col:
                       nc.vector.tensor_copy(st_col, s[:, TC - 1:TC]),
                       reads=[("l2a_s", q)], writes=[("l2a_state", n, j)], name=f"{pre}_su{j}_{n}")
                p.emit("pool", lambda e, sq=sq, s=s, n=n:
                       nc.gpsimd.tensor_mul(sq, s, cbc[:, n * TC:(n + 1) * TC]),
                       reads=[("l2a_s", q), ("l2a_cbc", n)], writes=[("l2a_sq2", q)], name=f"{pre}_sp{j}_{n}")
                p.emit("pe", lambda e, ypb=ypb, sq=sq, n=n:
                       nc.tensor.matmul(ypb, ctx.ident_bf, sq, start=(n == 0), stop=(n == NS - 1)),
                       reads=[("l2a_sq2", q), "c_identb"], writes=[ypbk], name=f"{pre}_ym{j}_{n}")
            yg = ctx.scratch("l2a_ygs", [P, TC], F32)
            p.emit("dve", lambda e, yg=yg, ypb=ypb, j=j:
                   nc.vector.scalar_tensor_tensor(yg, xc[:, j * TC:(j + 1) * TC], dpr_c[:, j:j + 1], ypb, ALU.mult, ALU.add),
                   reads=[ypbk, ("l2a_xc", j), dpr_k], writes=[("l2a_ygs",)], name=f"{pre}_yd{j}")
            p.emit("dve", lambda e, yg=yg, j=j:
                   nc.vector.tensor_mul(ygb[:, j * TC:(j + 1) * TC], yg, gs[:, j * TC:(j + 1) * TC]),
                   reads=[("l2a_ygs",), ("l2a_gs", j)], writes=[("l2a_ygb", j)], name=f"{pre}_yz{j}")
        yg_rhs = [(ygb[:, j * TC:(j + 1) * TC], ("l2a_ygb", j)) for j in range(DT)]
        for m in range(DT):
            pb, pbk = ctx.psum()
            ctx.mm(pb, pbk, [(wout_t(j, m * P, (m + 1) * P), ("l2a_wout", j)) for j in range(DT)],
                   yg_rhs, name=f"{pre}_mo{m}")
            xo = ctx.scratch("l2a_xo", [P, TC], F32)
            p.emit("dve", lambda e, pb=pb, xo=xo, m=m:
                   nc.vector.tensor_scalar_add(xo, pb, bout_c[:, m:m + 1]),
                   reads=[pbk, bout_k], writes=[("l2a_xo",)], name=f"{pre}_bo{m}")
            p.emit("dve", lambda e, xo=xo, m=m, par=par:
                   nc.vector.tensor_add(xo, xo, xbuf[par][:, m * TC:(m + 1) * TC]),
                   reads=[("l2a_xo",), ("l2a_xb", par, m)], writes=[("l2a_xo",)], name=f"{pre}_res{m}")
            p.dma("sp", xout_d[m * P:(m + 1) * P, t0:t0 + TC], xo,
                  reads=[("l2a_xo",)], writes=[("xo", ch, m)], name=f"{pre}_st{m}")



MOE_RUN_EXPERTS = 8
MOE_ACCUM_DVE = False
MOE_DEBUG_T2 = False


def emit_moe_ffn(ctx, xin_d, xout_d, W, kind):
    nc, p = ctx.nc, ctx.p
    NE = 8
    FF = 4096
    FT = FF // P
    pre0 = "l1f" if kind == "l1" else "l3f"
    act_fn = AF.Relu if kind == "l1" else AF.Gelu
    wg_name = "Wg" if kind == "l1" else "Wr"
    bg_name = "bg" if kind == "l1" else "br"

    xn_d = nc.dram_tensor(pre0 + "_xn", [DM, TOK], BF16, kind="Internal").ap()
    y_d = nc.dram_tensor(pre0 + "_y", [DM, TOK], F32, kind="Internal").ap()

    wgf = ctx.scratch(pre0 + "_wgf", [P, DT * NE], F32)
    for kt in range(DT):
        p.dma("sp", wgf[:, kt * NE:(kt + 1) * NE], W[wg_name][kt * P:(kt + 1) * P, :],
              writes=[(pre0 + "_wgf", kt)], name=f"{pre0}_ldwg{kt}")
    bgr = ctx.scratch(pre0 + "_bgr", [1, NE], F32)
    p.dma("sp", bgr, W[bg_name].rearrange("(a b) -> a b", b=NE), writes=[(pre0, "bgr")], name=pre0 + "_ldbg")
    bg_ps, bg_psk = ctx.bcast_row(bgr, (pre0, "bgr"), NE, name=pre0 + "_bgbc")
    bgbc = ctx.scratch(pre0 + "_bgbc", [P, NE], F32)
    p.emit("act", lambda e: nc.scalar.copy(bgbc, bg_ps), reads=[bg_psk], writes=[(pre0, "bgbc")], name=pre0 + "_bgev")
    gateT = ctx.scratch(pre0 + "_gT", [NE, TOK], BF16)
    bcols = []
    for e in range(NE):
        b1c, b1k = ctx.vec_to_cols(pre0 + f"_b1_{e}", W["b1"][e, :], FF)
        b2c, b2k = ctx.vec_to_cols(pre0 + f"_b2_{e}", W["b2"][e, :], DM)
        bcols.append((b1c, b1k, b2c, b2k))
    _moemark = ctx.arena_mark()

    xbuf = [ctx.scratch(f"{pre0}_xb{q}", [P, DT * TC], F32) for q in range(2)]
    xnb = [ctx.scratch(f"{pre0}_xn{q}", [P, DT * TC], BF16) for q in range(2)]
    xnf = [ctx.scratch(f"{pre0}_xf{q}", [P, DT * TC], F32) for q in range(2)]
    for ch in range(NCH):
        par = ch % 2
        pre = f"{pre0}P{ch}"
        t0 = ch * TC
        for j in range(DT):
            p.dma("sp", xbuf[par][:, j * TC:(j + 1) * TC], xin_d[j * P:(j + 1) * P, t0:t0 + TC],
                  writes=[(pre0 + "_xb", par, j)], name=f"{pre}_ld{j}")
        emit_pre_ln(ctx, lambda j: (xbuf[par][:, j * TC:(j + 1) * TC], (pre0 + "_xb", par, j)),
                    lambda j: (xnf[par][:, j * TC:(j + 1) * TC], (pre0 + "_xf", par, j)),
                    name=pre0 + "ln",
                    out2_tile=lambda j: (xnb[par][:, j * TC:(j + 1) * TC], (pre0 + "_xn", par, j)))
        for j in range(DT):
            p.dma("sp", xn_d[j * P:(j + 1) * P, t0:t0 + TC], xnb[par][:, j * TC:(j + 1) * TC],
                  reads=[(pre0 + "_xn", par, j)], writes=[("xn_d", ch, j)], name=f"{pre}_stn{j}")
        for tt in range(TC // P):
            tq = tt % 2
            pb, pbk = ctx.psum()
            for j in range(DT):
                p.emit("pe", lambda e, pb=pb, j=j, tt=tt, par=par:
                       nc.tensor.matmul(pb[:, 0:NE], xnf[par][:, j * TC + tt * P: j * TC + (tt + 1) * P],
                                        wgf[:, j * NE:(j + 1) * NE], start=(j == 0), stop=(j == DT - 1)),
                       reads=[(pre0 + "_xf", par, j), (pre0 + "_wgf", j)], writes=[pbk], name=f"{pre}_rmm{tt}_{j}")
            lg = ctx.scratch(pre0 + f"_lg{tq}", [P, NE], F32)
            p.emit("dve", lambda e, pb=pb, lg=lg: nc.vector.tensor_add(lg, pb[:, 0:NE], bgbc),
                   reads=[pbk, (pre0, "bgbc")], writes=[(pre0, "lg", tq)], name=f"{pre}_lg{tt}")
            nmx = ctx.scratch(pre0 + f"_nmx{tq}", [P, 1], F32)
            p.emit("dve", lambda e, lg=lg, nmx=nmx: nc.vector.reduce_max(nmx, lg, axis=AX.X, negate=True),
                   reads=[(pre0, "lg", tq)], writes=[(pre0, "nmx", tq)], name=f"{pre}_mx{tt}")
            ex = ctx.scratch(pre0 + f"_ex{tq}", [P, NE], F32)
            sm = ctx.scratch(pre0 + f"_sm{tq}", [P, 1], F32)
            p.emit("act", lambda e, lg=lg, ex=ex, sm=sm, nmx=nmx:
                   nc.scalar.activation(ex, lg, AF.Exp, bias=nmx, accum_out=sm),
                   reads=[(pre0, "lg", tq), (pre0, "nmx", tq)], writes=[(pre0, "ex", tq), (pre0, "sm", tq)], name=f"{pre}_ex{tt}")
            p.emit("dve", lambda e, sm=sm: nc.vector.reciprocal(sm, sm),
                   reads=[(pre0, "sm", tq)], writes=[(pre0, "sm", tq)], name=f"{pre}_rc{tt}")
            g = ctx.scratch(pre0 + f"_g{tq}", [P, NE], F32)
            p.emit("dve", lambda e, ex=ex, sm=sm, g=g: nc.vector.tensor_scalar_mul(g, ex, sm),
                   reads=[(pre0, "ex", tq), (pre0, "sm", tq)], writes=[(pre0, "g", tq)], name=f"{pre}_g{tt}")
            w_tm = ctx.scratch(pre0 + f"_wtm{tq}", [P, NE], BF16)
            m1 = ctx.scratch(pre0 + f"_m1{tq}", [P, 1], F32)
            if kind == "l3":
                p.emit("dve", lambda e, g=g, m1=m1: nc.vector.reduce_max(m1, g, axis=AX.X),
                       reads=[(pre0, "g", tq)], writes=[(pre0, "m1", tq)], name=f"{pre}_m1{tt}")
                p.emit("dve", lambda e, g=g, m1=m1, w_tm=w_tm:
                       nc.vector.tensor_scalar(w_tm, g, m1, m1, ALU.is_equal, ALU.mult),
                       reads=[(pre0, "g", tq), (pre0, "m1", tq)], writes=[(pre0, "wtm", tq)], name=f"{pre}_w{tt}")
            else:
                m2 = ctx.scratch(pre0 + f"_m2{tq}", [P, 1], F32)
                v2 = ctx.scratch(pre0 + f"_v2{tq}", [P, NE], F32)
                p1 = ctx.scratch(pre0 + f"_p1{tq}", [P, 1], F32)
                p2 = ctx.scratch(pre0 + f"_p2{tq}", [P, 1], F32)
                w1 = ctx.scratch(pre0 + f"_w1{tq}", [P, NE], F32)
                p.emit("dve", lambda e, g=g, m1=m1: nc.vector.reduce_max(m1, g, axis=AX.X),
                       reads=[(pre0, "g", tq)], writes=[(pre0, "m1", tq)], name=f"{pre}_m1{tt}")
                p.emit("dve", lambda e, g=g, m1=m1, v2=v2:
                       nc.vector.scalar_tensor_tensor(v2, g, m1, g, ALU.is_lt, ALU.mult),
                       reads=[(pre0, "g", tq), (pre0, "m1", tq)], writes=[(pre0, "v2", tq)], name=f"{pre}_v2{tt}")
                p.emit("dve", lambda e, v2=v2, m2=m2: nc.vector.reduce_max(m2, v2, axis=AX.X),
                       reads=[(pre0, "v2", tq)], writes=[(pre0, "m2", tq)], name=f"{pre}_m2{tt}")
                p.emit("dve", lambda e, m1=m1, m2=m2, p1=p1: nc.vector.tensor_sub(p1, m2, m1),
                       reads=[(pre0, "m1", tq), (pre0, "m2", tq)], writes=[(pre0, "p1", tq)], name=f"{pre}_d{tt}")
                p.emit("act", lambda e, p1=p1: nc.scalar.activation(p1, p1, AF.Sigmoid, scale=-1.0),
                       reads=[(pre0, "p1", tq)], writes=[(pre0, "p1", tq)], name=f"{pre}_sg{tt}")
                p.emit("dve", lambda e, p1=p1, p2=p2: nc.vector.tensor_scalar(p2, p1, -1.0, 1.0, ALU.mult, ALU.add),
                       reads=[(pre0, "p1", tq)], writes=[(pre0, "p2", tq)], name=f"{pre}_p2{tt}")
                p.emit("dve", lambda e, g=g, m1=m1, w1=w1, p1=p1:
                       nc.vector.tensor_scalar(w1, g, m1, p1, ALU.is_equal, ALU.mult),
                       reads=[(pre0, "g", tq), (pre0, "m1", tq), (pre0, "p1", tq)], writes=[(pre0, "w1", tq)], name=f"{pre}_w1{tt}")
                p.emit("dve", lambda e, g=g, m2=m2, w_tm=w_tm, p2=p2:
                       nc.vector.tensor_scalar(w_tm, g, m2, p2, ALU.is_equal, ALU.mult),
                       reads=[(pre0, "g", tq), (pre0, "m2", tq), (pre0, "p2", tq)], writes=[(pre0, "wtm", tq)], name=f"{pre}_w2{tt}")
                p.emit("dve", lambda e, w1=w1, w_tm=w_tm: nc.vector.tensor_add(w_tm, w_tm, w1),
                       reads=[(pre0, "w1", tq), (pre0, "wtm", tq)], writes=[(pre0, "wtm", tq)], name=f"{pre}_w{tt}")
            pbt, pbtk = ctx.psum_bf()
            p.emit("pe", lambda e, pbt=pbt, w_tm=w_tm:
                   nc.tensor.transpose(pbt[0:NE, 0:P], w_tm, ctx.ident_bf),
                   reads=[(pre0, "wtm", tq), "c_identb"], writes=[pbtk], name=f"{pre}_wt{tt}")
            p.emit("act", lambda e, pbt=pbt, t0=t0, tt=tt:
                   nc.scalar.copy(gateT[:, t0 + tt * P: t0 + (tt + 1) * P], pbt[0:NE, 0:P]),
                   reads=[pbtk], writes=[(pre0 + "_gT", ch, tt)], name=f"{pre}_wte{tt}")

    ctx.arena_to(_moemark)
    w1b = ctx.scratch(pre0 + "_w1b", [P, DT * FF], BF16)
    w2b = ctx.scratch(pre0 + "_w2b", [P, FT * DM], BF16)
    hb = ctx.scratch(pre0 + "_hb", [P, FT * TC], BF16)
    xnc = ctx.scratch(pre0 + "_xnc", [P, DT * TC], BF16)
    wbc = ctx.scratch(pre0 + "_wbc", [P, TC], BF16)
    if MOE_DEBUG_T2:
        dbg_d = nc.dram_tensor(pre0 + "_dbg", [MOE_RUN_EXPERTS * NCH * DT * P, TC], F32, kind="Internal").ap()
        dbg2_d = nc.dram_tensor(pre0 + "_dbg2", [MOE_RUN_EXPERTS * NCH * P, TC], F32, kind="Internal").ap()
        dbg3_d = nc.dram_tensor(pre0 + "_dbg3", [MOE_RUN_EXPERTS * NCH * P, TC], F32, kind="Internal").ap()
        dbg4_d = nc.dram_tensor(pre0 + "_dbg4", [MOE_RUN_EXPERTS * NCH * P, TC], F32, kind="Internal").ap()
        ctx._dbg5 = nc.dram_tensor(pre0 + "_dbg5", [MOE_RUN_EXPERTS * NCH * P, TC], F32, kind="Internal").ap()
        ctx._dbg6 = nc.dram_tensor(pre0 + "_dbg6", [MOE_RUN_EXPERTS * NCH * P, TC], BF16, kind="Internal").ap()
    for e in range(MOE_RUN_EXPERTS):
        epre = f"{pre0}E{e}"
        for kt in range(DT):
            p.dma("pool", w1b[:, kt * FF:(kt + 1) * FF], W["W1"][e * DM + kt * P:e * DM + (kt + 1) * P, :],
                  writes=[(pre0 + "_w1b", kt)], name=f"{epre}_lw1{kt}")
        for kt in range(FT):
            p.dma("pool", w2b[:, kt * DM:(kt + 1) * DM], W["W2"][e * FF + kt * P:e * FF + (kt + 1) * P, :],
                  writes=[(pre0 + "_w2b", kt)], name=f"{epre}_lw2{kt}")
        b1c, b1k, b2c, b2k = bcols[e]
        for ch in range(NCH):
            t0 = ch * TC
            cpre = f"{epre}c{ch}"
            for j in range(DT):
                p.dma("sp", xnc[:, j * TC:(j + 1) * TC], xn_d[j * P:(j + 1) * P, t0:t0 + TC],
                      reads=[("xn_d", ch, j)], writes=[(pre0 + "_xnc", j)], name=f"{cpre}_ldx{j}")
            xn_rhs = [(xnc[:, j * TC:(j + 1) * TC], (pre0 + "_xnc", j)) for j in range(DT)]
            if MOE_DEBUG_T2:
                row = (e * NCH + ch) * P
                dbg5_d = ctx._dbg5
                p.dma("pool", dbg5_d[row:row + P, :], w1b[:, 0:TC],
                      reads=[(pre0 + "_w1b", 0)], writes=[("dbg_w1", e, ch)], name=f"{cpre}_dbgw1")
                p.dma("sp", ctx._dbg6[row:row + P, :], xnc[:, 0:TC],
                      reads=[(pre0 + "_xnc", 0)], writes=[("dbg_xn", e, ch)], name=f"{cpre}_dbgxn")
            for fg in range(FT):
                pb, pbk = ctx.psum()
                ctx.mm(pb, pbk, [(w1b[:, j * FF + fg * P:j * FF + (fg + 1) * P], (pre0 + "_w1b", j)) for j in range(DT)],
                       xn_rhs, name=f"{cpre}_h{fg}")
                p.emit("act", lambda e_, pb=pb, fg=fg, b1c=b1c:
                       nc.scalar.activation(hb[:, fg * TC:(fg + 1) * TC], pb, act_fn, bias=b1c[:, fg:fg + 1]),
                       reads=[pbk, b1k], writes=[(pre0 + "_hb", fg)], name=f"{cpre}_ha{fg}")
            pbg, pbgk = ctx.psum()
            p.emit("pe", lambda e_, pbg=pbg, t0=t0, ee=e:
                   nc.tensor.matmul(pbg, ctx.oh8[:, ee * P:(ee + 1) * P], gateT[:, t0:t0 + TC], start=True, stop=True),
                   reads=["c_oh8"] + [(pre0 + "_gT", c_, tt) for c_ in range(NCH) for tt in range(TC // P)],
                   writes=[pbgk], name=f"{cpre}_gbc")
            p.emit("act", lambda e_, pbg=pbg: nc.scalar.copy(wbc, pbg),
                   reads=[pbgk], writes=[(pre0 + "_wbc",)], name=f"{cpre}_gbe")
            if MOE_DEBUG_T2:
                row = (e * NCH + ch) * P
                p.dma("pool", dbg2_d[row:row + P, :], wbc,
                      reads=[(pre0 + "_wbc",)], writes=[("dbg_wbc", e, ch)], name=f"{cpre}_dbgw")
                p.dma("pool", dbg4_d[row:row + P, :], hb[:, 0:TC],
                      reads=[(pre0 + "_hb", 0)], writes=[("dbg_h", e, ch)], name=f"{cpre}_dbgh")
            h_rhs = [(hb[:, j * TC:(j + 1) * TC], (pre0 + "_hb", j)) for j in range(FT)]
            for m in range(DT):
                mq = m % 2
                pb, pbk = ctx.psum()
                ctx.mm(pb, pbk, [(w2b[:, j * DM + m * P:j * DM + (m + 1) * P], (pre0 + "_w2b", j)) for j in range(FT)],
                       h_rhs, name=f"{cpre}_o{m}")
                t1 = ctx.scratch(pre0 + f"_t1{mq}", [P, TC], F32)
                t2 = ctx.scratch(pre0 + f"_t2{mq}", [P, TC], F32)
                p.emit("dve", lambda e_, pb=pb, t1=t1, m=m, b2c=b2c:
                       nc.vector.tensor_scalar_add(t1, pb, b2c[:, m:m + 1]),
                       reads=[pbk, b2k], writes=[(pre0, "t1", mq)], name=f"{cpre}_b2{m}")
                if MOE_DEBUG_T2 and m == 0:
                    row = (e * NCH + ch) * P
                    p.dma("sp", dbg3_d[row:row + P, :], t1,
                          reads=[(pre0, "t1", mq)], writes=[("dbg_t1", e, ch)], name=f"{cpre}_dbgt1")
                p.emit("dve", lambda e_, t1=t1, t2=t2:
                       nc.vector.tensor_mul(t2, t1, wbc),
                       reads=[(pre0, "t1", mq), (pre0 + "_wbc",)], writes=[(pre0, "t2", mq)], name=f"{cpre}_gg{m}")
                if MOE_DEBUG_T2:
                    row = ((e * NCH + ch) * DT + m) * P
                    p.dma("sp", dbg_d[row:row + P, :], t2,
                          reads=[(pre0, "t2", mq)], writes=[("dbg_t2", e, ch, m)], name=f"{cpre}_dbg{m}")
                if e == 0:
                    p.dma("pool", y_d[m * P:(m + 1) * P, t0:t0 + TC], t2,
                          reads=[(pre0, "t2", mq)], writes=[("y_d", ch, m)], name=f"{cpre}_sy{m}")
                elif MOE_ACCUM_DVE:
                    yt = ctx.scratch(pre0 + "_yt", [P, TC], F32)
                    p.dma("sp", yt, y_d[m * P:(m + 1) * P, t0:t0 + TC],
                          reads=[("y_d", ch, m)], writes=[(pre0, "yt")], name=f"{cpre}_ly{m}")
                    p.emit("dve", lambda e_, t2=t2, yt=yt: nc.vector.tensor_add(t2, t2, yt),
                           reads=[(pre0, "t2", mq), (pre0, "yt")], writes=[(pre0, "t2", mq)], name=f"{cpre}_ad{m}")
                    p.dma("pool", y_d[m * P:(m + 1) * P, t0:t0 + TC], t2,
                          reads=[(pre0, "t2", mq), ("y_d", ch, m)], writes=[("y_d", ch, m)], name=f"{cpre}_wy{m}")
                else:
                    p.dma("pool", y_d[m * P:(m + 1) * P, t0:t0 + TC], t2,
                          reads=[(pre0, "t2", mq), ("y_d", ch, m)], writes=[("y_d", ch, m)],
                          name=f"{cpre}_ay{m}", accum_op=ALU.add)

    ctx.arena_to(_moemark)
    xbuf = [ctx.scratch(f"{pre0}_xb{q}", [P, DT * TC], F32) for q in range(2)]
    for ch in range(NCH):
        par = ch % 2
        t0 = ch * TC
        fpre = f"{pre0}F{ch}"
        for j in range(DT):
            p.dma("sp", xbuf[par][:, j * TC:(j + 1) * TC], xin_d[j * P:(j + 1) * P, t0:t0 + TC],
                  writes=[(pre0 + "_xb", par, j)], name=f"{fpre}_ldx{j}")
            yb = ctx.scratch(pre0 + f"_yb{par}", [P, DT * TC], F32)
            p.dma("sp", yb[:, j * TC:(j + 1) * TC], y_d[j * P:(j + 1) * P, t0:t0 + TC],
                  reads=[("y_d", ch, j)], writes=[(pre0 + "_yb", par, j)], name=f"{fpre}_ldy{j}")
            xo = ctx.scratch(pre0 + f"_xo{par}_{j % 2}", [P, TC], F32)
            p.emit("dve", lambda e, j=j, par=par, yb=yb, xo=xo:
                   nc.vector.tensor_add(xo, xbuf[par][:, j * TC:(j + 1) * TC], yb[:, j * TC:(j + 1) * TC]),
                   reads=[(pre0 + "_xb", par, j), (pre0 + "_yb", par, j)], writes=[(pre0 + "_xo", par, j % 2)],
                   name=f"{fpre}_add{j}")
            p.dma("sp", xout_d[j * P:(j + 1) * P, t0:t0 + TC], xo,
                  reads=[(pre0 + "_xo", par, j % 2)], writes=[("xo", ch, j)], name=f"{fpre}_st{j}")


def emit_l1_ffn(ctx, xin_d, xout_d, W):
    emit_moe_ffn(ctx, xin_d, xout_d, W, "l1")


def emit_l3_ffn(ctx, xin_d, xout_d, W):
    emit_moe_ffn(ctx, xin_d, xout_d, W, "l3")



def emit_l1_attn(ctx, xin_d, xout_d, W):
    nc, p = ctx.nc, ctx.p
    NH = 16
    HD = 64
    _, wq_t, _ = ctx.load_w_bf16_T("l1a_wq", W["Wq"], DM, DM)
    _, wk_t, _ = ctx.load_w_bf16_T("l1a_wk", W["Wk"], DM, DM)
    _, wv_t, _ = ctx.load_w_bf16_T("l1a_wv", W["Wv"], DM, DM)
    _, wf_t, _ = ctx.load_w_bf16_T("l1a_wf", W["Wfc"], DM, DM)

    def vop(eng):
        return nc.vector if eng == "dve" else nc.gpsimd

    xbuf = ctx.scratch("l1a_xb", [P, DT * TC], F32)
    xnb = ctx.scratch("l1a_xn", [P, DT * TC], BF16)
    qf = ctx.scratch("l1a_qf", [P, DT * TC], BF16)
    kf = ctx.scratch("l1a_kf", [P, DT * TC], BF16)
    vf = ctx.scratch("l1a_vf", [P, DT * TC], BF16)
    of = ctx.scratch("l1a_of", [P, DT * TC], BF16)
    qtms = [ctx.scratch(f"l1a_qtm{q}", [P, DM], BF16) for q in range(2)]
    ktms = [ctx.scratch(f"l1a_ktm{q}", [P, DM], BF16) for q in range(2)]
    vtms = [ctx.scratch(f"l1a_vtm{q}", [P, DM], BF16) for q in range(2)]
    otms = [ctx.scratch(f"l1a_otm{q}", [P, DM], BF16) for q in range(2)]
    mbufs = [ctx.scratch(f"l1a_mb{q}", [P, DM], BF16) for q in range(2)]
    mbuf2s = [ctx.scratch(f"l1a_mb2{q}", [P, DM], BF16) for q in range(2)]
    sbufs = [ctx.scratch(f"l1a_sb{q}", [P, NH * NH], F32) for q in range(2)]
    abufs = [ctx.scratch(f"l1a_ab{q}", [P, NH * NH], BF16) for q in range(2)]
    nmxs = [ctx.scratch(f"l1a_nmx{q}", [P, NH], F32) for q in range(2)]
    ssums = [ctx.scratch(f"l1a_ssum{q}", [P, NH], F32) for q in range(2)]
    tmpbs = [ctx.scratch(f"l1a_tmp{q}", [P, DM], BF16) for q in range(2)]
    tmpb2s = [ctx.scratch(f"l1a_tmp2{q}", [P, DM], BF16) for q in range(2)]

    s3 = lambda ap: ap.rearrange("p (h d) -> p h d", d=HD)
    a3 = lambda ap: ap.rearrange("p (h t) -> p h t", t=NH)

    for ch in range(NCH):
        pre = f"l1a{ch}"
        t0 = ch * TC
        for j in range(DT):
            p.dma("sp", xbuf[:, j * TC:(j + 1) * TC], xin_d[j * P:(j + 1) * P, t0:t0 + TC],
                  writes=[("l1a_xb", j)], name=f"{pre}_ld{j}")
        emit_pre_ln(ctx, lambda j: (xbuf[:, j * TC:(j + 1) * TC], ("l1a_xb", j)),
                    lambda j: (xnb[:, j * TC:(j + 1) * TC], ("l1a_xn", j)),
                    name="l1aln")
        xn_rhs = [(xnb[:, j * TC:(j + 1) * TC], ("l1a_xn", j)) for j in range(DT)]
        for which, wt, wname, dst, dk in [(0, wq_t, "l1a_wq", qf, "l1a_qf"),
                                          (1, wk_t, "l1a_wk", kf, "l1a_kf"),
                                          (2, wv_t, "l1a_wv", vf, "l1a_vf")]:
            for m in range(DT):
                pb, pbk = ctx.psum()
                ctx.mm(pb, pbk, [(wt(j, m * P, (m + 1) * P), (wname, j, m)) for j in range(DT)],
                       xn_rhs, name=f"{pre}_p{which}_{m}")
                p.emit("act", lambda e, pb=pb, dst=dst, m=m: nc.scalar.copy(dst[:, m * TC:(m + 1) * TC], pb),
                       reads=[pbk], writes=[(dk, m)], name=f"{pre}_e{which}_{m}")
        for tt in range(TC // P):
            ts0 = tt * P
            tq = tt % 2
            tpre = f"{pre}t{tt}"
            qtm, ktm, vtm, otm = qtms[tq], ktms[tq], vtms[tq], otms[tq]
            sbuf_, abuf = sbufs[tq], abufs[tq]
            nmx, ssum = nmxs[tq], ssums[tq]
            for (srcb, srck, dstb, dstk) in [(qf, "l1a_qf", qtm, ("l1a_qtm", tq)),
                                             (kf, "l1a_kf", ktm, ("l1a_ktm", tq)),
                                             (vf, "l1a_vf", vtm, ("l1a_vtm", tq))]:
                pbt, pbtk = ctx.psum_bf()
                for j in range(DT):
                    p.emit("pe", lambda e, srcb=srcb, j=j, pbt=pbt, ts0=ts0:
                           nc.tensor.transpose(pbt[:, j * P:(j + 1) * P], srcb[:, j * TC + ts0:j * TC + ts0 + P], ctx.ident_bf),
                           reads=[(srck, j), "c_identb", dstk], writes=[pbtk], name=f"{tpre}_tr{j}")
                p.emit("act", lambda e, pbt=pbt, dstb=dstb: nc.scalar.copy(dstb, pbt),
                       reads=[pbtk], writes=[dstk], name=f"{tpre}_te")
            ecnt = {"dve": 0, "pool": 0}
            for th in range(NH):
                eng = "dve" if th % 2 == 0 else "pool"
                uq = ecnt[eng] % 2
                ecnt[eng] += 1
                mb = mbufs[uq] if eng == "dve" else mbuf2s[uq]
                mk = ("l1a_mb", uq) if eng == "dve" else ("l1a_mb2", uq)
                krep = ktm[:, th * HD:(th + 1) * HD].unsqueeze(1).to_broadcast([P, NH, HD])
                p.emit(eng, lambda e, krep=krep, eng=eng, mb=mb, qtm=qtm:
                       vop(eng).tensor_tensor(s3(mb), s3(qtm), krep, ALU.mult),
                       reads=[("l1a_qtm", tq), ("l1a_ktm", tq)], writes=[mk], name=f"{tpre}_qk{th}")
                sout = a3(sbuf_)[:, :, th]
                p.emit("dve", lambda e, sout=sout, mb=mb:
                       nc.vector.reduce_sum(sout, s3(mb), axis=AX.X),
                       reads=[mk], writes=[("l1a_sb", tq, th)], name=f"{tpre}_rs{th}")
            p.emit("dve", lambda e, nmx=nmx, sbuf_=sbuf_: nc.vector.reduce_max(nmx, a3(sbuf_), axis=AX.X, negate=True),
                   reads=[("l1a_sb", tq, th_) for th_ in range(NH)], writes=[("l1a_nmx", tq)], name=f"{tpre}_mx")
            p.emit("dve", lambda e, nmx=nmx, sbuf_=sbuf_: nc.vector.tensor_add(a3(sbuf_), a3(sbuf_), nmx.unsqueeze(2).to_broadcast([P, NH, NH])),
                   reads=[("l1a_nmx", tq)] + [("l1a_sb", tq, th_) for th_ in range(NH)], writes=[("l1a_sb2", tq)], name=f"{tpre}_sub")
            p.emit("act", lambda e, sbuf_=sbuf_: nc.scalar.activation(sbuf_, sbuf_, AF.Exp, scale=0.125),
                   reads=[("l1a_sb2", tq)], writes=[("l1a_sb3", tq)], name=f"{tpre}_exp")
            p.emit("dve", lambda e, ssum=ssum, sbuf_=sbuf_: nc.vector.reduce_sum(ssum, a3(sbuf_), axis=AX.X),
                   reads=[("l1a_sb3", tq)], writes=[("l1a_ssum", tq)], name=f"{tpre}_sm")
            p.emit("dve", lambda e, ssum=ssum: nc.vector.reciprocal(ssum, ssum),
                   reads=[("l1a_ssum", tq)], writes=[("l1a_ssum", tq)], name=f"{tpre}_rc")
            p.emit("dve", lambda e, abuf=abuf, sbuf_=sbuf_, ssum=ssum: nc.vector.tensor_mul(a3(abuf), a3(sbuf_), ssum.unsqueeze(2).to_broadcast([P, NH, NH])),
                   reads=[("l1a_sb3", tq), ("l1a_ssum", tq)], writes=[("l1a_ab", tq)], name=f"{tpre}_nm")
            po0, po0k = ctx.psum()
            po1, po1k = ctx.psum()
            ecnt = {"dve": 0, "pool": 0}
            for th in range(NH):
                eng = "dve" if th % 2 == 0 else "pool"
                uq = ecnt[eng] % 2
                ecnt[eng] += 1
                tb = tmpbs[uq] if eng == "dve" else tmpb2s[uq]
                tk = ("l1a_tmp", uq) if eng == "dve" else ("l1a_tmp2", uq)
                arep = a3(abuf)[:, :, th].unsqueeze(2).to_broadcast([P, NH, HD])
                vrep = vtm[:, th * HD:(th + 1) * HD].unsqueeze(1).to_broadcast([P, NH, HD])
                p.emit(eng, lambda e, arep=arep, vrep=vrep, eng=eng, tb=tb:
                       vop(eng).tensor_tensor(s3(tb), arep, vrep, ALU.mult),
                       reads=[("l1a_ab", tq), ("l1a_vtm", tq)], writes=[tk], name=f"{tpre}_av{th}")
                p.emit("pe", lambda e, th=th, po0=po0, tb=tb: nc.tensor.matmul(po0, ctx.ident_bf, tb[:, 0:TC], start=(th == 0), stop=(th == NH - 1)),
                       reads=[tk, "c_identb"], writes=[po0k], name=f"{tpre}_oa{th}")
                p.emit("pe", lambda e, th=th, po1=po1, tb=tb: nc.tensor.matmul(po1, ctx.ident_bf, tb[:, TC:DM], start=(th == 0), stop=(th == NH - 1)),
                       reads=[tk, "c_identb"], writes=[po1k], name=f"{tpre}_ob{th}")
            p.emit("act", lambda e, po0=po0, otm=otm: nc.scalar.copy(otm[:, 0:TC], po0),
                   reads=[po0k], writes=[("l1a_otm", tq, 0)], name=f"{tpre}_oe0")
            p.emit("act", lambda e, po1=po1, otm=otm: nc.scalar.copy(otm[:, TC:DM], po1),
                   reads=[po1k], writes=[("l1a_otm", tq, 1)], name=f"{tpre}_oe1")
            pbt, pbtk = ctx.psum_bf()
            for j in range(DT):
                p.emit("pe", lambda e, j=j, pbt=pbt, otm=otm:
                       nc.tensor.transpose(pbt[:, j * P:(j + 1) * P], otm[:, j * P:(j + 1) * P], ctx.ident_bf),
                       reads=[("l1a_otm", tq, 0), ("l1a_otm", tq, 1), "c_identb", ("l1a_of", tt)], writes=[pbtk], name=f"{tpre}_otr{j}")
            dst = of.rearrange("p (j t) -> p j t", t=TC)[:, :, ts0:ts0 + P]
            p.emit("act", lambda e, pbt=pbt, dst=dst: nc.scalar.copy(dst, pbt.rearrange("p (j t) -> p j t", t=P)),
                   reads=[pbtk], writes=[("l1a_of", tt)], name=f"{tpre}_ofe")
        of_rhs = [(of[:, j * TC:(j + 1) * TC], ("l1a_of", tt_) ) for j in range(DT) for tt_ in [0]]
        of_rhs = [(of[:, j * TC:(j + 1) * TC], ("l1a_of",)) for j in range(DT)]
        p.emit("dve", lambda e: nc.vector.engine_nop(),
               reads=[("l1a_of", tt_) for tt_ in range(TC // P)], writes=[("l1a_of",)], name=f"{pre}_ofsync")
        for m in range(DT):
            pb, pbk = ctx.psum()
            ctx.mm(pb, pbk, [(wf_t(j, m * P, (m + 1) * P), ("l1a_wf", j, m)) for j in range(DT)],
                   of_rhs, name=f"{pre}_fc{m}")
            xo = ctx.scratch(f"l1a_xo{m % 2}", [P, TC], F32)
            p.emit("dve", lambda e, pb=pb, m=m, xo=xo:
                   nc.vector.tensor_add(xo, pb, xbuf[:, m * TC:(m + 1) * TC]),
                   reads=[pbk, ("l1a_xb", m)], writes=[("l1a_xo", m % 2)], name=f"{pre}_res{m}")
            p.dma("sp", xout_d[m * P:(m + 1) * P, t0:t0 + TC], xo,
                  reads=[("l1a_xo", m % 2)], writes=[("xo", ch, m)], name=f"{pre}_st{m}")



def emit_l0_ffn(ctx, xin_d, xout_d, W):
    nc, p = ctx.nc, ctx.p
    NH, HD = 16, 64
    FF = 2048
    FT = FF // P
    NR = 8
    CP = SEQ // TC
    QT = SEQ // P

    qkvb, qkvb_k = ctx.vec_to_cols("l0f_qkvb", W["qkv_b"], 3072)
    outb, outb_k = ctx.vec_to_cols("l0f_outb", W["out_b"], DM)
    n1w, n1w_k = ctx.vec_to_cols("l0f_n1w", W["norm1_w"], DM)
    n1b, n1b_k = ctx.vec_to_cols("l0f_n1b", W["norm1_b"], DM)
    n2w, n2w_k = ctx.vec_to_cols("l0f_n2w", W["norm2_w"], DM)
    n2b, n2b_k = ctx.vec_to_cols("l0f_n2b", W["norm2_b"], DM)
    sb1c, sb1_k = ctx.vec_to_cols("l0f_sb1", W["sb1"][0, :], FF)
    sb2c, sb2_k = ctx.vec_to_cols("l0f_sb2", W["sb2"][0, :], DM)
    rbcols = []
    for e in range(NR):
        r1, r1k = ctx.vec_to_cols(f"l0f_rb1_{e}", W["rb1"][e, :], FF)
        r2, r2k = ctx.vec_to_cols(f"l0f_rb2_{e}", W["rb2"][e, :], DM)
        rbcols.append((r1, r1k, r2, r2k))
    craw = ctx.scratch("l0f_craw", [NR, DM], F32)
    p.dma("sp", craw, W["centroid"], writes=[("l0f_craw",)], name="l0f_cld")
    centT = ctx.scratch("l0f_centT", [P, DT * NR], F32)
    for j in range(DT):
        pb, pbk = ctx.psum()
        p.emit("pe", lambda e, pb=pb, j=j:
               nc.tensor.transpose(pb[:, 0:NR], craw[:, j * P:(j + 1) * P], ctx.ident[:NR, :NR]),
               reads=[("l0f_craw",), "c_ident"], writes=[pbk], name=f"l0f_ctr{j}")
        p.emit("act", lambda e, pb=pb, j=j: nc.scalar.copy(centT[:, j * NR:(j + 1) * NR], pb[:, 0:NR]),
               reads=[pbk], writes=[("l0f_centT", j)], name=f"l0f_cev{j}")
    rbr = ctx.scratch("l0f_rbr", [1, NR], F32)
    p.dma("sp", rbr, W["route_bias"].rearrange("(a b) -> a b", b=NR), writes=[("l0f_rbr",)], name="l0f_rbld")
    rb_ps, rb_psk = ctx.bcast_row(rbr, ("l0f_rbr",), NR, name="l0f_rbbc")
    rbbc = ctx.scratch("l0f_rbbc", [P, NR], F32)
    p.emit("act", lambda e: nc.scalar.copy(rbbc, rb_ps), reads=[rb_psk], writes=[("l0f_rbbc",)], name="l0f_rbev")
    gateT = ctx.scratch("l0f_gT", [NR, TOK], BF16)

    WA = 8 * 3072 + 8 * 1024
    warena = ctx.scratch("l0f_warena", [P, WA], BF16)
    for kt in range(DT):
        p.dma("pool", warena[:, kt * 3072:(kt + 1) * 3072], W["qkv_w"][kt * P:(kt + 1) * P, :],
              reads=[("l0f_wguard",)], writes=[("l0f_wqkv", kt)], name=f"l0f_wq{kt}")
        p.dma("pool", warena[:, 8 * 3072 + kt * 1024: 8 * 3072 + (kt + 1) * 1024],
              W["out_w"][kt * P:(kt + 1) * P, :],
              reads=[("l0f_wguard",)], writes=[("l0f_wout", kt)], name=f"l0f_wo{kt}")
    qkv_t = lambda kt, m0, m1: warena[:, kt * 3072 + m0: kt * 3072 + m1]
    out_t = lambda kt, m0, m1: warena[:, 8 * 3072 + kt * 1024 + m0: 8 * 3072 + kt * 1024 + m1]
    rw1_t = lambda kt, m0, m1: warena[:, kt * 2048 + m0: kt * 2048 + m1]
    rw2_t = lambda kt, m0, m1: warena[:, 8 * 2048 + kt * 1024 + m0: 8 * 2048 + kt * 1024 + m1]

    v32_d = nc.dram_tensor("l0f_v32", [DM, TOK], F32, kind="Internal").ap()
    q_d = nc.dram_tensor("l0f_q", [DM, TOK], BF16, kind="Internal").ap()
    o_d = nc.dram_tensor("l0f_o", [DM, TOK], BF16, kind="Internal").ap()
    x1_d = nc.dram_tensor("l0f_x1", [DM, TOK], F32, kind="Internal").ap()
    x1b_d = nc.dram_tensor("l0f_x1b", [DM, TOK], BF16, kind="Internal").ap()
    y2_d = nc.dram_tensor("l0f_y2", [DM, TOK], F32, kind="Internal").ap()

    _mark = ctx.arena_mark()

    for b in range(NB):
        bt = b * SEQ
        kfm = ctx.scratch("l0f_kfm", [P, DT * SEQ], BF16)
        vtb = ctx.scratch("l0f_vt", [P, NH * QT * HD], BF16)
        vt_tile = lambda h, kt: vtb[:, (h * QT + kt) * HD:(h * QT + kt + 1) * HD]
        _mark2 = ctx.arena_mark()
        xbuf = ctx.scratch("l0f_xb", [P, DT * TC], F32)
        vb16 = ctx.scratch("l0f_vb", [P, DT * TC], BF16)
        v32 = xbuf
        vtmps = [ctx.scratch(f"l0f_vtmp{q}", [P, TC], BF16) for q in range(2)]

        for cc in range(CP):
            pre = f"l0fQ{b}_{cc}"
            t0 = bt + cc * TC
            for j in range(DT):
                p.dma("sp", xbuf[:, j * TC:(j + 1) * TC], xin_d[j * P:(j + 1) * P, t0:t0 + TC],
                      writes=[("l0f_xb", j)], name=f"{pre}_ld{j}")
            emit_pre_ln(ctx, lambda j: (xbuf[:, j * TC:(j + 1) * TC], ("l0f_xb", j)),
                        lambda j: (v32[:, j * TC:(j + 1) * TC], ("l0f_xb", j)),
                        name="l0fln",
                        out2_tile=lambda j: (vb16[:, j * TC:(j + 1) * TC], ("l0f_vb", j)))
            for j in range(DT):
                p.dma("sp", v32_d[j * P:(j + 1) * P, t0:t0 + TC], v32[:, j * TC:(j + 1) * TC],
                      reads=[("l0f_xb", j)], writes=[("v32_d", b, cc, j)], name=f"{pre}_sv{j}")
            vb_rhs = [(vb16[:, j * TC:(j + 1) * TC], ("l0f_vb", j)) for j in range(DT)]
            for m in range(24):
                pb, pbk = ctx.psum()
                ctx.mm(pb, pbk, [(qkv_t(j, m * P, (m + 1) * P), ("l0f_wqkv", j)) for j in range(DT)],
                       vb_rhs, name=f"{pre}_mm{m}")
                mq = m % 2
                vtmp = vtmps[mq]
                if m < 8:
                    p.emit("act", lambda e, pb=pb, m=m, vtmp=vtmp: nc.scalar.activation(vtmp, pb, AF.Identity, bias=qkvb[:, m:m + 1]),
                           reads=[pbk, qkvb_k, ("l0f_vtmp", mq)], writes=[("l0f_vtmp", mq)], name=f"{pre}_qe{m}")
                    p.dma("sp", q_d[m * P:(m + 1) * P, t0:t0 + TC], vtmp,
                          reads=[("l0f_vtmp", mq)], writes=[("q_d", b, cc, m)], name=f"{pre}_qs{m}")
                elif m < 16:
                    p.emit("act", lambda e, pb=pb, m=m, cc=cc:
                           nc.scalar.activation(kfm[:, (m - 8) * SEQ + cc * TC:(m - 8) * SEQ + (cc + 1) * TC], pb, AF.Identity, bias=qkvb[:, m:m + 1]),
                           reads=[pbk, qkvb_k], writes=[("l0f_kfm", m - 8, cc)], name=f"{pre}_ke{m}")
                else:
                    p.emit("act", lambda e, pb=pb, m=m, vtmp=vtmp: nc.scalar.activation(vtmp, pb, AF.Identity, bias=qkvb[:, m:m + 1]),
                           reads=[pbk, qkvb_k, ("l0f_vtmp", mq)], writes=[("l0f_vtmp", mq)], name=f"{pre}_ve{m}")
                    mj = m - 16
                    for hh in range(2):
                        h = 2 * mj + hh
                        pbt, pbtk = ctx.psum_bf()
                        for ts in range(4):
                            p.emit("pe", lambda e, hh=hh, ts=ts, pbt=pbt, vtmp=vtmp:
                                   nc.tensor.transpose(pbt[:, ts * HD:(ts + 1) * HD], vtmp[hh * HD:(hh + 1) * HD, ts * P:(ts + 1) * P],
                                                       ctx.ident_bf[hh * HD:(hh + 1) * HD, hh * HD:(hh + 1) * HD]),
                                   reads=[("l0f_vtmp", mq), "c_identb", ("l0f_vt", h, cc)], writes=[pbtk], name=f"{pre}_vt{mj}_{hh}_{ts}")
                        dst = vtb.rearrange("p (h k d) -> p h k d", k=QT, d=HD)[:, h, cc * 4:(cc + 1) * 4, :]
                        p.emit("act", lambda e, pbt=pbt, dst=dst:
                               nc.scalar.copy(dst, pbt[:, 0:4 * HD].rearrange("p (k d) -> p k d", d=HD)),
                               reads=[pbtk], writes=[("l0f_vt", h, cc)], name=f"{pre}_vte{mj}_{hh}")
        ctx.arena_to(_mark2)
        qtiles = [ctx.scratch(f"l0f_qt{q}", [P, DT * P], BF16) for q in range(2)]
        pbufs = [ctx.scratch(f"l0f_pbuf{q}", [P, SEQ], BF16) for q in range(2)]
        ptbs = [ctx.scratch(f"l0f_ptb{q}", [P, QT * P], BF16) for q in range(2)]
        osbs = [ctx.scratch(f"l0f_osb{q}", [P, DT * P], BF16) for q in range(2)]
        nmx4s = [ctx.scratch(f"l0f_nmx{q}", [P, 8], F32) for q in range(2)]
        bias8s = [ctx.scratch(f"l0f_b8{q}", [P, 1], F32) for q in range(2)]
        rcps = [ctx.scratch(f"l0f_rcp{q}", [P, 1], F32) for q in range(2)]
        for qi in range(QT):
            apre = f"l0fA{b}_{qi}"
            kl = (qi + 1) * P
            nkb = ceil_div(kl, 512)
            qq = qi % 2
            qtile = qtiles[qq]
            osb = osbs[qq]
            p.dma("sp", qtile.rearrange("p (a t) -> p a t", t=P),
                  q_d[:, bt + qi * P: bt + (qi + 1) * P].rearrange("(a b) t -> b a t", b=P),
                  reads=[("q_d", b, qi // 4, m_) for m_ in range(8)] + [("l0f_qt", qq)],
                  writes=[("l0f_qt", qq)], name=f"{apre}_lq")
            def emit_score(h):
                hpre = f"{apre}h{h}"
                hq = h % 2
                pbuf = pbufs[hq]
                nmx4, bias8, rcp = nmx4s[hq], bias8s[hq], rcps[hq]
                hb2 = (h % 2) * HD
                qap = qtile[hb2:hb2 + HD, (h // 2) * P:(h // 2 + 1) * P]
                kap = lambda k0, k1, h=h: kfm[(h % 2) * HD:(h % 2) * HD + HD, (h // 2) * SEQ + k0:(h // 2) * SEQ + k1]
                sbanks = []
                for kb in range(nkb):
                    k0 = kb * 512
                    k1 = min(kl, k0 + 512)
                    pb, pbk = ctx.psum()
                    sbanks.append((pb, pbk, k0, k1))
                    p.emit("pe", lambda e, pb=pb, qap=qap, k0=k0, k1=k1, h=h:
                           nc.tensor.matmul(pb[:, 0:k1 - k0], qap, kap(k0, k1, h), start=True, stop=True),
                           reads=[("l0f_qt", qq)] + [("l0f_kfm", h // 2, cc_) for cc_ in range(CP)],
                           writes=[pbk], name=f"{hpre}_s{kb}")
                pbL, pbLk, k0L, k1L = sbanks[-1]
                wL = k1L - k0L
                p.emit("dve", lambda e, pbL=pbL, wL=wL:
                       nc.vector.tensor_add(pbL[:, wL - P:wL], pbL[:, wL - P:wL], ctx.cmask),
                       reads=[pbLk, "c_cmask"], writes=[pbLk], name=f"{hpre}_msk")
                for kb, (pb, pbk, k0, k1) in enumerate(sbanks):
                    p.emit("dve", lambda e, pb=pb, kb=kb, k0=k0, k1=k1, nmx4=nmx4:
                           nc.vector.reduce_max(nmx4[:, kb:kb + 1], pb[:, 0:k1 - k0], axis=AX.X, negate=True),
                           reads=[pbk], writes=[("l0f_nmx", hq, kb)], name=f"{hpre}_mx{kb}")
                p.emit("dve", lambda e, nkb=nkb, nmx4=nmx4, bias8=bias8:
                       nc.vector.tensor_reduce(bias8, nmx4[:, 0:nkb], axis=AX.X, op=ALU.min),
                       reads=[("l0f_nmx", hq, kb_) for kb_ in range(nkb)], writes=[("l0f_b8a", hq)], name=f"{hpre}_mn")
                p.emit("dve", lambda e, bias8=bias8: nc.vector.tensor_scalar_mul(bias8, bias8, 0.125),
                       reads=[("l0f_b8a", hq)], writes=[("l0f_b8", hq)], name=f"{hpre}_bs")
                for kb, (pb, pbk, k0, k1) in enumerate(sbanks):
                    p.emit("act", lambda e, pb=pb, kb=kb, k0=k0, k1=k1, pbuf=pbuf, bias8=bias8, nmx4=nmx4:
                           nc.scalar.activation(pbuf[:, k0:k1], pb[:, 0:k1 - k0], AF.Exp, scale=0.125,
                                                bias=bias8, accum_out=nmx4[:, 4 + kb:5 + kb]),
                           reads=[pbk, ("l0f_b8", hq)], writes=[("l0f_pb", hq, kb), ("l0f_sm", hq, kb)], name=f"{hpre}_ex{kb}")
                p.emit("dve", lambda e, nkb=nkb, rcp=rcp, nmx4=nmx4: nc.vector.reduce_sum(rcp, nmx4[:, 4:4 + nkb], axis=AX.X),
                       reads=[("l0f_sm", hq, kb_) for kb_ in range(nkb)], writes=[("l0f_rcp", hq)], name=f"{hpre}_sm")
                p.emit("dve", lambda e, rcp=rcp: nc.vector.reciprocal(rcp, rcp),
                       reads=[("l0f_rcp", hq)], writes=[("l0f_rcp", hq)], name=f"{hpre}_rc")
                p.emit("dve", lambda e, kl=kl, pbuf=pbuf, rcp=rcp: nc.vector.tensor_scalar_mul(pbuf[:, 0:kl], pbuf[:, 0:kl], rcp),
                       reads=[("l0f_rcp", hq)] + [("l0f_pb", hq, kb_) for kb_ in range(nkb)],
                       writes=[("l0f_pbn", hq)], name=f"{hpre}_nm")

            def emit_out(h):
                hpre = f"{apre}h{h}"
                hq = h % 2
                pbuf, ptb = pbufs[hq], ptbs[hq]
                hb2 = (h % 2) * HD
                ntr = qi + 1
                for g in range(ceil_div(ntr, 8)):
                    pbt, pbtk = ctx.psum_bf()
                    cnt = min(8, ntr - g * 8)
                    for u in range(cnt):
                        kt = g * 8 + u
                        p.emit("pe", lambda e, kt=kt, u=u, pbt=pbt, pbuf=pbuf:
                               nc.tensor.transpose(pbt[:, u * P:(u + 1) * P], pbuf[:, kt * P:(kt + 1) * P], ctx.ident_bf),
                               reads=[("l0f_pbn", hq), "c_identb", ("l0f_ptb", hq, g)], writes=[pbtk], name=f"{hpre}_pt{kt}")
                    p.emit("act", lambda e, pbt=pbt, g=g, cnt=cnt, ptb=ptb:
                           nc.scalar.copy(ptb[:, g * 8 * P: g * 8 * P + cnt * P], pbt[:, 0:cnt * P]),
                           reads=[pbtk], writes=[("l0f_ptb", hq, g)], name=f"{hpre}_pte{g}")
                po, pok = ctx.psum()
                poap = po[hb2:hb2 + HD, 0:P]
                tpos = (0, hb2)
                for kt in range(ntr):
                    p.emit("pe", lambda e, kt=kt, poap=poap, h=h, ntr=ntr, tpos=tpos, ptb=ptb:
                           nc.tensor.matmul(poap, vt_tile(h, kt), ptb[:, kt * P:(kt + 1) * P],
                                            start=(kt == 0), stop=(kt == ntr - 1), tile_position=tpos),
                           reads=[("l0f_ptb", hq, kt // 8), ("l0f_vt", h, kt // 4)], writes=[pok], name=f"{hpre}_pv{kt}")
                p.emit("act", lambda e, poap=poap, h=h, hb2=hb2, osb=osb:
                       nc.scalar.copy(osb[hb2:hb2 + HD, (h // 2) * P:(h // 2 + 1) * P], poap),
                       reads=[pok], writes=[("l0f_osb", qq, h)], name=f"{hpre}_oe")

            for h in range(NH):
                emit_score(h)
                if h > 0:
                    emit_out(h - 1)
            emit_out(NH - 1)
            p.dma("sp", o_d[:, bt + qi * P: bt + (qi + 1) * P].rearrange("(a b) t -> b a t", b=P),
                  osb.rearrange("p (a t) -> p a t", t=P),
                  reads=[("l0f_osb", qq, h_) for h_ in range(NH)], writes=[("o_d", b, qi), ("l0f_osb", qq)], name=f"{apre}_so")
        ctx.arena_to(_mark)

        woutb = ctx.scratch("l0f_woutb", [P, DT * DM], BF16)
        for kt in range(DT):
            p.dma("pool", woutb[:, kt * 1024:(kt + 1) * 1024], W["out_w"][kt * P:(kt + 1) * P, :],
                  writes=[("l0f_wout", kt)], name=f"l0fP{b}_wo{kt}")
        out_t = lambda kt, m0, m1: woutb[:, kt * 1024 + m0: kt * 1024 + m1]
        obuf = ctx.scratch("l0f_ob", [P, DT * TC], BF16)
        v32b = ctx.scratch("l0f_v32b", [P, DT * TC], F32)
        s1 = ctx.scratch("l0f_s1", [P, DT * TC], F32)
        x1f = ctx.scratch("l0f_x1f", [P, DT * TC], F32)
        x1b = ctx.scratch("l0f_x1b", [P, DT * TC], BF16)
        for cc in range(CP):
            pre = f"l0fP{b}_{cc}"
            t0 = bt + cc * TC
            for j in range(DT):
                p.dma("sp", obuf[:, j * TC:(j + 1) * TC],
                      o_d[j * P:(j + 1) * P, t0:t0 + TC],
                      reads=[("o_d", b, qi_) for qi_ in range(cc * 4, cc * 4 + 4)],
                      writes=[("l0f_ob", j)], name=f"{pre}_lo{j}")
                p.dma("sp", v32b[:, j * TC:(j + 1) * TC], v32_d[j * P:(j + 1) * P, t0:t0 + TC],
                      reads=[("v32_d", b, cc, j)], writes=[("l0f_v32b", j)], name=f"{pre}_lv{j}")
            o_rhs = [(obuf[:, j * TC:(j + 1) * TC], ("l0f_ob", j)) for j in range(DT)]
            for m in range(DT):
                pb, pbk = ctx.psum()
                ctx.mm(pb, pbk, [(out_t(j, m * P, (m + 1) * P), ("l0f_wout", j)) for j in range(DT)],
                       o_rhs, name=f"{pre}_om{m}")
                p.emit("dve", lambda e, pb=pb, m=m:
                       nc.vector.tensor_scalar_add(s1[:, m * TC:(m + 1) * TC], pb, outb[:, m:m + 1]),
                       reads=[pbk, outb_k], writes=[("l0f_s1", m)], name=f"{pre}_ob{m}")
                p.emit("dve", lambda e, m=m:
                       nc.vector.tensor_add(s1[:, m * TC:(m + 1) * TC], s1[:, m * TC:(m + 1) * TC], v32b[:, m * TC:(m + 1) * TC]),
                       reads=[("l0f_s1", m), ("l0f_v32b", m)], writes=[("l0f_s1", m)], name=f"{pre}_sv{m}")
            emit_pre_ln(ctx, lambda j: (s1[:, j * TC:(j + 1) * TC], ("l0f_s1", j)),
                        lambda j: (x1f[:, j * TC:(j + 1) * TC], ("l0f_x1f", j)),
                        name="l0fln1", gamma_beta=(n1w, n1w_k, n1b, n1b_k),
                        out2_tile=lambda j: (x1b[:, j * TC:(j + 1) * TC], ("l0f_x1b", j)))
            for j in range(DT):
                p.dma("sp", x1_d[j * P:(j + 1) * P, t0:t0 + TC], x1f[:, j * TC:(j + 1) * TC],
                      reads=[("l0f_x1f", j)], writes=[("x1_d", b * CP + cc, j)], name=f"{pre}_sx{j}")
                p.dma("sp", x1b_d[j * P:(j + 1) * P, t0:t0 + TC], x1b[:, j * TC:(j + 1) * TC],
                      reads=[("l0f_x1b", j)], writes=[("x1b_d", b * CP + cc, j)], name=f"{pre}_sxb{j}")
            for tt in range(TC // P):
                rpre = f"{pre}r{tt}"
                pb, pbk = ctx.psum()
                for j in range(DT):
                    p.emit("pe", lambda e, pb=pb, j=j, tt=tt:
                           nc.tensor.matmul(pb[:, 0:NR], x1f[:, j * TC + tt * P: j * TC + (tt + 1) * P],
                                            centT[:, j * NR:(j + 1) * NR], start=(j == 0), stop=(j == DT - 1)),
                           reads=[("l0f_x1f", j), ("l0f_centT", j)], writes=[pbk], name=f"{rpre}_mm{j}")
                sim = ctx.scratch("l0f_sim", [P, NR], F32)
                v2 = ctx.scratch("l0f_v2", [P, NR], F32)
                w_tm = ctx.scratch("l0f_wtm", [P, NR], BF16)
                nmr = ctx.scratch("l0f_nmr", [P, 1], F32)
                smr = ctx.scratch("l0f_smr", [P, 1], F32)
                m1c = ctx.scratch("l0f_m1", [P, 1], F32)
                m2c = ctx.scratch("l0f_m2", [P, 1], F32)
                tq = ctx.scratch("l0f_tq", [P, NR], F32)
                mk_ = ctx.scratch("l0f_mk", [P, NR], F32)
                p.emit("dve", lambda e, pb=pb: nc.vector.reduce_max(nmr, pb[:, 0:NR], axis=AX.X, negate=True),
                       reads=[pbk], writes=[("l0f_nmr",)], name=f"{rpre}_mx")
                p.emit("act", lambda e, pb=pb: nc.scalar.activation(sim, pb[:, 0:NR], AF.Exp, bias=nmr, accum_out=smr),
                       reads=[pbk, ("l0f_nmr",)], writes=[("l0f_sim",), ("l0f_smr",)], name=f"{rpre}_ex")
                p.emit("dve", lambda e: nc.vector.reciprocal(smr, smr),
                       reads=[("l0f_smr",)], writes=[("l0f_smr",)], name=f"{rpre}_rc")
                p.emit("dve", lambda e: nc.vector.tensor_scalar_mul(sim, sim, smr),
                       reads=[("l0f_sim",), ("l0f_smr",)], writes=[("l0f_sim",)], name=f"{rpre}_nm")
                p.emit("dve", lambda e: nc.vector.tensor_add(v2, sim, rbbc),
                       reads=[("l0f_sim",), ("l0f_rbbc",)], writes=[("l0f_v2",)], name=f"{rpre}_v2")
                p.emit("dve", lambda e: nc.vector.reduce_max(m1c, v2, axis=AX.X),
                       reads=[("l0f_v2",)], writes=[("l0f_m1",)], name=f"{rpre}_m1")
                p.emit("dve", lambda e: nc.vector.tensor_scalar_sub(tq, v2, m1c),
                       reads=[("l0f_v2",), ("l0f_m1",)], writes=[("l0f_tq",)], name=f"{rpre}_t")
                p.emit("dve", lambda e: nc.vector.tensor_single_scalar(mk_, tq, 0.0, ALU.is_equal),
                       reads=[("l0f_tq",)], writes=[("l0f_mk",)], name=f"{rpre}_k0")
                p.emit("dve", lambda e: nc.vector.scalar_tensor_tensor(tq, mk_, -1e30, tq, ALU.mult, ALU.add),
                       reads=[("l0f_tq",), ("l0f_mk",)], writes=[("l0f_tq",)], name=f"{rpre}_k1")
                p.emit("dve", lambda e: nc.vector.reduce_max(m2c, tq, axis=AX.X),
                       reads=[("l0f_tq",)], writes=[("l0f_m2",)], name=f"{rpre}_m2")
                p.emit("dve", lambda e: nc.vector.tensor_add(m2c, m2c, m1c),
                       reads=[("l0f_m2",), ("l0f_m1",)], writes=[("l0f_m2",)], name=f"{rpre}_m2b")
                p.emit("dve", lambda e: nc.vector.tensor_scalar(mk_, v2, m2c, None, ALU.is_ge),
                       reads=[("l0f_v2",), ("l0f_m2",)], writes=[("l0f_mk",)], name=f"{rpre}_kp")
                p.emit("dve", lambda e: nc.vector.tensor_mul(tq, sim, mk_),
                       reads=[("l0f_sim",), ("l0f_mk",)], writes=[("l0f_tq",)], name=f"{rpre}_gp")
                p.emit("dve", lambda e: nc.vector.reduce_sum(smr, tq, axis=AX.X),
                       reads=[("l0f_tq",)], writes=[("l0f_smr",)], name=f"{rpre}_gs")
                p.emit("dve", lambda e: nc.vector.reciprocal(smr, smr),
                       reads=[("l0f_smr",)], writes=[("l0f_smr",)], name=f"{rpre}_gr")
                p.emit("dve", lambda e: nc.vector.tensor_scalar_mul(w_tm, tq, smr),
                       reads=[("l0f_tq",), ("l0f_smr",)], writes=[("l0f_wtm",)], name=f"{rpre}_gw")
                pbt, pbtk = ctx.psum_bf()
                p.emit("pe", lambda e, pbt=pbt: nc.tensor.transpose(pbt[0:NR, 0:P], w_tm, ctx.ident_bf),
                       reads=[("l0f_wtm",), "c_identb"], writes=[pbtk], name=f"{rpre}_wt")
                p.emit("act", lambda e, pbt=pbt, t0=t0, tt=tt:
                       nc.scalar.copy(gateT[:, t0 + tt * P: t0 + (tt + 1) * P], pbt[0:NR, 0:P]),
                       reads=[pbtk], writes=[("l0f_gT", b * CP + cc, tt)], name=f"{rpre}_wte")
        ctx.arena_to(_mark)

    p.emit("dve", lambda e: nc.vector.engine_nop(),
           writes=[("l0f_wqkv", kt) for kt in range(DT)] + [("l0f_wout", kt) for kt in range(DT)] + [("l0f_wguard",)],
           name="l0f_wfree")
    hsh = ctx.scratch("l0f_hsh", [P, FT * TC], BF16)
    x1c = ctx.scratch("l0f_x1c", [P, DT * TC], BF16)
    x1fcs = [ctx.scratch(f"l0f_x1fc{q}", [P, TC], F32) for q in range(2)]
    wbc = ctx.scratch("l0f_wbc", [P, TC], BF16)
    t1ss = [ctx.scratch(f"l0f_t1{q}", [P, TC], F32) for q in range(2)]
    t2ss = [ctx.scratch(f"l0f_t2{q}", [P, TC], F32) for q in range(2)]
    sw1b = ctx.scratch("l0f_sw1", [P, DT * FF], BF16)
    sw2b = ctx.scratch("l0f_sw2", [P, FT * DM], BF16)
    for kt in range(DT):
        p.dma("pool", sw1b[:, kt * FF:(kt + 1) * FF], W["sw1"][kt * P:(kt + 1) * P, :],
              writes=[("l0f_sw1", kt)], name=f"l0f_sw1_{kt}")
    for kt in range(FT):
        p.dma("pool", sw2b[:, kt * DM:(kt + 1) * DM], W["sw2"][kt * P:(kt + 1) * P, :],
              writes=[("l0f_sw2", kt)], name=f"l0f_sw2_{kt}")
    for e in range(-1, NR):
        epre = f"l0fE{e}"
        if e >= 0:
            for kt in range(DT):
                p.dma("pool", rw1_t(kt, 0, FF), W["rw1"][e * DM + kt * P:e * DM + (kt + 1) * P, :],
                      reads=[("l0f_wguard",)], writes=[("l0f_rw1", kt)], name=f"{epre}_lw1{kt}")
            for kt in range(FT):
                p.dma("pool", rw2_t(kt, 0, DM), W["rw2"][e * FF + kt * P:e * FF + (kt + 1) * P, :],
                      reads=[("l0f_wguard",)], writes=[("l0f_rw2", kt)], name=f"{epre}_lw2{kt}")
        for ch in range(NCH):
            t0 = ch * TC
            cpre = f"{epre}c{ch}"
            for j in range(DT):
                p.dma("sp", x1c[:, j * TC:(j + 1) * TC], x1b_d[j * P:(j + 1) * P, t0:t0 + TC],
                      reads=[("x1b_d", ch, j)], writes=[("l0f_x1c", j)], name=f"{cpre}_lx{j}")
            x1_rhs = [(x1c[:, j * TC:(j + 1) * TC], ("l0f_x1c", j)) for j in range(DT)]
            if e >= 0:
                w1t, w1keys = rw1_t, [("l0f_rw1", j) for j in range(DT)]
                w2t, w2keys = rw2_t, [("l0f_rw2", j) for j in range(FT)]
                b1c, b1k, b2c, b2k = rbcols[e]
                act_f = AF.Gelu
            else:
                w1t, w1keys = (lambda kt, m0, m1: sw1b[:, kt * FF + m0:kt * FF + m1]), [("l0f_sw1", j) for j in range(DT)]
                w2t, w2keys = (lambda kt, m0, m1: sw2b[:, kt * DM + m0:kt * DM + m1]), [("l0f_sw2", j) for j in range(FT)]
                b1c, b1k, b2c, b2k = sb1c, sb1_k, sb2c, sb2_k
                act_f = AF.Gelu
            for fg in range(FT):
                pb, pbk = ctx.psum()
                ctx.mm(pb, pbk, [(w1t(j, fg * P, (fg + 1) * P), w1keys[j]) for j in range(DT)],
                       x1_rhs, name=f"{cpre}_h{fg}")
                p.emit("act", lambda e_, pb=pb, fg=fg, b1c=b1c:
                       nc.scalar.activation(hsh[:, fg * TC:(fg + 1) * TC], pb, act_f, bias=b1c[:, fg:fg + 1]),
                       reads=[pbk, b1k], writes=[("l0f_hsh", fg)], name=f"{cpre}_ha{fg}")
            if e >= 0:
                pbg, pbgk = ctx.psum()
                p.emit("pe", lambda e_, pbg=pbg, t0=t0, ee=e:
                       nc.tensor.matmul(pbg, ctx.oh8[:, ee * P:(ee + 1) * P], gateT[:, t0:t0 + TC], start=True, stop=True),
                       reads=["c_oh8"] + [("l0f_gT", c_, tt_) for c_ in range(NCH) for tt_ in range(TC // P)],
                       writes=[pbgk], name=f"{cpre}_gbc")
                p.emit("act", lambda e_, pbg=pbg: nc.scalar.copy(wbc, pbg),
                       reads=[pbgk], writes=[("l0f_wbc",)], name=f"{cpre}_gbe")
            h_rhs = [(hsh[:, j * TC:(j + 1) * TC], ("l0f_hsh", j)) for j in range(FT)]
            for m in range(DT):
                mq = m % 2
                x1fc, t1s, t2s = x1fcs[mq], t1ss[mq], t2ss[mq]
                pb, pbk = ctx.psum()
                ctx.mm(pb, pbk, [(w2t(j, m * P, (m + 1) * P), w2keys[j]) for j in range(FT)],
                       h_rhs, name=f"{cpre}_o{m}")
                if e < 0:
                    p.dma("sp", x1fc, x1_d[m * P:(m + 1) * P, t0:t0 + TC],
                          reads=[("x1_d", ch, m)], writes=[("l0f_x1fc", mq)], name=f"{cpre}_lf{m}")
                    p.emit("dve", lambda e_, pb=pb, m=m, b2c=b2c, t1s=t1s:
                           nc.vector.tensor_scalar_add(t1s, pb, b2c[:, m:m + 1]),
                           reads=[pbk, b2k], writes=[("l0f_t1", mq)], name=f"{cpre}_b2{m}")
                    p.emit("dve", lambda e_, t1s=t1s, t2s=t2s, x1fc=x1fc:
                           nc.vector.tensor_add(t2s, t1s, x1fc),
                           reads=[("l0f_t1", mq), ("l0f_x1fc", mq)], writes=[("l0f_t2", mq)], name=f"{cpre}_sx{m}")
                    p.dma("pool", y2_d[m * P:(m + 1) * P, t0:t0 + TC], t2s,
                          reads=[("l0f_t2", mq)], writes=[("y2_d", ch, m)], name=f"{cpre}_sy{m}")
                else:
                    p.emit("dve", lambda e_, pb=pb, m=m, b2c=b2c, t1s=t1s:
                           nc.vector.tensor_scalar_add(t1s, pb, b2c[:, m:m + 1]),
                           reads=[pbk, b2k], writes=[("l0f_t1", mq)], name=f"{cpre}_b2{m}")
                    p.emit("dve", lambda e_, t1s=t1s, t2s=t2s:
                           nc.vector.tensor_mul(t2s, t1s, wbc),
                           reads=[("l0f_t1", mq), ("l0f_wbc",)], writes=[("l0f_t2", mq)], name=f"{cpre}_gg{m}")
                    p.dma("pool", y2_d[m * P:(m + 1) * P, t0:t0 + TC], t2s,
                          reads=[("l0f_t2", mq), ("y2_d", ch, m)], writes=[("y2_d", ch, m)],
                          name=f"{cpre}_ay{m}", accum_op=ALU.add)
    ctx.arena_to(_mark)

    y2b = ctx.scratch("l0f_y2b", [P, DT * TC], F32)
    xfb = ctx.scratch("l0f_xfb", [P, DT * TC], F32)
    o2s = [ctx.scratch(f"l0f_o2{q}", [P, TC], F32) for q in range(2)]
    for ch in range(NCH):
        t0 = ch * TC
        fpre = f"l0fF{ch}"
        for j in range(DT):
            p.dma("sp", y2b[:, j * TC:(j + 1) * TC], y2_d[j * P:(j + 1) * P, t0:t0 + TC],
                  reads=[("y2_d", ch, j)], writes=[("l0f_y2b", j)], name=f"{fpre}_ly{j}")
            p.dma("sp", xfb[:, j * TC:(j + 1) * TC], xin_d[j * P:(j + 1) * P, t0:t0 + TC],
                  writes=[("l0f_xfb", j)], name=f"{fpre}_lx{j}")
        emit_pre_ln(ctx, lambda j: (y2b[:, j * TC:(j + 1) * TC], ("l0f_y2b", j)),
                    lambda j: (y2b[:, j * TC:(j + 1) * TC], ("l0f_y2b", j)),
                    name="l0fln2", gamma_beta=(n2w, n2w_k, n2b, n2b_k))
        for j in range(DT):
            o2 = o2s[j % 2]
            p.emit("dve", lambda e, j=j, o2=o2:
                   nc.vector.tensor_add(o2, y2b[:, j * TC:(j + 1) * TC], xfb[:, j * TC:(j + 1) * TC]),
                   reads=[("l0f_y2b", j), ("l0f_xfb", j)], writes=[("l0f_o2", j % 2)], name=f"{fpre}_add{j}")
            p.dma("sp", xout_d[j * P:(j + 1) * P, t0:t0 + TC], o2,
                  reads=[("l0f_o2", j % 2)], writes=[("xo", ch, j)], name=f"{fpre}_st{j}")



L3A_DEBUG = False


def emit_l3_attn(ctx, xin_d, xout_d, W):
    nc, p = ctx.nc, ctx.p
    if L3A_DEBUG:
        dbg_d = nc.dram_tensor("l3a_dbg", [NCH * DM, TC], F32, kind="Internal").ap()
        dbg2_d = nc.dram_tensor("l3a_dbg2", [NCH, TC], F32, kind="Internal").ap()
        dbg4_d = nc.dram_tensor("l3a_dbg4", [NCH, TC], F32, kind="Internal").ap()
        dbg5_d = nc.dram_tensor("l3a_dbg5", [NCH * P, TC], F32, kind="Internal").ap()
        dbg6_d = nc.dram_tensor("l3a_dbg6", [NCH, TC], F32, kind="Internal").ap()
        dbg3_d = nc.dram_tensor("l3a_dbg3", [NCH * DM, TC], F32, kind="Internal").ap()
    NHd = 16
    HD = 64
    NSt = 128
    TF = 32
    SEL = 28
    NSEL = 12
    CONVD = 1280
    CT10 = CONVD // P
    DIP = 2336
    HIN = 1056
    KCONV = 4
    EPS = 1e-5

    _, wip_t, _ = ctx.load_w_bf16_T("l3a_wip", W["in_proj_w"], DIP, DM)
    _, whp_t, _ = ctx.load_w_bf16_T("l3a_whp", W["h_proj_w"], 40, HIN)
    _, wop_t, _ = ctx.load_w_bf16_T("l3a_wop", W["out_proj_w"], DM, DM)
    nw_c, nw_k = ctx.vec_to_cols("l3a_nw", W["norm_w"], DM)
    cb_c, cb_k = ctx.vec_to_cols("l3a_cb", W["conv_b"], CONVD)
    cwraw = ctx.scratch("l3a_cwr", [CT10, P * KCONV], F32)
    p.dma("sp", cwraw, W["conv_w"].rearrange("(a b) k -> a (b k)", b=P), writes=[("l3a_cwr",)], name="l3a_cwld")
    cw = ctx.scratch("l3a_cw", [P, CT10 * KCONV], F32)
    for k in range(KCONV):
        pb, pbk = ctx.psum()
        ap_in = cwraw.rearrange("a (b k) -> a b k", k=KCONV)[:, :, k]
        p.emit("pe", lambda e, ap_in=ap_in, pb=pb: nc.tensor.transpose(pb[:, 0:CT10], ap_in, ctx.ident[:CT10, :CT10]),
               reads=[("l3a_cwr",), "c_ident"], writes=[pbk], name=f"l3a_cwtr{k}")
        p.emit("act", lambda e, pb=pb, k=k: nc.scalar.copy(
            cw.rearrange("p (j q) -> p j q", q=KCONV)[:, :, k], pb[:, 0:CT10]),
            reads=[pbk], writes=[("l3a_cw", k)], name=f"l3a_cwev{k}")
    cw_col = lambda j, k: cw[:, (j * KCONV + k):(j * KCONV + k) + 1]
    def row_bcast(tag, dram_ap, n):
        r = ctx.scratch(f"l3a_{tag}r", [1, n], F32)
        p.dma("sp", r, dram_ap.rearrange("(a b) -> a b", b=n), writes=[("l3a", tag, "r")], name=f"l3a_{tag}ld")
        ps_, psk_ = ctx.bcast_row(r, ("l3a", tag, "r"), n, name=f"l3a_{tag}bc")
        out = ctx.scratch(f"l3a_{tag}", [P, n], F32)
        p.emit("act", lambda e, ps_=ps_, out=out: nc.scalar.copy(out, ps_),
               reads=[psk_], writes=[("l3a", tag)], name=f"l3a_{tag}ev")
        return out, ("l3a", tag)
    gam_bc, gam_k = row_bcast("gam", W["gamma"], NHd)
    dtb_bc, dtb_k = row_bcast("dtb", W["dt_bias"], 1)
    dpar_bc, dpar_k = row_bcast("dp", W["D"], NHd)
    alog_bc, alog_k = row_bcast("alog", W["A_log"], NHd)
    negA = ctx.scratch("l3a_negA", [P, NHd], F32)
    p.emit("act", lambda e: nc.scalar.activation(negA, alog_bc, AF.Exp),
           reads=[alog_k], writes=[("l3a_negA1",)], name="l3a_nA1")
    p.emit("dve", lambda e: nc.vector.tensor_scalar_mul(negA, negA, -1.0),
           reads=[("l3a_negA1",)], writes=[("l3a_negA",)], name="l3a_nA2")
    invd_np = (1.0 / np.arange(1, SEQ + 1, dtype=np.float64)).astype(np.float32).reshape(1, SEQ)
    invd_d = nc.inline_tensor(invd_np, name="l3a_invd_d")
    invd = ctx.scratch("l3a_invd", [1, TC], F32)
    tsep_d = nc.dram_tensor("l3a_tsep", [NHd, TC], F32, kind="Internal").ap()
    trow1 = ctx.scratch("l3a_trow", [1, TC], F32)
    tri_np = np.where(np.arange(P)[None, :] >= np.arange(P)[:, None], 1.0, 0.0).astype(np.float32)
    tri_d = nc.inline_tensor(tri_np, name="l3a_tri_d")
    tri01 = ctx.scratch("l3a_tri", [P, P], BF16)
    p.dma("pool", tri01, tri_d.ap(), writes=[("l3a_tri",)], name="l3a_trild")

    cscar = ctx.scratch("l3a_cscar", [P, DT], F32)
    sT = ctx.scratch("l3a_sT", [P, NHd * HD], F32)
    sTb16 = ctx.scratch("l3a_sTb", [P, NHd * HD], BF16)
    zero1 = ctx.scratch("l3a_zero", [P, 1], F32)
    p.emit("dve", lambda e: nc.vector.memset(zero1, 0.0), writes=[("l3a_zero",)], name="l3a_z0")

    HW = TC + 3
    xbuf = ctx.scratch("l3a_xb", [P, DT * TC], F32)
    xnb = ctx.scratch("l3a_xn", [P, DT * TC], BF16)
    cst = ctx.scratch("l3a_cst", [P, TC], F32)
    srb = ctx.scratch("l3a_sr", [P, DT * TC], BF16)
    zb = ctx.scratch("l3a_zb", [P, DT * TC], BF16)
    xib = ctx.scratch("l3a_xi", [P, CT10 * HW], BF16)
    xcb = ctx.scratch("l3a_xc", [P, DT * TC], BF16)
    bcb = ctx.scratch("l3a_bc", [P, 2 * TC], BF16)
    dtaf = ctx.scratch("l3a_dtaf", [32, TC], F32)
    dtat = ctx.scratch("l3a_dtat", [P, 4 * 32], F32)
    hbs = ctx.scratch("l3a_hbs", [40, TC], F32)
    hbt = ctx.scratch("l3a_hbt", [P, 4 * 40], F32)
    dttm = ctx.scratch("l3a_dttm", [P, 4 * NHd], F32)
    tfm = ctx.scratch("l3a_tfm", [NHd, TC], F32)
    tcolb = ctx.scratch("l3a_tcol", [P, 4 * NHd], F32)
    trbs = [ctx.scratch(f"l3a_trb{q}", [P, TC], F32) for q in range(2)]
    GT_OFF = [0, TC, TC + (TC - P), TC + (TC - P) + (TC - 2 * P)]
    gtb = ctx.scratch("l3a_gt", [P, GT_OFF[3] + (TC - 3 * P)], BF16)
    gt_sl = lambda ii: gtb[:, GT_OFF[ii]:GT_OFF[ii] + (TC - ii * P)]
    xtb = ctx.scratch("l3a_xt", [P, NHd * 4 * HD], BF16)
    btb = ctx.scratch("l3a_bt", [P, 4 * P], BF16)
    cdec1 = ctx.scratch("l3a_cdec", [P, TC], BF16)
    yg = ctx.scratch("l3a_yg", [P, DT * TC], BF16)
    m8a = ctx.scratch("l3a_m8a", [P, 8], F32)
    m8b = ctx.scratch("l3a_m8b", [P, 8], F32)
    hb2t = ctx.scratch("l3a_hb2", [P, SEL], F32)
    t28 = ctx.scratch("l3a_t28", [P, SEL], F32)
    dcols = [ctx.scratch(f"l3a_dcol{q}", [P, 1], F32) for q in range(2)]
    e3cs = [ctx.scratch(f"l3a_e3{q}", [P, 1], F32) for q in range(2)]
    xws = [ctx.scratch(f"l3a_xw{q}", [P, HD], BF16) for q in range(2)]
    stmp1 = ctx.scratch("l3a_stmp", [P, HD], F32)
    rows3 = ctx.scratch("l3a_rows", [1, TC], F32)

    xt_tile = lambda h, ii: xtb[:, (h * 4 + ii) * HD:(h * 4 + ii + 1) * HD]

    for ch in range(NCH):
        pre = f"l3a{ch}"
        t0 = ch * TC
        b = ch // CPB
        first = (ch % CPB == 0)
        cloc = ch % CPB
        for j in range(DT):
            p.dma("sp", xbuf[:, j * TC:(j + 1) * TC], xin_d[j * P:(j + 1) * P, t0:t0 + TC],
                  writes=[("l3a_xb", j)], name=f"{pre}_ld{j}")
        emit_pre_ln(ctx, lambda j: (xbuf[:, j * TC:(j + 1) * TC], ("l3a_xb", j)),
                    lambda j: (xbuf[:, j * TC:(j + 1) * TC], ("l3a_xb", j)),
                    name="l3aln",
                    out2_tile=lambda j: (xnb[:, j * TC:(j + 1) * TC], ("l3a_xn", j)))
        p.dma("sp", invd, invd_d.ap()[:, cloc * TC:(cloc + 1) * TC],
              reads=[("l3a_invd",)], writes=[("l3a_invd",)], name=f"{pre}_ivld")
        pbI, pbIk = ctx.psum()
        p.emit("pe", lambda e, pbI=pbI:
               nc.tensor.matmul(pbI, ctx.ones_row, invd, start=True, stop=True),
               reads=[("l3a_invd",), "c_ones"], writes=[pbIk], name=f"{pre}_ibc")

        for j in range(DT):
            if first:
                p.emit("dve", lambda e, j=j: nc.vector.memset(cscar[:, j:j + 1], 0.0),
                       reads=[("l3a_cscar", j)], writes=[("l3a_cscar", j)], name=f"{pre}_cz{j}")
            p.emit("dve", lambda e, j=j:
                   nc.vector.tensor_tensor_scan(cst, xbuf[:, j * TC:(j + 1) * TC],
                                                zero1.to_broadcast([P, TC]), cscar[:, j:j + 1], ALU.add, ALU.add),
                   reads=[("l3a_xb", j), ("l3a_zero",), ("l3a_cscar", j), ("l3a_cst",)], writes=[("l3a_cst",)], name=f"{pre}_cs{j}")
            p.emit("dve", lambda e, j=j: nc.vector.tensor_copy(cscar[:, j:j + 1], cst[:, TC - 1:TC]),
                   reads=[("l3a_cst",)], writes=[("l3a_cscar", j)], name=f"{pre}_cc{j}")
            p.emit("dve", lambda e, j=j, pbI=pbI:
                   nc.vector.tensor_mul(cst, cst, pbI),
                   reads=[("l3a_cst",), pbIk, ("l3a_cscar", j)], writes=[("l3a_cst",)], name=f"{pre}_sm{j}")
            p.emit("dve", lambda e, j=j:
                   nc.vector.tensor_sub(srb[:, j * TC:(j + 1) * TC], xbuf[:, j * TC:(j + 1) * TC], cst),
                   reads=[("l3a_xb", j), ("l3a_cst",), ("l3a_srB",)], writes=[("l3a_sr", j), ("l3a_srB",)], name=f"{pre}_sr{j}")
        xn_rhs = [(xnb[:, j * TC:(j + 1) * TC], ("l3a_xn", j)) for j in range(DT)]
        for m in range(18):
            pb, pbk = ctx.psum()
            ctx.mm(pb, pbk, [(wip_t(j, m * P, (m + 1) * P), ("l3a_wip", j, m)) for j in range(DT)],
                   xn_rhs, name=f"{pre}_ip{m}")
            if m < DT:
                p.emit("act", lambda e, pb=pb, m=m: nc.scalar.activation(zb[:, m * TC:(m + 1) * TC], pb, AF.Silu),
                       reads=[pbk], writes=[("l3a_zb", m)], name=f"{pre}_ze{m}")
            else:
                mj = m - DT
                hk = ("l3a_xih", mj)
                if first:
                    p.emit("dve", lambda e, mj=mj: nc.vector.memset(xib[:, mj * HW:mj * HW + 3], 0.0),
                           reads=[("l3a_xi", mj)], writes=[hk], name=f"{pre}_hz{mj}")
                else:
                    p.emit("dve", lambda e, mj=mj:
                           nc.vector.tensor_copy(xib[:, mj * HW:mj * HW + 3], xib[:, (mj + 1) * HW - 3:(mj + 1) * HW]),
                           reads=[("l3a_xi", mj)], writes=[hk], name=f"{pre}_hc{mj}")
                p.emit("act", lambda e, pb=pb, mj=mj: nc.scalar.copy(xib[:, mj * HW + 3:(mj + 1) * HW], pb),
                       reads=[pbk, hk], writes=[("l3a_xi", mj)], name=f"{pre}_xe{mj}")
        pb, pbk = ctx.psum()
        ctx.mm(pb[0:32, 0:TC], pbk, [(wip_t(j, 18 * P, 18 * P + 32), ("l3a_wip", j, 18)) for j in range(DT)],
               xn_rhs, name=f"{pre}_ipdt")
        p.emit("act", lambda e, pb=pb: nc.scalar.copy(dtaf, pb[0:32, 0:TC]),
               reads=[pbk], writes=[("l3a_dtaf",)], name=f"{pre}_dte")
        for mj in range(CT10):
            mq = 0
            acc = ctx.scratch("l3a_cacc", [P, TC], F32)
            base = mj * HW
            p.emit("dve", lambda e, base=base, acc=acc, mj=mj:
                   nc.vector.tensor_scalar(acc, xib[:, base + 3:base + 3 + TC], cw_col(mj, 3), cb_c[:, mj:mj + 1], ALU.mult, ALU.add),
                   reads=[("l3a_xi", mj), ("l3a_xih", mj), ("l3a_cw", 3), cb_k], writes=[("l3a_cacc", mq)], name=f"{pre}_cv3_{mj}")
            for k in range(KCONV - 1):
                p.emit("dve", lambda e, base=base, acc=acc, mj=mj, k=k:
                       nc.vector.scalar_tensor_tensor(acc, xib[:, base + k:base + k + TC], cw_col(mj, k), acc, ALU.mult, ALU.add),
                       reads=[("l3a_xi", mj), ("l3a_xih", mj), ("l3a_cw", k), ("l3a_cacc", mq)],
                       writes=[("l3a_cacc", mq)], name=f"{pre}_cv{k}_{mj}")
            if mj < DT:
                p.emit("act", lambda e, acc=acc, mj=mj: nc.scalar.activation(xcb[:, mj * TC:(mj + 1) * TC], acc, AF.Silu),
                       reads=[("l3a_cacc", mq)], writes=[("l3a_xc", mj)], name=f"{pre}_cs{mj}")
            else:
                p.emit("act", lambda e, acc=acc, mj=mj:
                       nc.scalar.activation(bcb[:, (mj - DT) * TC:(mj - DT + 1) * TC], acc, AF.Silu),
                       reads=[("l3a_cacc", mq)], writes=[("l3a_bcb", mj - DT)], name=f"{pre}_cbs{mj}")
        pbh, pbhk = ctx.psum()
        dtab = ctx.scratch("l3a_dtab", [32, TC], BF16)
        p.emit("dve", lambda e: nc.vector.tensor_copy(dtab, dtaf),
               reads=[("l3a_dtaf",)], writes=[("l3a_dtab",)], name=f"{pre}_dtc")
        hp_lhs = [(whp_t(ct, 0, 40) if ct < 8 else whp_t(ct, 0, 40)[0:32, :], ("l3a_whp", ct)) for ct in range(9)]
        hp_rhs = [(srb[:, j * TC:(j + 1) * TC], ("l3a_sr", j)) for j in range(DT)] + [(dtab, ("l3a_dtab",))]
        for i_, ((la, lk), (ra, rk)) in enumerate(zip(hp_lhs, hp_rhs)):
            p.emit("pe", lambda e, la=la, ra=ra, i_=i_, pbh=pbh:
                   nc.tensor.matmul(pbh[0:40, 0:TC], la, ra, start=(i_ == 0), stop=(i_ == 8)),
                   reads=[lk, rk, ("l3a_srB",)], writes=[pbhk], name=f"{pre}_hp{i_}")
        p.emit("act", lambda e, pbh=pbh: nc.scalar.copy(hbs, pbh[0:40, 0:TC]),
               reads=[pbhk], writes=[("l3a_hbs",)], name=f"{pre}_hbe")
        for tt in range(4):
            pbt, pbtk = ctx.psum()
            p.emit("pe", lambda e, tt=tt, pbt=pbt:
                   nc.tensor.transpose(pbt[:, 0:40], hbs[:, tt * P:(tt + 1) * P], ctx.ident[0:40, 0:40]),
                   reads=[("l3a_hbs",), "c_ident"], writes=[pbtk], name=f"{pre}_htr{tt}")
            p.emit("act", lambda e, tt=tt, pbt=pbt: nc.scalar.copy(hbt[:, tt * 40:(tt + 1) * 40], pbt[:, 0:40]),
                   reads=[pbtk], writes=[("l3a_hbt", tt)], name=f"{pre}_hte{tt}")
            pbt2, pbt2k = ctx.psum()
            p.emit("pe", lambda e, tt=tt, pbt2=pbt2:
                   nc.tensor.transpose(pbt2[:, 0:32], dtaf[:, tt * P:(tt + 1) * P], ctx.ident[0:32, 0:32]),
                   reads=[("l3a_dtaf",), "c_ident"], writes=[pbt2k], name=f"{pre}_dtr{tt}")
            p.emit("act", lambda e, tt=tt, pbt2=pbt2: nc.scalar.copy(dtat[:, tt * 32:(tt + 1) * 32], pbt2[:, 0:32]),
                   reads=[pbt2k], writes=[("l3a_dtat", tt)], name=f"{pre}_dte{tt}")
        for tt in range(4):
            tpre = f"{pre}s{tt}"
            hsl = hbt[:, tt * 40: tt * 40 + SEL]
            p.emit("dve", lambda e, hsl=hsl: nc.vector.max(out=m8a, in_=hsl),
                   reads=[("l3a_hbt", tt)], writes=[("l3a_m8a",)], name=f"{tpre}_m8a")
            p.emit("dve", lambda e, hsl=hsl: nc.vector.match_replace(out=hb2t, in_to_replace=m8a, in_values=hsl, imm_value=-1e30),
                   reads=[("l3a_hbt", tt), ("l3a_m8a",)], writes=[("l3a_hb2",)], name=f"{tpre}_mr")
            p.emit("dve", lambda e: nc.vector.max(out=m8b, in_=hb2t),
                   reads=[("l3a_hb2",)], writes=[("l3a_m8b",)], name=f"{tpre}_m8b")
            dtt = dttm[:, tt * NHd:(tt + 1) * NHd]
            for jj in range(NSEL):
                mv = m8a[:, jj:jj + 1] if jj < 8 else m8b[:, jj - 8:jj - 7]
                mvk = ("l3a_m8a",) if jj < 8 else ("l3a_m8b",)
                p.emit("dve", lambda e, mv=mv, hsl=hsl, tt=tt:
                       nc.vector.scalar_tensor_tensor(t28, hsl, mv, dtat[:, tt * 32:tt * 32 + SEL], ALU.is_equal, ALU.mult),
                       reads=[("l3a_hbt", tt), mvk, ("l3a_dtat", tt)], writes=[("l3a_t28",)], name=f"{tpre}_g{jj}")
                p.emit("dve", lambda e, dtt=dtt, jj=jj: nc.vector.reduce_sum(dtt[:, jj:jj + 1], t28, axis=AX.X),
                       reads=[("l3a_t28",)], writes=[("l3a_dttm", tt, jj)], name=f"{tpre}_r{jj}")
            p.emit("dve", lambda e, dtt=dtt, tt=tt:
                   nc.vector.tensor_copy(dtt[:, NSEL:NHd], dtat[:, tt * 32 + SEL:tt * 32 + 32]),
                   reads=[("l3a_dtat", tt)], writes=[("l3a_dttm", tt, "sh")], name=f"{tpre}_sh")
            p.emit("dve", lambda e, dtt=dtt: nc.vector.tensor_scalar_add(dtt, dtt, dtb_bc[:, 0:1]),
                   reads=[("l3a_dttm", tt, jj_) for jj_ in range(NSEL)] + [("l3a_dttm", tt, "sh"), dtb_k],
                   writes=[("l3a_dttm", tt, "b")], name=f"{tpre}_db")
            sbt = ctx.scratch("l3a_sbt", [P, NSEL], F32)
            p.emit("dve", lambda e, tt=tt, sbt=sbt:
                   nc.vector.tensor_mul(sbt, hbt[:, tt * 40 + SEL:tt * 40 + 40], gam_bc[:, 0:NSEL]),
                   reads=[("l3a_hbt", tt), gam_k], writes=[("l3a_sbt",)], name=f"{tpre}_sb")
            p.emit("dve", lambda e, dtt=dtt, sbt=sbt:
                   nc.vector.tensor_add(dtt[:, 0:NSEL], dtt[:, 0:NSEL], sbt),
                   reads=[("l3a_dttm", tt, "b"), ("l3a_sbt",)], writes=[("l3a_dttm", tt, "b2")], name=f"{tpre}_sba")
            na = ctx.scratch("l3a_na", [P, NHd], F32)
            p.emit("dve", lambda e, dtt=dtt, na=na:
                   nc.vector.scalar_tensor_tensor(na, dtt, -1.0, dtt, ALU.mult, ALU.min),
                   reads=[("l3a_dttm", tt, "b2")], writes=[("l3a_na",)], name=f"{tpre}_na")
            p.emit("act", lambda e, na=na: nc.scalar.activation(na, na, AF.Exp),
                   reads=[("l3a_na",)], writes=[("l3a_na",)], name=f"{tpre}_ex")
            p.emit("act", lambda e, na=na: nc.scalar.activation(na, na, AF.Ln, bias=1.0),
                   reads=[("l3a_na",)], writes=[("l3a_na",)], name=f"{tpre}_ln")
            p.emit("dve", lambda e, dtt=dtt: nc.vector.tensor_scalar_max(dtt, dtt, 0.0),
                   reads=[("l3a_dttm", tt, "b2")], writes=[("l3a_dttm", tt, "rl")], name=f"{tpre}_rl")
            p.emit("dve", lambda e, dtt=dtt, na=na: nc.vector.tensor_add(dtt, dtt, na),
                   reads=[("l3a_dttm", tt, "rl"), ("l3a_na",)], writes=[("l3a_dttm", tt)], name=f"{tpre}_sp")
            pbt, pbtk = ctx.psum()
            p.emit("pe", lambda e, dtt=dtt, pbt=pbt:
                   nc.tensor.transpose(pbt[0:NHd, 0:P], dtt, ctx.ident),
                   reads=[("l3a_dttm", tt), "c_ident"], writes=[pbtk], name=f"{tpre}_ttr")
            p.emit("act", lambda e, tt=tt, pbt=pbt: nc.scalar.copy(tfm[:, tt * P:(tt + 1) * P], pbt[0:NHd, 0:P]),
                   reads=[pbtk], writes=[("l3a_tfm", tt)], name=f"{tpre}_tte")
        p.emit("dve", lambda e:
               nc.vector.tensor_tensor_scan(tfm, tfm, zero1[0:NHd, :].to_broadcast([NHd, TC]), 0.0, ALU.add, ALU.add),
               reads=[("l3a_tfm", tt_) for tt_ in range(4)] + [("l3a_zero",)], writes=[("l3a_T",)], name=f"{pre}_Tcs")
        for tt in range(4):
            pbt, pbtk = ctx.psum()
            p.emit("pe", lambda e, tt=tt, pbt=pbt:
                   nc.tensor.transpose(pbt[:, 0:NHd], tfm[:, tt * P:(tt + 1) * P], ctx.ident[0:NHd, 0:NHd]),
                   reads=[("l3a_T",), "c_ident"], writes=[pbtk], name=f"{pre}_Ttr{tt}")
            p.emit("act", lambda e, tt=tt, pbt=pbt: nc.scalar.copy(tcolb[:, tt * NHd:(tt + 1) * NHd], pbt[:, 0:NHd]),
                   reads=[pbtk], writes=[("l3a_tcol", tt)], name=f"{pre}_Tte{tt}")
        p.dma("sp", tsep_d, tfm, reads=[("l3a_T",), ("l3a_tsep",)], writes=[("l3a_tsep",)], name=f"{pre}_tst")
        for ii in range(4):
            wv = TC - ii * P
            pb, pbk = ctx.psum()
            p.emit("pe", lambda e, ii=ii, wv=wv, pb=pb:
                   nc.tensor.matmul(pb[:, 0:wv], bcb[:, ii * P:(ii + 1) * P], bcb[:, TC + ii * P:2 * TC], start=True, stop=True),
                   reads=[("l3a_bcb", 0), ("l3a_bcb", 1)], writes=[pbk], name=f"{pre}_g{ii}")
            p.emit("act", lambda e, ii=ii, wv=wv, pb=pb: nc.scalar.copy(gt_sl(ii), pb[:, 0:wv]),
                   reads=[pbk], writes=[("l3a_gt", ii)], name=f"{pre}_ge{ii}")
            pbt, pbtk = ctx.psum_bf()
            p.emit("pe", lambda e, ii=ii, pbt=pbt:
                   nc.tensor.transpose(pbt[:, 0:P], bcb[:, ii * P:(ii + 1) * P], ctx.ident_bf),
                   reads=[("l3a_bcb", 0), "c_identb"], writes=[pbtk], name=f"{pre}_btr{ii}")
            p.emit("act", lambda e, ii=ii, pbt=pbt: nc.scalar.copy(btb[:, ii * P:(ii + 1) * P], pbt[:, 0:P]),
                   reads=[pbtk], writes=[("l3a_btb", ii)], name=f"{pre}_bte{ii}")
        for jt in range(DT):
            for hh in range(2):
                h = 2 * jt + hh
                pbt, pbtk = ctx.psum_bf()
                for ii in range(4):
                    p.emit("pe", lambda e, jt=jt, hh=hh, ii=ii, pbt=pbt:
                           nc.tensor.transpose(pbt[:, ii * HD:(ii + 1) * HD],
                                               xcb[hh * HD:(hh + 1) * HD, jt * TC + ii * P:jt * TC + (ii + 1) * P],
                                               ctx.ident_bf[hh * HD:(hh + 1) * HD, hh * HD:(hh + 1) * HD]),
                           reads=[("l3a_xc", jt), "c_identb", ("l3a_xt", h)], writes=[pbtk], name=f"{pre}_xtr{h}_{ii}")
                p.emit("act", lambda e, h=h, pbt=pbt: nc.scalar.copy(xtb[:, h * 4 * HD:(h + 1) * 4 * HD], pbt[:, 0:4 * HD]),
                       reads=[pbtk], writes=[("l3a_xt", h)], name=f"{pre}_xte{h}")
        if first:
            p.emit("dve", lambda e: nc.vector.memset(sT, 0.0),
                   reads=[("l3a_sT", h_) for h_ in range(NHd)],
                   writes=[("l3a_sT", h_) for h_ in range(NHd)], name=f"{pre}_s0")
            p.emit("dve", lambda e: nc.vector.memset(sTb16, 0.0),
                   reads=[("l3a_sTb", h_) for h_ in range(NHd)],
                   writes=[("l3a_sTb", h_) for h_ in range(NHd)], name=f"{pre}_s0b")
        for h in range(NHd):
            hq = h % 2
            hpre = f"{pre}h{h}"
            nA = negA[:, h:h + 1]
            trb = trbs[hq]
            trow = trow1
            p.dma("sp", trow, tsep_d[h:h + 1, :], reads=[("l3a_tsep",), ("l3a_trow",)],
                  writes=[("l3a_trow",)], name=f"{hpre}_trl")
            pbr, pbrk = ctx.psum()
            p.emit("pe", lambda e, pbr=pbr, trow=trow:
                   nc.tensor.matmul(pbr, ctx.ones_row, trow, start=True, stop=True),
                   reads=[("l3a_trow",), "c_ones"], writes=[pbrk], name=f"{hpre}_trb")
            p.emit("act", lambda e, pbr=pbr, trb=trb: nc.scalar.copy(trb, pbr),
                   reads=[pbrk], writes=[("l3a_trb", hq)], name=f"{hpre}_trbe")
            ypb, ypbk = ctx.psum_fixed(4 + hq)
            hb2 = (h % 2) * HD
            ysl = ypb[hb2:hb2 + HD, 0:TC]
            cdec = cdec1
            drow = ctx.scratch("l3a_z1_1", [P, TC], F32)
            p.emit("act", lambda e, nA=nA, drow=drow, trb=trb: nc.scalar.activation(drow, trb, AF.Exp, scale=nA),
                   reads=[("l3a_trb", hq), ("l3a_negA",)], writes=[("l3a_z1", 1)], name=f"{hpre}_dr")
            p.emit("dve", lambda e, drow=drow, cdec=cdec: nc.vector.tensor_mul(cdec, bcb[:, TC:2 * TC], drow),
                   reads=[("l3a_bcb", 1), ("l3a_z1", 1)], writes=[("l3a_cdec",)], name=f"{hpre}_cd")
            p.emit("pe", lambda e, h=h, ysl=ysl, hb2=hb2, cdec=cdec:
                   nc.tensor.matmul(ysl, sTb16[:, h * HD:(h + 1) * HD], cdec, start=True, stop=False, tile_position=(0, hb2)),
                   reads=[("l3a_sTb", h), ("l3a_cdec",)], writes=[ypbk], name=f"{hpre}_yi")
            for ii in range(4):
                iq = ii % 2
                wv = TC - ii * P
                z1 = ctx.scratch(f"l3a_z1_{iq}", [P, TC], F32)
                e2 = ctx.scratch(f"l3a_e2_{iq}", [P, TC], F32)
                mt = ctx.scratch(f"l3a_mt_{iq}", [P, TC], BF16)
                p.emit("dve", lambda e, ii=ii, wv=wv, z1=z1, h=h, trb=trb:
                       nc.vector.tensor_scalar(z1[:, 0:wv], trb[:, ii * P:TC], tcolb[:, ii * NHd + h:ii * NHd + h + 1], 0.0, ALU.subtract, ALU.max),
                       reads=[("l3a_trb", hq), ("l3a_tcol", ii)], writes=[("l3a_z1", iq)], name=f"{hpre}_z{ii}")
                p.emit("act", lambda e, wv=wv, z1=z1, e2=e2, nA=nA:
                       nc.scalar.activation(e2[:, 0:wv], z1[:, 0:wv], AF.Exp, scale=nA),
                       reads=[("l3a_z1", iq), ("l3a_negA",)], writes=[("l3a_e2", iq)], name=f"{hpre}_e{ii}")
                p.emit("dve", lambda e, wv=wv, e2=e2, mt=mt, ii=ii, h=h:
                       nc.vector.scalar_tensor_tensor(mt[:, 0:wv], e2[:, 0:wv], dttm[:, ii * NHd + h:ii * NHd + h + 1],
                                                      gt_sl(ii), ALU.mult, ALU.mult),
                       reads=[("l3a_e2", iq), ("l3a_dttm", ii), ("l3a_gt", ii)], writes=[("l3a_mt", iq)], name=f"{hpre}_m{ii}")
                p.emit("dve", lambda e, mt=mt:
                       nc.vector.tensor_mul(mt[:, 0:P], mt[:, 0:P], tri01),
                       reads=[("l3a_mt", iq), ("l3a_tri",)], writes=[("l3a_mt", iq)], name=f"{hpre}_tm{ii}")
                p.emit("pe", lambda e, ii=ii, wv=wv, h=h, ysl=ypb, hb2=hb2, mt=mt:
                       nc.tensor.matmul(ypb[hb2:hb2 + HD, ii * P:TC], xt_tile(h, ii), mt[:, 0:wv],
                                        start=False, stop=(ii == 3), tile_position=(0, hb2)),
                       reads=[("l3a_mt", iq), ("l3a_xt", h)], writes=[ypbk], name=f"{hpre}_ym{ii}")
            p.emit("dve", lambda e, h=h, hb2=hb2, ypb=ypb:
                   nc.vector.scalar_tensor_tensor(yg[hb2 + (h // 2) * 0:0, :] if False else yg.rearrange("p (j t) -> p j t", t=TC)[hb2:hb2 + HD, h // 2, :],
                                                  xcb.rearrange("p (j t) -> p j t", t=TC)[hb2:hb2 + HD, h // 2, :],
                                                  dpar_bc[hb2:hb2 + HD, h:h + 1], ypb[hb2:hb2 + HD, 0:TC], ALU.mult, ALU.add),
                   reads=[ypbk, ("l3a_xc", h // 2), dpar_k, ("l3a_ygB",)], writes=[("l3a_yg", h), ("l3a_ygB",)], name=f"{hpre}_ye")
            spb, spbk = ctx.psum()
            for ii in range(4):
                iq = ii % 2
                dcol = dcols[iq]
                xw = xws[iq]
                p.emit("dve", lambda e, ii=ii, h=h, dcol=dcol, trb=trb:
                       nc.vector.tensor_scalar(dcol, trb[:, TC - 1:TC], tcolb[:, ii * NHd + h:ii * NHd + h + 1], None, ALU.subtract),
                       reads=[("l3a_tcol", ii), ("l3a_trb", hq)], writes=[("l3a_dcol", iq)], name=f"{hpre}_w{ii}a")
                p.emit("act", lambda e, nA=nA, dcol=dcol: nc.scalar.activation(dcol, dcol, AF.Exp, scale=nA),
                       reads=[("l3a_dcol", iq), ("l3a_negA",)], writes=[("l3a_dcol", iq)], name=f"{hpre}_w{ii}b")
                p.emit("dve", lambda e, ii=ii, h=h, dcol=dcol:
                       nc.vector.tensor_mul(dcol, dcol, dttm[:, ii * NHd + h:ii * NHd + h + 1]),
                       reads=[("l3a_dcol", iq), ("l3a_dttm", ii)], writes=[("l3a_dcol", iq)], name=f"{hpre}_w{ii}c")
                p.emit("dve", lambda e, ii=ii, h=h, dcol=dcol, xw=xw:
                       nc.vector.tensor_scalar_mul(xw, xt_tile(h, ii), dcol),
                       reads=[("l3a_xt", h), ("l3a_dcol", iq)], writes=[("l3a_xw", iq)], name=f"{hpre}_w{ii}d")
                p.emit("pe", lambda e, ii=ii, h=h, spb=spb, xw=xw:
                       nc.tensor.matmul(spb[:, 0:HD], btb[:, ii * P:(ii + 1) * P], xw,
                                        start=(ii == 0), stop=(ii == 3)),
                       reads=[("l3a_btb", ii), ("l3a_xw", iq)], writes=[spbk], name=f"{hpre}_sm{ii}")
            e3c = e3cs[hq]
            stmp = stmp1
            p.emit("act", lambda e, h=h, nA=nA, e3c=e3c, trb=trb:
                   nc.scalar.activation(e3c, trb[:, TC - 1:TC], AF.Exp, scale=nA),
                   reads=[("l3a_trb", hq), ("l3a_negA",)], writes=[("l3a_e3", hq)], name=f"{hpre}_de")
            p.emit("dve", lambda e, h=h, e3c=e3c, stmp=stmp:
                   nc.vector.tensor_scalar_mul(stmp, sT[:, h * HD:(h + 1) * HD], e3c),
                   reads=[("l3a_sT", h), ("l3a_e3", hq)], writes=[("l3a_stmp",)], name=f"{hpre}_sd")
            p.emit("dve", lambda e, h=h, spb=spb, stmp=stmp:
                   nc.vector.tensor_add(sT[:, h * HD:(h + 1) * HD], stmp, spb[:, 0:HD]),
                   reads=[("l3a_stmp",), spbk], writes=[("l3a_sT", h)], name=f"{hpre}_su")
            p.emit("dve", lambda e, h=h:
                   nc.vector.tensor_copy(sTb16[:, h * HD:(h + 1) * HD], sT[:, h * HD:(h + 1) * HD]),
                   reads=[("l3a_sT", h)], writes=[("l3a_sTb", h)], name=f"{hpre}_sb")
        sq8 = srb
        for j in range(DT):
            p.emit("dve", lambda e, j=j:
                   nc.vector.tensor_mul(yg[:, j * TC:(j + 1) * TC], yg[:, j * TC:(j + 1) * TC], zb[:, j * TC:(j + 1) * TC]),
                   reads=[("l3a_yg", 2 * j), ("l3a_yg", 2 * j + 1), ("l3a_zb", j), ("l3a_ygB",)],
                   writes=[("l3a_yz", j), ("l3a_ygB",)], name=f"{pre}_yz{j}")
            p.emit("act", lambda e, j=j: nc.scalar.activation(sq8[:, j * TC:(j + 1) * TC], yg[:, j * TC:(j + 1) * TC], AF.Square),
                   reads=[("l3a_yz", j), ("l3a_ygB",), ("l3a_srB",)], writes=[("l3a_sq8", j), ("l3a_sr", j), ("l3a_srB",)], name=f"{pre}_sq{j}")
            if L3A_DEBUG:
                p.dma("pool", dbg_d[(ch * DT + j) * P:(ch * DT + j + 1) * P, :], yg[:, j * TC:(j + 1) * TC],
                      reads=[("l3a_yz", j), ("l3a_ygB",)], writes=[("l3a_dbgw", ch, j)], name=f"{pre}_dbg{j}")
        pbS, pbSk = ctx.psum()
        ctx.mm(pbS[0:1, 0:TC], pbSk, [(ctx.ones_col_bf, "c_onesb")] * DT,
               [(sq8[:, j * TC:(j + 1) * TC], ("l3a_sq8", j)) for j in range(DT)], name=f"{pre}_ssq",
               extra_reads=[("l3a_srB",)])
        ms_row = rows3[:, 0:TC]
        p.emit("dve", lambda e, ms_row=ms_row, pbS=pbS: nc.vector.tensor_scalar_mul(
                   ms_row, pbS[0:1, 0:TC], 1.0 / DM),
               reads=[pbSk], writes=[("l3a_ms",)], name=f"{pre}_ms")
        if L3A_DEBUG:
            p.dma("pool", dbg4_d[ch:ch + 1, :], ms_row, reads=[("l3a_ms",)],
                  writes=[("l3a_dbg4", ch)], name=f"{pre}_dbgms")
            msact = ctx.scratch("l3a_trow", [1, TC], F32)
            p.emit("act", lambda e, pbS=pbS, msact=msact: nc.scalar.copy(msact, pbS[0:1, 0:TC]),
                   reads=[pbSk, ("l3a_trow",)], writes=[("l3a_trow",)], name=f"{pre}_dbgac")
            p.dma("pool", dbg6_d[ch:ch + 1, :], msact, reads=[("l3a_trow",)],
                  writes=[("l3a_dbg6", ch)], name=f"{pre}_dbgm6")
            p.dma("pool", dbg5_d[ch * P:(ch + 1) * P, :], sq8[:, 0:TC],
                  reads=[("l3a_sq8", 0), ("l3a_srB",)], writes=[("l3a_dbg5", ch)], name=f"{pre}_dbgsq")
        epsc, epsk = ctx.constcol(EPS)
        p.emit("act", lambda e, ms_row=ms_row: nc.scalar.activation(
                   ms_row, ms_row, AF.Sqrt, bias=epsc[0:1, :]),
               reads=[("l3a_ms",), epsk], writes=[("l3a_ms2",), ("l3a_ms",)], name=f"{pre}_msq")
        p.emit("dve", lambda e, ms_row=ms_row: nc.vector.reciprocal(ms_row, ms_row),
               reads=[("l3a_ms2",)], writes=[("l3a_ms3",)], name=f"{pre}_mrc")
        if L3A_DEBUG:
            p.dma("pool", dbg2_d[ch:ch + 1, :], ms_row,
                  reads=[("l3a_ms3",)], writes=[("l3a_dbg2", ch)], name=f"{pre}_dbgm")
        rb_, rbk_ = ctx.bcast_row(ms_row, ("l3a_ms3",), TC, name=f"{pre}_rbc")
        for j in range(DT):
            t1 = ctx.scratch("l3a_cst", [P, TC], F32)
            p.emit("dve", lambda e, j=j, t1=t1, rb_=rb_:
                   nc.vector.tensor_mul(t1, yg[:, j * TC:(j + 1) * TC], rb_),
                   reads=[("l3a_yz", j), ("l3a_ygB",), rbk_, ("l3a_cst",)], writes=[("l3a_cst",)], name=f"{pre}_on{j}")
            p.emit("dve", lambda e, j=j, t1=t1:
                   nc.vector.tensor_scalar_mul(xcb[:, j * TC:(j + 1) * TC], t1, nw_c[:, j:j + 1]),
                   reads=[("l3a_cst",), nw_k, ("l3a_xc", j)], writes=[("l3a_xc", j)], name=f"{pre}_ow{j}")
        if L3A_DEBUG:
            for j in range(DT):
                p.dma("pool", dbg3_d[(ch * DT + j) * P:(ch * DT + j + 1) * P, :], xcb[:, j * TC:(j + 1) * TC],
                      reads=[("l3a_xc", j)], writes=[("l3a_dbg3", ch, j)], name=f"{pre}_dbn{j}")
        on_rhs = [(xcb[:, j * TC:(j + 1) * TC], ("l3a_xc", j)) for j in range(DT)]
        for m in range(DT):
            pb, pbk = ctx.psum()
            ctx.mm(pb, pbk, [(wop_t(j, m * P, (m + 1) * P), ("l3a_wop", j, m)) for j in range(DT)],
                   on_rhs, name=f"{pre}_op{m}")
            xres = ctx.scratch(f"l3a_z1_{m % 2}", [P, TC], F32)
            p.dma("sp", xres, xin_d[m * P:(m + 1) * P, t0:t0 + TC],
                  reads=[("l3a_z1", m % 2)], writes=[("l3a_z1", m % 2)], name=f"{pre}_lr{m}")
            xo = ctx.scratch(f"l3a_e2_{m % 2}", [P, TC], F32)
            p.emit("dve", lambda e, pb=pb, m=m, xo=xo, xres=xres:
                   nc.vector.tensor_add(xo, pb, xres),
                   reads=[pbk, ("l3a_z1", m % 2), ("l3a_e2", m % 2)], writes=[("l3a_e2", m % 2)], name=f"{pre}_res{m}")
            p.dma("sp", xout_d[m * P:(m + 1) * P, t0:t0 + TC], xo,
                  reads=[("l3a_e2", m % 2)], writes=[("xo", ch, m), ("l3a_e2", m % 2)], name=f"{pre}_st{m}")



SUBLAYERS = {
    "L2_ffn": (emit_l2_ffn, ["W_r", "W_k", "W_v", "mu_r", "mu_k"]),
    "L0_attn": (emit_l0_attn, ["W_in", "conv_w", "conv_b", "W_xproj", "W_dt",
                               "b_dt", "A_log", "D", "W_out"]),
    "L2_attn": (emit_l2_attn, ["W_in", "b_in", "conv_w", "conv_b", "W_x", "b_x",
                               "A_log", "D_param", "W_out", "b_out"]),
    "L1_ffn": (emit_l1_ffn, ["W1", "b1", "W2", "b2", "Wg", "bg"]),
    "L1_attn": (emit_l1_attn, ["Wq", "Wk", "Wv", "Wfc"]),
    "L3_attn": (emit_l3_attn, ["in_proj_w", "h_proj_w", "conv_w", "conv_b", "dt_bias",
                               "gamma", "A_log", "D", "norm_w", "out_proj_w"]),
    "L0_ffn": (emit_l0_ffn, ["qkv_w", "qkv_b", "out_w", "out_b", "norm1_w", "norm1_b",
                             "sw1", "sb1", "sw2", "sb2", "rw1", "rb1", "rw2", "rb2",
                             "centroid", "route_bias", "norm2_w", "norm2_b"]),
    "L3_ffn": (emit_l3_ffn, ["Wr", "br", "W1", "b1", "W2", "b2"]),
}

WEIGHT_SHAPES = {
    "L2_ffn": {"W_r": [DM, DM], "W_k": [DM, DM], "W_v": [DM, DM],
               "mu_r": [DM], "mu_k": [DM]},
    "L0_attn": {"W_in": [DM, 4096], "conv_w": [2048, 4], "conv_b": [2048],
                "W_xproj": [2048, 2080], "W_dt": [2048, 2048], "b_dt": [2048],
                "A_log": [2048, 16], "D": [2048], "W_out": [2048, DM]},
    "L2_attn": {"W_in": [DM, 2048], "b_in": [2048], "conv_w": [DM, 4], "conv_b": [DM],
                "W_x": [DM, 32], "b_x": [32], "A_log": [DM, 16], "D_param": [DM],
                "W_out": [DM, DM], "b_out": [DM]},
    "L1_ffn": {"W1": [8 * DM, 4096], "b1": [8, 4096], "W2": [8 * 4096, DM],
               "b2": [8, DM], "Wg": [DM, 8], "bg": [8]},
    "L1_attn": {"Wq": [DM, DM], "Wk": [DM, DM], "Wv": [DM, DM], "Wfc": [DM, DM]},
    "L3_attn": {"in_proj_w": [2336, DM], "h_proj_w": [40, 1056], "conv_w": [1280, 4],
                "conv_b": [1280], "dt_bias": [1], "gamma": [16], "A_log": [16],
                "D": [16], "norm_w": [DM], "out_proj_w": [DM, DM]},
    "L0_ffn": {"qkv_w": [DM, 3072], "qkv_b": [3072], "out_w": [DM, DM], "out_b": [DM],
               "norm1_w": [DM], "norm1_b": [DM], "sw1": [DM, 2048], "sb1": [1, 2048],
               "sw2": [2048, DM], "sb2": [1, DM], "rw1": [8 * DM, 2048], "rb1": [8, 2048],
               "rw2": [8 * 2048, DM], "rb2": [8, DM], "centroid": [8, DM],
               "route_bias": [8], "norm2_w": [DM], "norm2_b": [DM]},
    "L3_ffn": {"Wr": [DM, 8], "br": [8], "W1": [8 * DM, 4096], "b1": [8, 4096],
               "W2": [8 * 4096, DM], "b2": [8, DM]},
}


def build_sub_nc(sub_name):
    nc = bass.Bass("TRN2", target_bir_lowering=False)
    p = Prog(nc)
    ctx = Ctx(nc, p)
    xin = nc.dram_tensor("xin", [DM, TOK], F32, kind="ExternalInput").ap()
    xout = nc.dram_tensor("xout", [DM, TOK], F32, kind="ExternalOutput").ap()
    emit, wnames = SUBLAYERS[sub_name]
    shapes = WEIGHT_SHAPES[sub_name]
    W = {w: nc.dram_tensor("w_" + w, list(shapes[w]), F32, kind="ExternalInput").ap()
         for w in wnames}
    emit(ctx, xin, xout, W)
    p.build()
    return nc


def run_sub(sub_name, x_fm_shards, weights, cores=None):
    nc = build_sub_nc(sub_name)
    _, wnames = SUBLAYERS[sub_name]
    shapes = WEIGHT_SHAPES[sub_name]
    wprep = {("w_" + w): np.ascontiguousarray(
                 np.asarray(weights[w], np.float32).reshape(shapes[w]))
             for w in wnames}
    in_maps = []
    for s in x_fm_shards:
        m = {"xin": s}
        m.update(wprep)
        in_maps.append(m)
    if cores is None:
        cores = list(range(len(x_fm_shards)))
    res = run_bass_kernel_spmd(nc, in_maps, core_ids=cores)
    return [np.asarray(r["xout"], np.float32) for r in res.results]



SUB_ORDER = ["L0_attn", "L0_ffn", "L1_attn", "L1_ffn",
             "L2_attn", "L2_ffn", "L3_attn", "L3_ffn"]

N_CORES = 8


def build_full_nc():
    nc = bass.Bass("TRN2", target_bir_lowering=False)
    p = Prog(nc)
    ctx = Ctx(nc, p)
    xin = nc.dram_tensor("xin", [DM, TOK], F32, kind="ExternalInput").ap()
    xout = nc.dram_tensor("xout", [DM, TOK], F32, kind="ExternalOutput").ap()
    acts = [nc.dram_tensor(f"act{i}", [DM, TOK], F32, kind="Internal").ap()
            for i in range(len(SUB_ORDER) - 1)]
    bufs = [xin] + acts + [xout]
    for i, sub in enumerate(SUB_ORDER):
        if i > 0:
            ctx.arena_reset()
        emit, wnames = SUBLAYERS[sub]
        shapes = WEIGHT_SHAPES[sub]
        W = {w: nc.dram_tensor(f"w_{sub}_{w}", list(shapes[w]),
                               F32, kind="ExternalInput").ap()
             for w in wnames}
        emit(ctx, bufs[i], bufs[i + 1], W)
    p.build()
    return nc


def prep_full_weights(inputs):
    wprep = {}
    for sub in SUB_ORDER:
        _, wnames = SUBLAYERS[sub]
        shapes = WEIGHT_SHAPES[sub]
        for w in wnames:
            wprep[f"w_{sub}_{w}"] = np.ascontiguousarray(
                np.asarray(inputs[f"{sub}_{w}"], np.float32).reshape(shapes[w]))
    return wprep


def kernel(**inputs):
    x = np.asarray(inputs["L0_attn_x"], np.float32)
    B, S, D = x.shape
    bpc = B // N_CORES
    shards = [np.ascontiguousarray(x[c * bpc:(c + 1) * bpc].reshape(bpc * S, D).T)
              for c in range(N_CORES)]
    nc = build_full_nc()
    wprep = prep_full_weights(inputs)
    in_maps = []
    for s in shards:
        m = {"xin": s}
        m.update(wprep)
        in_maps.append(m)
    res = run_bass_kernel_spmd(nc, in_maps, core_ids=list(range(N_CORES)))
    outs = [np.asarray(r["xout"], np.float32) for r in res.results]
    return np.concatenate(
        [s.T.reshape(bpc, S, D) for s in outs], axis=0).astype(np.float32)



# version 06
# speedup vs baseline: 1.3162x; 1.0330x over previous
import math
import numpy as np

import concourse.bass as bass
import concourse.mybir as mybir
from concourse.bass_utils import run_bass_kernel_spmd

F32 = mybir.dt.float32
BF16 = mybir.dt.bfloat16
AF = mybir.ActivationFunctionType
ALU = mybir.AluOpType
AX = mybir.AxisListType

P = 128

ENGINE_INFO = {
    "pe": ("tensor", "tensor"),
    "act": ("scalar", "scalar"),
    "dve": ("vector", "vector"),
    "pool": ("gpsimd", "gpsimd"),
    "sp": ("sync", "sync"),
}

SAME_ENGINE_SYNC = True


class _Op:
    __slots__ = ("idx", "eng", "fn", "reads", "writes", "name", "is_dma",
                 "deps", "inc_val", "waits", "dma_slot", "dma_cycle")

    def __init__(self, idx, eng, fn, reads, writes, name, is_dma):
        self.idx = idx
        self.eng = eng
        self.fn = fn
        self.reads = reads
        self.writes = writes
        self.name = name
        self.is_dma = is_dma
        self.deps = []
        self.inc_val = None
        self.waits = []
        self.dma_slot = None
        self.dma_cycle = None


class Prog:
    def __init__(self, nc: bass.Bass):
        self.nc = nc
        self.ops = []
        self._lw = {}
        self._rd = {}
        self.sems = {}
        for e in ENGINE_INFO:
            self.sems[e] = nc.alloc_semaphore(f"s_{e}")
        self.DMA_SLOTS = {"sp": 12, "pool": 12, "act": 4}
        self.dsems = {q: [nc.alloc_semaphore(f"d_{q}{i}") for i in range(n)]
                      for q, n in self.DMA_SLOTS.items()}
        self._elast = {}
        self._qlast = {}
        self._qcount = {}

    def emit(self, eng, fn, reads=(), writes=(), name="", dma=False, extra_deps=()):
        i = len(self.ops)
        op = _Op(i, eng, fn, tuple(reads), tuple(writes), name, dma)
        deps = set(extra_deps)
        for k in op.reads:
            j = self._lw.get(k)
            if j is not None:
                deps.add(j)
        for k in op.writes:
            j = self._lw.get(k)
            if j is not None:
                deps.add(j)
            deps.update(self._rd.get(k, ()))
        deps.discard(i)
        op.deps = sorted(deps)
        for k in op.reads:
            self._rd.setdefault(k, set()).add(i)
        for k in op.writes:
            self._lw[k] = i
            self._rd[k] = set()
        self.ops.append(op)
        if dma:
            self._qlast.setdefault(eng, {})
            R = self.DMA_SLOTS[eng]
            self._qlast[eng][self._qcount.get(eng, 0) % R] = i
            self._qcount[eng] = self._qcount.get(eng, 0) + 1
        else:
            self._elast[eng] = i
        return i

    def fence(self):
        deps = set()
        for e, i in self._elast.items():
            deps.add(i)
        for q, slots in self._qlast.items():
            deps.update(slots.values())
        nc = self.nc
        eng_attr = {e: a for e, (a, _) in ENGINE_INFO.items()}
        for e in ENGINE_INFO:
            self.emit(e, lambda eng, a=eng_attr[e]: getattr(nc, a).nop(),
                      name=f"fence_{e}", extra_deps=sorted(deps))

    def dma(self, queue, out_ap, in_ap, reads=(), writes=(), name="dma", **kw):
        attr = ENGINE_INFO[queue][0]

        def fn(eng, nc=self.nc, out_ap=out_ap, in_ap=in_ap, kw=kw):
            return getattr(nc, attr).dma_start(out_ap, in_ap, **kw)

        return self.emit(queue, fn, reads, writes, name, dma=True)

    def finalize(self):
        nc = self.nc
        need_inc = set()
        filt_deps = {}
        for op in self.ops:
            fd = []
            for j in op.deps:
                dj = self.ops[j]
                if dj.eng == op.eng and not dj.is_dma and not op.is_dma:
                    if dj.eng == "pe":
                        continue
                    if not SAME_ENGINE_SYNC:
                        continue
                fd.append(j)
            filt_deps[op.idx] = fd
            need_inc.update(fd)
        eng_count = {e: 0 for e in ENGINE_INFO}
        dma_count = {q: 0 for q in self.dsems}
        for op in self.ops:
            if op.is_dma:
                i = dma_count[op.eng]
                dma_count[op.eng] = i + 1
                R = self.DMA_SLOTS[op.eng]
                op.dma_slot = i % R
                op.dma_cycle = i // R
                op.inc_val = (op.dma_cycle + 1) * 16
            elif op.idx in need_inc:
                eng_count[op.eng] += 1
                op.inc_val = eng_count[op.eng]
        waited = {e: {} for e in ENGINE_INFO}
        for op in self.ops:
            wl = {}

            def add_wait(sem, v):
                if wl.get(sem.name, (None, 0))[1] < v:
                    wl[sem.name] = (sem, v)

            if op.is_dma and op.dma_cycle > 0:
                add_wait(self.dsems[op.eng][op.dma_slot], op.dma_cycle * 16)
            for j in filt_deps[op.idx]:
                dj = self.ops[j]
                if dj.is_dma:
                    add_wait(self.dsems[dj.eng][dj.dma_slot], dj.inc_val)
                else:
                    add_wait(self.sems[dj.eng], dj.inc_val)
            final = []
            wmap = waited[op.eng]
            for semname, (sem, v) in wl.items():
                if wmap.get(semname, 0) >= v:
                    continue
                wmap[semname] = v
                final.append((sem, v))
            op.waits = final
        self._final_dma = {}
        for q, n in dma_count.items():
            R = self.DMA_SLOTS[q]
            self._final_dma[q] = [((n - 1 - s) // R + 1) * 16 if s < min(n, R) else 0
                                  for s in range(R)]
        self._final_eng = dict(eng_count)

    def replay(self):
        nc = self.nc
        per_eng = {e: [] for e in ENGINE_INFO}
        for op in self.ops:
            per_eng[op.eng].append(op)

        with nc.Block() as block:
            for ename, (attr, battr) in ENGINE_INFO.items():
                ops = per_eng[ename]

                def body(eng, ename=ename, ops=ops):
                    for op in ops:
                        for (sem, v) in op.waits:
                            eng.wait_ge(sem, v)
                        inst = op.fn(eng)
                        if op.inc_val is not None:
                            if inst is None:
                                raise RuntimeError(f"op {op.name} returned no inst")
                            if op.is_dma:
                                inst.then_inc(self.dsems[ename][op.dma_slot], 16)
                            else:
                                inst.then_inc(self.sems[ename], 1)
                    if ename in self.dsems:
                        for s, v in enumerate(self._final_dma.get(ename, [])):
                            if v:
                                eng.wait_ge(self.dsems[ename][s], v)

                getattr(block, battr)(body)

    def build(self):
        self.finalize()
        self.replay()


def ceil_div(a, b):
    return (a + b) // b if a % b else a // b



DM = 1024
NB = 2
SEQ = 2048
TOK = NB * SEQ
TC = 512
NCH = TOK // TC
CPB = SEQ // TC
DT = DM // P


class Ctx:
    def __init__(self, nc, prog):
        self.nc = nc
        self.p = prog
        self.ident = nc.alloc_sbuf_tensor("c_ident", [P, P], F32).ap()
        self.ident_bf = nc.alloc_sbuf_tensor("c_identb", [P, P], BF16).ap()
        self.ones_row = nc.alloc_sbuf_tensor("c_ones", [1, P], F32).ap()
        id_d = nc.inline_tensor(np.eye(P, dtype=np.float32), name="c_ident_d")
        prog.dma("sp", self.ident, id_d.ap(), writes=["c_ident"], name="ld_ident")
        prog.dma("pool", self.ident_bf, id_d.ap(), writes=["c_identb"], name="ld_identb")
        on_d = nc.inline_tensor(np.ones((1, P), dtype=np.float32), name="c_ones_d")
        prog.dma("sp", self.ones_row, on_d.ap(), writes=["c_ones"], name="ld_ones")
        self.ones_col_bf = nc.alloc_sbuf_tensor("c_onesb", [P, 1], BF16).ap()
        prog.emit("dve", lambda e: nc.vector.memset(self.ones_col_bf, 1.0),
                  writes=["c_onesb"], name="set_onesb")
        self.ones_col_f = nc.alloc_sbuf_tensor("c_onesf", [P, 1], F32).ap()
        prog.emit("dve", lambda e: nc.vector.memset(self.ones_col_f, 1.0),
                  writes=["c_onesf"], name="set_onesf")
        cm = np.where(np.arange(P)[None, :] <= np.arange(P)[:, None], 0.0, -3e38).astype(np.float32)
        cm_d = nc.inline_tensor(cm, name="c_cmask_d")
        self.cmask = nc.alloc_sbuf_tensor("c_cmask", [P, P], F32).ap()
        prog.dma("sp", self.cmask, cm_d.ap(), writes=["c_cmask"], name="ld_cmask")
        oh32_np = np.zeros((32, P * 32), dtype=np.float32)
        for n in range(32):
            oh32_np[n, n * P:(n + 1) * P] = 1.0
        oh32_d = nc.inline_tensor(oh32_np, name="c_oh32_d")
        self.oh32 = nc.alloc_sbuf_tensor("c_oh32", [32, P * 32], BF16).ap()
        prog.dma("pool", self.oh32, oh32_d.ap(), writes=["c_oh32"], name="ld_oh32")
        oh8_np = np.zeros((8, P * 8), dtype=np.float32)
        for n in range(8):
            oh8_np[n, n * P:(n + 1) * P] = 1.0
        oh8_d = nc.inline_tensor(oh8_np, name="c_oh8_d")
        self.oh8 = nc.alloc_sbuf_tensor("c_oh8", [8, P * 8], BF16).ap()
        prog.dma("pool", self.oh8, oh8_d.ap(), writes=["c_oh8"], name="ld_oh8")
        self.ps = [nc.alloc_psum_tensor(f"psf{i}", [P, 512], F32).ap() for i in range(6)]
        self.psb = [nc.alloc_psum_tensor(f"psb{i}", [P, 1024], BF16).ap() for i in range(2)]
        self._ps_rr = 0
        self._psb_rr = 0
        self._uniq = 0
        self.init_arena()

    def uname(self, base):
        self._uniq += 1
        return f"{base}_{self._uniq}"

    def psum(self):
        i = self._ps_rr % 4
        self._ps_rr += 1
        return self.ps[i], ("psf", i)

    def psum_bf(self):
        i = self._psb_rr % 2
        self._psb_rr += 1
        return self.psb[i], ("psb", i)

    def sbuf(self, name, shape, dtype=F32):
        return self.nc.alloc_sbuf_tensor(name, shape, dtype).ap()

    def init_arena(self):
        nc = self.nc
        avail = nc.sbuf_bytes_remaining - 256
        self._arena_res = nc.alloc_sbuf_tensor("ARENA", [P, avail], mybir.dt.uint8)
        self._arena_base = nc.lookup_mloc(self._arena_res).addr
        self._arena_size = avail
        self._arena_off = 0
        self._arena_gen = 0
        self._scratch = {}

    def arena_reset(self, fence=True):
        if fence:
            self.p.fence()
        self._arena_off = 0
        self._arena_gen += 1
        self._scratch = {}

    def arena_mark(self):
        return (set(self._scratch.keys()), self._arena_off)

    def arena_to(self, mark, fence=True):
        keepset, off = mark
        olds = {k: v for k, v in self._scratch.items() if k in keepset}
        self.arena_reset(fence=fence)
        self._scratch.update(olds)
        self._arena_off = off

    def psum_fixed(self, i):
        return self.ps[i], ("psf", i)

    def constcol(self, value):
        if not hasattr(self, "_cc"):
            self._cc = {}
        if value not in self._cc:
            i = len(self._cc)
            ap = self.nc.alloc_sbuf_tensor(f"c_cc{i}", [P, 1], F32).ap()
            self.p.emit("dve", lambda e, ap=ap, value=value: self.nc.vector.memset(ap, float(value)),
                        writes=[("c_cc", value)], name=f"set_cc{i}")
            self._cc[value] = (ap, ("c_cc", value))
        return self._cc[value]

    def scratch(self, tag, shape, dtype=F32):
        if not hasattr(self, "_scratch_global"):
            self._scratch_global = {}
        key = (tag, tuple(shape), dtype)
        if key not in self._scratch:
            nbytes = int(np.prod(shape[1:])) * mybir.dt.size(dtype)
            off = (self._arena_off + 31) & ~31
            assert off + nbytes <= self._arena_size, (
                f"arena overflow at {tag}: {off}+{nbytes} > {self._arena_size}")
            gkey = (tag, tuple(shape), dtype, off)
            if gkey in self._scratch_global:
                self._scratch[key] = self._scratch_global[gkey]
            else:
                t = self.nc.alloc_sbuf_tensor_at(
                    f"{tag}@g{self._arena_gen}", shape, dtype,
                    offset=self._arena_base + off)
                self._scratch[key] = t.ap()
                self._scratch_global[gkey] = self._scratch[key]
            self._arena_off = off + nbytes
        return self._scratch[key]

    def load_w_bf16(self, name, dram_ap, K, M):
        nt = K // P
        buf = self.scratch(name, [P, nt * M], BF16)
        keys = []
        for kt in range(nt):
            k = (name, kt)
            keys.append(k)
            self.p.dma("pool", buf[:, kt * M:(kt + 1) * M],
                       dram_ap[kt * P:(kt + 1) * P, :],
                       writes=[k], name=f"{name}_ld{kt}")

        def tile(kt, m0, m1):
            return buf[:, kt * M + m0: kt * M + m1]

        return buf, tile, keys

    def load_w_bf16_T(self, name, dram_ap, R, C):
        nc, p = self.nc, self.p
        rt_n, ct_n = ceil_div(R, P), ceil_div(C, P)
        buf = self.scratch(name, [P, ct_n * R], BF16)
        tmp = self.scratch(f"wTtmp{C}", [P, C], BF16)
        keys = []
        for rt in range(rt_n):
            rp = min(P, R - rt * P)
            tk = (f"wTtmp{C}",)
            p.dma("pool", tmp[:rp, :], dram_ap[rt * P:rt * P + rp, :],
                  writes=[tk], name=f"{name}_ldT{rt}")
            for ct in range(ct_n):
                cp = min(P, C - ct * P)
                pb, pbk = self.psum_bf()
                p.emit("pe", lambda e, tmp=tmp, rp=rp, cp=cp, ct=ct, pb=pb:
                       nc.tensor.transpose(pb[0:cp, 0:rp], tmp[:rp, ct * P:ct * P + cp], self.ident_bf[:rp, :rp]),
                       reads=[tk, "c_identb"], writes=[pbk], name=f"{name}_tr{rt}_{ct}")
                k = (name, ct, rt)
                keys.append(k)
                p.emit("act", lambda e, pb=pb, buf=buf, ct=ct, rt=rt, rp=rp, cp=cp:
                       nc.scalar.copy(buf[0:cp, ct * R + rt * P: ct * R + rt * P + rp], pb[0:cp, 0:rp]),
                       reads=[pbk], writes=[k], name=f"{name}_ev{rt}_{ct}")

        def tile(ct, r0, r1):
            return buf[:, ct * R + r0: ct * R + r1]

        return buf, tile, keys

    def vec_to_cols(self, name, dram_ap, D, scale=None, bias=None):
        nc, p = self.nc, self.p
        if D < P:
            nt = 1
            tmp = self.scratch(f"v2ctmp_s{D}", [1, D], F32)
            p.dma("sp", tmp, dram_ap.rearrange("(a b) -> a b", b=D),
                  reads=[("v2ctmp", D)], writes=[(name, "tmp"), ("v2ctmp", D)], name=f"{name}_ld")
            pb, pbk = self.psum()
            p.emit("pe", lambda e, pb=pb, tmp=tmp, D=D: nc.tensor.transpose(pb[0:D, 0:1], tmp, self.ident[:1, :1]),
                   reads=[(name, "tmp"), ("v2ctmp", D), "c_ident"], writes=[pbk], name=f"{name}_tr")
            cols = self.scratch(name, [D, 1], F32)
            sc = 1.0 if scale is None else scale
            bi = 0.0 if bias is None else bias
            p.emit("act", lambda e: nc.scalar.activation(cols, pb[0:D, 0:1], AF.Identity, bias=bi, scale=sc),
                   reads=[pbk], writes=[name], name=f"{name}_ev")
            return cols, name
        nt = D // P
        tmp = self.scratch(f"v2ctmp{nt}", [nt, P], F32)
        p.dma("sp", tmp, dram_ap.rearrange("(a b) -> a b", b=P),
              reads=[("v2ctmp", nt)], writes=[(name, "tmp"), ("v2ctmp", nt)], name=f"{name}_ld")
        pb, pbk = self.psum()
        p.emit("pe", lambda e, pb=pb, tmp=tmp, nt=nt: nc.tensor.transpose(pb[:, 0:nt], tmp, self.ident[:nt, :nt]),
               reads=[(name, "tmp"), ("v2ctmp", nt), "c_ident"], writes=[pbk], name=f"{name}_tr")
        cols = self.scratch(name, [P, nt], F32)
        if scale is None and bias is None:
            p.emit("act", lambda e: nc.scalar.copy(cols, pb[:, 0:nt]),
                   reads=[pbk], writes=[name], name=f"{name}_ev")
        else:
            sc = 1.0 if scale is None else scale
            bi = 0.0 if bias is None else bias
            p.emit("act", lambda e: nc.scalar.activation(cols, pb[:, 0:nt], AF.Identity, bias=bi, scale=sc),
                   reads=[pbk], writes=[name], name=f"{name}_ev")
        return cols, name

    def mm(self, psum_ap, psum_key, lhsT_list, rhs_list, name="mm", extra_reads=()):
        nc, p = self.nc, self.p
        n = len(lhsT_list)
        assert len(rhs_list) == n
        for i in range(n):
            la, lk = lhsT_list[i]
            ra, rk = rhs_list[i]
            p.emit("pe", lambda e, la=la, ra=ra, i=i:
                   nc.tensor.matmul(psum_ap, la, ra, start=(i == 0), stop=(i == n - 1)),
                   reads=[lk, rk] + list(extra_reads), writes=[psum_key],
                   name=f"{name}_{i}")

    def bcast_row(self, row_ap, row_key, n_free, name="bc", nrows=P):
        nc, p = self.nc, self.p
        pb, pbk = self.psum()
        p.emit("pe", lambda e: nc.tensor.matmul(pb[0:nrows, 0:n_free], self.ones_row[:, 0:nrows], row_ap, start=True, stop=True),
               reads=[row_key, "c_ones"], writes=[pbk], name=name)
        return pb[0:nrows, 0:n_free], pbk



def emit_pre_ln(ctx, x_tile, out_tile, T=TC, name="ln", out2_tile=None,
                gamma_beta=None, eps=1e-5):
    nc, p = ctx.nc, ctx.p
    un = name
    sq = ctx.scratch(un + "_sq", [P, DT * T], BF16)
    for j in range(DT):
        xa, xk = x_tile(j)
        p.emit("act", lambda e, xa=xa, j=j: nc.scalar.activation(sq[:, j * T:(j + 1) * T], xa, AF.Square),
               reads=[xk], writes=[(un, "sq", j)], name=f"{un}_sq{j}")
    pb, pbk = ctx.psum()
    pb2, pbk2 = ctx.psum()
    ctx.mm(pb[0:1, 0:T], pbk, [(ctx.ones_col_f, "c_onesf")] * DT,
           [x_tile(j) for j in range(DT)], name=f"{un}_sum")
    ctx.mm(pb2[0:1, 0:T], pbk2, [(ctx.ones_col_bf, "c_onesb")] * DT,
           [(sq[:, j * T:(j + 1) * T], (un, "sq", j)) for j in range(DT)], name=f"{un}_ssq")
    rows = ctx.scratch(un + "_rows", [1, 2 * T], F32)
    m_row = rows[:, 0:T]; v_row = rows[:, T:2 * T]; r_row = v_row
    p.emit("dve", lambda e: nc.vector.tensor_scalar_mul(m_row, pb[0:1, 0:T], 1.0 / DM),
           reads=[pbk], writes=[(un, "m")], name=f"{un}_m")
    p.emit("dve", lambda e: nc.vector.scalar_tensor_tensor(v_row, m_row, -1.0, m_row, ALU.mult, ALU.mult),
           reads=[(un, "m")], writes=[(un, "mm")], name=f"{un}_mm")
    p.emit("dve", lambda e: nc.vector.scalar_tensor_tensor(v_row, pb2[0:1, 0:T], 1.0 / DM, v_row, ALU.mult, ALU.add),
           reads=[pbk2, (un, "mm")], writes=[(un, "v")], name=f"{un}_v")
    epsc, epsk = ctx.constcol(float(eps))
    p.emit("act", lambda e: nc.scalar.activation(r_row, v_row, AF.Sqrt, bias=epsc[0:1, :]),
           reads=[(un, "v"), epsk], writes=[(un, "sd"), (un, "v")], name=f"{un}_sd")
    p.emit("dve", lambda e: nc.vector.reciprocal(r_row, r_row),
           reads=[(un, "sd")], writes=[(un, "r")], name=f"{un}_r")
    mb, mbk = ctx.bcast_row(m_row, (un, "m"), T, name=f"{un}_bm")
    rb, rbk = ctx.bcast_row(r_row, (un, "r"), T, name=f"{un}_br")
    for j in range(DT):
        xa, xk = x_tile(j)
        t1 = ctx.scratch(un + "_t1s", [P, T], F32)
        p.emit("dve", lambda e, xa=xa, t1=t1: nc.vector.tensor_sub(t1, xa, mb),
               reads=[xk, mbk], writes=[(un, "t1s")], name=f"{un}_sub{j}")
        d0, k0 = out_tile(j)
        if gamma_beta is None:
            p.emit("dve", lambda e, t1=t1, d0=d0: nc.vector.tensor_mul(d0, t1, rb),
                   reads=[(un, "t1s"), rbk], writes=[k0], name=f"{un}_mul{j}")
        else:
            g_cols, g_key, b_cols, b_key = gamma_beta
            t2 = ctx.scratch(un + "_t2s", [P, T], F32)
            p.emit("dve", lambda e, t1=t1, t2=t2: nc.vector.tensor_mul(t2, t1, rb),
                   reads=[(un, "t1s"), rbk], writes=[(un, "t2s")], name=f"{un}_mul{j}")
            p.emit("dve", lambda e, t2=t2, d0=d0, j=j:
                   nc.vector.tensor_scalar(d0, t2, g_cols[:, j:j + 1], b_cols[:, j:j + 1], ALU.mult, ALU.add),
                   reads=[(un, "t2s"), g_key, b_key], writes=[k0], name=f"{un}_gb{j}")
        if out2_tile is not None:
            d1, k1 = out2_tile(j)
            p.emit("act", lambda e, d0=d0, d1=d1: nc.scalar.copy(d1, d0),
                   reads=[k0], writes=[k1], name=f"{un}_cp{j}")



def emit_l2_ffn(ctx, xin_d, xout_d, W):
    nc, p = ctx.nc, ctx.p
    _, wr_t, _ = ctx.load_w_bf16_T("l2f_wr", W["W_r"], DM, DM)
    _, wk_t, _ = ctx.load_w_bf16_T("l2f_wk", W["W_k"], DM, DM)
    _, wv_t, _ = ctx.load_w_bf16_T("l2f_wv", W["W_v"], DM, DM)
    mur, mur_k = ctx.vec_to_cols("l2f_mur", W["mu_r"], DM)
    muk, muk_k = ctx.vec_to_cols("l2f_muk", W["mu_k"], DM)
    omr, omr_k = ctx.vec_to_cols("l2f_omr", W["mu_r"], DM, scale=-1.0, bias=1.0)
    omk, omk_k = ctx.vec_to_cols("l2f_omk", W["mu_k"], DM, scale=-1.0, bias=1.0)

    W1 = TC + 1
    bufs = []
    for par in range(2):
        bufs.append(dict(
            xb=ctx.scratch(f"l2f_xb{par}", [P, DT * TC], F32),
            xn=ctx.scratch(f"l2f_xn{par}", [P, DT * W1], BF16),
            xr=ctx.scratch(f"l2f_xr{par}", [P, DT * TC], BF16),
            xk=ctx.scratch(f"l2f_xk{par}", [P, DT * TC], BF16),
            ka=ctx.scratch(f"l2f_ka{par}", [P, DT * TC], BF16),
            r=ctx.scratch(f"l2f_r{par}", [P, DT * TC], BF16),
        ))

    for ch in range(NCH):
        b, bprev = bufs[ch % 2], bufs[(ch + 1) % 2]
        pre = f"l2f{ch}"
        t0 = ch * TC
        first = (ch % CPB == 0)
        for j in range(DT):
            p.dma("sp", b["xb"][:, j * TC:(j + 1) * TC],
                  xin_d[j * P:(j + 1) * P, t0:t0 + TC], writes=[("l2f_xb", ch % 2, j)], name=f"{pre}_ld{j}")
        xt = lambda j: (b["xb"][:, j * TC:(j + 1) * TC], ("l2f_xb", ch % 2, j))
        emit_pre_ln(ctx, xt,
                    lambda j: (b["xn"][:, j * W1 + 1:(j + 1) * W1], ("l2f_xn", ch % 2, j)),
                    name="l2fln")
        for j in range(DT):
            hk = ("l2f_xnh", ch % 2, j)
            if first:
                p.emit("dve", lambda e, j=j, b=b: nc.vector.memset(b["xn"][:, j * W1:j * W1 + 1], 0.0),
                       writes=[hk], name=f"{pre}_hz{j}")
            else:
                p.emit("dve", lambda e, j=j, b=b, bprev=bprev:
                       nc.vector.tensor_copy(b["xn"][:, j * W1:j * W1 + 1], bprev["xn"][:, (j + 1) * W1 - 1:(j + 1) * W1]),
                       reads=[("l2f_xn", (ch + 1) % 2, j)], writes=[hk], name=f"{pre}_hc{j}")
        for j in range(DT):
            v = b["xn"][:, j * W1 + 1:(j + 1) * W1]
            vp = b["xn"][:, j * W1:(j + 1) * W1 - 1]
            vk = [("l2f_xn", ch % 2, j), ("l2f_xnh", ch % 2, j)]
            tg1 = f"l2f_t1_{ch % 2}"
            t1 = ctx.scratch(tg1, [P, TC], F32)
            p.emit("dve", lambda e, vp=vp, t1=t1, j=j: nc.vector.tensor_scalar_mul(t1, vp, omr[:, j:j + 1]),
                   reads=vk + [omr_k], writes=[tg1], name=f"{pre}_mr{j}")
            p.emit("dve", lambda e, v=v, t1=t1, j=j, b=b:
                   nc.vector.scalar_tensor_tensor(b["xr"][:, j * TC:(j + 1) * TC], v, mur[:, j:j + 1], t1, ALU.mult, ALU.add),
                   reads=vk + [mur_k, tg1], writes=[("l2f_xr", ch % 2, j)], name=f"{pre}_xr{j}")
            tg2 = f"l2f_t2_{ch % 2}"
            t2 = ctx.scratch(tg2, [P, TC], F32)
            p.emit("dve", lambda e, vp=vp, t2=t2, j=j: nc.vector.tensor_scalar_mul(t2, vp, omk[:, j:j + 1]),
                   reads=vk + [omk_k], writes=[tg2], name=f"{pre}_mk{j}")
            p.emit("dve", lambda e, v=v, t2=t2, j=j, b=b:
                   nc.vector.scalar_tensor_tensor(b["xk"][:, j * TC:(j + 1) * TC], v, muk[:, j:j + 1], t2, ALU.mult, ALU.add),
                   reads=vk + [muk_k, tg2], writes=[("l2f_xk", ch % 2, j)], name=f"{pre}_xk{j}")
        xr_rhs = [(b["xr"][:, j * TC:(j + 1) * TC], ("l2f_xr", ch % 2, j)) for j in range(DT)]
        xk_rhs = [(b["xk"][:, j * TC:(j + 1) * TC], ("l2f_xk", ch % 2, j)) for j in range(DT)]
        for m in range(DT):
            pbr, pbrk = ctx.psum()
            ctx.mm(pbr, pbrk, [(wr_t(j, m * P, (m + 1) * P), ("l2f_wr", j, m)) for j in range(DT)],
                   xr_rhs, name=f"{pre}_mr{m}")
            p.emit("act", lambda e, pbr=pbr, m=m, b=b:
                   nc.scalar.activation(b["r"][:, m * TC:(m + 1) * TC], pbr, AF.Sigmoid),
                   reads=[pbrk], writes=[("l2f_r", ch % 2, m)], name=f"{pre}_sg{m}")
            pbK, pbKk = ctx.psum()
            ctx.mm(pbK, pbKk, [(wk_t(j, m * P, (m + 1) * P), ("l2f_wk", j, m)) for j in range(DT)],
                   xk_rhs, name=f"{pre}_mk{m}")
            tg3 = f"l2f_t3_{ch % 2}"
            t3 = ctx.scratch(tg3, [P, TC], F32)
            p.emit("dve", lambda e, pbK=pbK, t3=t3: nc.vector.tensor_scalar_max(t3, pbK, 0.0),
                   reads=[pbKk], writes=[tg3], name=f"{pre}_rl{m}")
            p.emit("act", lambda e, t3=t3, m=m, b=b:
                   nc.scalar.activation(b["ka"][:, m * TC:(m + 1) * TC], t3, AF.Square),
                   reads=[tg3], writes=[("l2f_ka", ch % 2, m)], name=f"{pre}_sqr{m}")
        ka_rhs = [(b["ka"][:, j * TC:(j + 1) * TC], ("l2f_ka", ch % 2, j)) for j in range(DT)]
        for m in range(DT):
            pbV, pbVk = ctx.psum()
            ctx.mm(pbV, pbVk, [(wv_t(j, m * P, (m + 1) * P), ("l2f_wv", j, m)) for j in range(DT)],
                   ka_rhs, name=f"{pre}_mv{m}")
            tg4 = f"l2f_t4_{ch % 2}"
            t4 = ctx.scratch(tg4, [P, TC], F32)
            p.emit("dve", lambda e, pbV=pbV, t4=t4, m=m, b=b:
                   nc.vector.tensor_mul(t4, b["r"][:, m * TC:(m + 1) * TC], pbV),
                   reads=[pbVk, ("l2f_r", ch % 2, m)], writes=[tg4], name=f"{pre}_g{m}")
            p.emit("dve", lambda e, t4=t4, m=m, b=b:
                   nc.vector.tensor_add(t4, t4, b["xb"][:, m * TC:(m + 1) * TC]),
                   reads=[tg4, ("l2f_xb", ch % 2, m)], writes=[tg4], name=f"{pre}_res{m}")
            p.dma("sp", xout_d[m * P:(m + 1) * P, t0:t0 + TC], t4,
                  reads=[tg4], writes=[("xo", ch, m)], name=f"{pre}_st{m}")



def emit_l0_attn(ctx, xin_d, xout_d, W):
    nc, p = ctx.nc, ctx.p
    DI = 2048
    NT = DI // P
    NS = 16
    KCONV = 4

    keep = []

    def keep_alloc(tag, shape, dtype):
        ap = ctx.scratch(tag, shape, dtype)
        keep.append((tag, tuple(shape), dtype))
        return ap

    bdt, bdt_k = ctx.vec_to_cols("l0a_bdt", W["b_dt"], DI)
    dcol, dcol_k = ctx.vec_to_cols("l0a_D", W["D"], DI)
    cb, cb_k = ctx.vec_to_cols("l0a_cb", W["conv_b"], DI)
    cwraw = ctx.scratch("l0a_cwr", [NT, P * KCONV], F32)
    p.dma("sp", cwraw, W["conv_w"].rearrange("(a b) k -> a (b k)", b=P), writes=[("l0a_cwr",)], name="l0a_cw_ld")
    cw = ctx.scratch("l0a_cw", [P, NT * KCONV], F32)
    for k in range(KCONV):
        pb, pbk = ctx.psum()
        ap_in = cwraw.rearrange("a (b k) -> a b k", k=KCONV)[:, :, k]
        p.emit("pe", lambda e, ap_in=ap_in, pb=pb: nc.tensor.transpose(pb[:, 0:NT], ap_in, ctx.ident[:NT, :NT]),
               reads=[("l0a_cwr",), "c_ident"], writes=[pbk], name=f"l0a_cwtr{k}")
        p.emit("act", lambda e, pb=pb, k=k: nc.scalar.copy(
            cw.rearrange("p (j q) -> p j q", q=KCONV)[:, :, k], pb[:, 0:NT]),
            reads=[pbk], writes=[("l0a_cw", k)], name=f"l0a_cwev{k}")
    cw_col = lambda j, k: cw[:, (j * KCONV + k):(j * KCONV + k) + 1]
    acol = ctx.scratch("l0a_acol", [P, NT * NS], F32)
    for j in range(NT):
        p.dma("sp", acol[:, j * NS:(j + 1) * NS], W["A_log"][j * P:(j + 1) * P, :],
              writes=[("l0a_acol_raw", j)], name=f"l0a_al{j}")
    p.emit("act", lambda e: nc.scalar.activation(acol, acol, AF.Exp),
           reads=[("l0a_acol_raw", j) for j in range(NT)], writes=[("l0a_acol_e",)], name="l0a_aexp")
    p.emit("dve", lambda e: nc.vector.tensor_scalar_mul(acol, acol, -1.0),
           reads=[("l0a_acol_e",)], writes=[("l0a_acol",)], name="l0a_aneg")
    state = ctx.scratch("l0a_state", [P, NT * NS], F32)
    bc_all = ctx.scratch("l0a_bc", [32, TOK], BF16)
    oh32 = ctx.oh32
    WA = 16 * 2080
    warena = ctx.scratch("l0a_warena", [P, WA], BF16)

    _mark = ctx.arena_mark()

    def phase_reset():
        ctx.arena_to(_mark)

    def load_warena(tag, dram_ap, K, M):
        nt = K // P
        keys = []
        for kt in range(nt):
            k = ("l0a_wa", tag, kt)
            keys.append(k)
            p.dma("pool", warena[:, kt * M:(kt + 1) * M], dram_ap[kt * P:(kt + 1) * P, :],
                  reads=[("l0a_wa_guard",)], writes=[k], name=f"l0a_w{tag}_{kt}")

        def tile(kt, m0, m1):
            return warena[:, kt * M + m0: kt * M + m1]
        return tile, keys

    def free_warena(keys):
        p.emit("dve", lambda e: nc.vector.engine_nop(),
               reads=[], writes=list(keys) + [("l0a_wa_guard",)], name="l0a_wfree")

    xc_d = nc.dram_tensor("l0a_xc", [DI, TOK], BF16, kind="Internal").ap()
    zs_d = nc.dram_tensor("l0a_zs", [DI, TOK], BF16, kind="Internal").ap()
    dr_d = nc.dram_tensor("l0a_dr", [DI, TOK], BF16, kind="Internal").ap()
    yg_d = nc.dram_tensor("l0a_yg", [DI, TOK], BF16, kind="Internal").ap()

    win_t, win_keys = load_warena("win", W["W_in"], DM, 2 * DI)
    HW = TC + 3
    xbuf = [ctx.scratch(f"l0a_xb{par}", [P, DT * TC], F32) for par in range(2)]
    xnb = [ctx.scratch(f"l0a_xn{par}", [P, DT * TC], BF16) for par in range(2)]
    xib = [ctx.scratch(f"l0a_xi{par}", [P, NT * HW], BF16) for par in range(2)]
    for ch in range(NCH):
        par = ch % 2
        pre = f"l0aA{ch}"
        t0 = ch * TC
        first = (ch % CPB == 0)
        for j in range(DT):
            p.dma("sp", xbuf[par][:, j * TC:(j + 1) * TC], xin_d[j * P:(j + 1) * P, t0:t0 + TC],
                  writes=[("l0a_xb", par, j)], name=f"{pre}_ld{j}")
        emit_pre_ln(ctx, lambda j: (xbuf[par][:, j * TC:(j + 1) * TC], ("l0a_xb", par, j)),
                    lambda j: (xnb[par][:, j * TC:(j + 1) * TC], ("l0a_xn", par, j)),
                    name="l0aln")
        xn_rhs = [(xnb[par][:, j * TC:(j + 1) * TC], ("l0a_xn", par, j)) for j in range(DT)]
        for m in range(NT):
            hk = ("l0a_xih", par, m)
            if first:
                p.emit("dve", lambda e, m=m, par=par: nc.vector.memset(xib[par][:, m * HW:m * HW + 3], 0.0),
                       writes=[hk], name=f"{pre}_hz{m}")
            else:
                q = (ch + 1) % 2
                p.emit("dve", lambda e, m=m, par=par, q=q:
                       nc.vector.tensor_copy(xib[par][:, m * HW:m * HW + 3], xib[q][:, (m + 1) * HW - 3:(m + 1) * HW]),
                       reads=[("l0a_xi", q, m)], writes=[hk], name=f"{pre}_hc{m}")
        for m in range(2 * NT):
            pb, pbk = ctx.psum()
            ctx.mm(pb, pbk, [(win_t(j, m * P, (m + 1) * P), ("l0a_wa", "win", j)) for j in range(DT)],
                   xn_rhs, name=f"{pre}_mm{m}")
            if m < NT:
                p.emit("act", lambda e, pb=pb, m=m, par=par:
                       nc.scalar.copy(xib[par][:, m * HW + 3:(m + 1) * HW], pb),
                       reads=[pbk], writes=[("l0a_xi", par, m)], name=f"{pre}_ev{m}")
            else:
                zt = ctx.scratch(f"l0a_z{par}_{m % 2}", [P, TC], BF16)
                p.emit("act", lambda e, pb=pb, zt=zt:
                       nc.scalar.activation(zt, pb, AF.Silu),
                       reads=[pbk], writes=[("l0a_z", par, m % 2)], name=f"{pre}_sz{m}")
                p.dma("sp", zs_d[(m - NT) * P:(m - NT + 1) * P, t0:t0 + TC], zt,
                      reads=[("l0a_z", par, m % 2)], writes=[("zs_d", ch, m - NT)], name=f"{pre}_stz{m}")
        for m in range(NT):
            mq = m % 2
            acc = ctx.scratch(f"l0a_cacc{par}_{mq}", [P, TC], F32)
            xi = xib[par]
            base = m * HW
            p.emit("dve", lambda e, xi=xi, base=base, acc=acc, m=m:
                   nc.vector.tensor_scalar(acc, xi[:, base + 3:base + 3 + TC], cw_col(m, 3), cb[:, m:m + 1], ALU.mult, ALU.add),
                   reads=[("l0a_xi", par, m), ("l0a_xih", par, m), ("l0a_cw", 3), cb_k],
                   writes=[("l0a_cacc", par, mq)], name=f"{pre}_cv3_{m}")
            for k in range(KCONV - 1):
                p.emit("dve", lambda e, xi=xi, base=base, acc=acc, m=m, k=k:
                       nc.vector.scalar_tensor_tensor(acc, xi[:, base + k:base + k + TC], cw_col(m, k), acc, ALU.mult, ALU.add),
                       reads=[("l0a_xi", par, m), ("l0a_xih", par, m), ("l0a_cw", k), ("l0a_cacc", par, mq)],
                       writes=[("l0a_cacc", par, mq)], name=f"{pre}_cv{k}_{m}")
            xct = ctx.scratch(f"l0a_xct{par}_{mq}", [P, TC], BF16)
            p.emit("act", lambda e, acc=acc, xct=xct: nc.scalar.activation(xct, acc, AF.Silu),
                   reads=[("l0a_cacc", par, mq)], writes=[("l0a_xct", par, mq)], name=f"{pre}_cs{m}")
            p.dma("sp", xc_d[m * P:(m + 1) * P, t0:t0 + TC], xct,
                  reads=[("l0a_xct", par, mq)], writes=[("xc_d", ch, m)], name=f"{pre}_stc{m}")
    free_warena(win_keys)
    phase_reset()

    wxp_t, wxp_keys = load_warena("wxp", W["W_xproj"], DI, DI + 2 * NS)
    xcb = [ctx.scratch(f"l0a_xcB{par}", [P, NT * TC], BF16) for par in range(2)]
    for ch in range(NCH):
        par = ch % 2
        pre = f"l0aB{ch}"
        t0 = ch * TC
        for j in range(NT):
            p.dma("sp", xcb[par][:, j * TC:(j + 1) * TC], xc_d[j * P:(j + 1) * P, t0:t0 + TC],
                  reads=[("xc_d", ch, j)], writes=[("l0a_xcB", par, j)], name=f"{pre}_ld{j}")
        xc_rhs = [(xcb[par][:, j * TC:(j + 1) * TC], ("l0a_xcB", par, j)) for j in range(NT)]
        for m in range(NT):
            pb, pbk = ctx.psum()
            ctx.mm(pb, pbk, [(wxp_t(j, m * P, (m + 1) * P), ("l0a_wa", "wxp", j)) for j in range(NT)],
                   xc_rhs, name=f"{pre}_mm{m}")
            drt = ctx.scratch(f"l0a_drt{par}_{m % 2}", [P, TC], BF16)
            p.emit("act", lambda e, pb=pb, drt=drt: nc.scalar.copy(drt, pb),
                   reads=[pbk], writes=[("l0a_drt", par, m % 2)], name=f"{pre}_ev{m}")
            p.dma("sp", dr_d[m * P:(m + 1) * P, t0:t0 + TC], drt,
                  reads=[("l0a_drt", par, m % 2)], writes=[("dr_d", ch, m)], name=f"{pre}_st{m}")
        pb, pbk = ctx.psum()
        ctx.mm(pb[0:32, 0:TC], pbk, [(wxp_t(j, NT * P, NT * P + 32), ("l0a_wa", "wxp", j)) for j in range(NT)],
               xc_rhs, name=f"{pre}_mmbc")
        p.emit("act", lambda e, pb=pb, t0=t0: nc.scalar.copy(bc_all[:, t0:t0 + TC], pb[0:32, 0:TC]),
               reads=[pbk], writes=[("l0a_bc", ch)], name=f"{pre}_evbc")
    free_warena(wxp_keys)
    phase_reset()

    wdt_t, wdt_keys = load_warena("wdt", W["W_dt"], DI, DI)
    delta = ctx.scratch("l0a_delta", [P, NT * TC], F32)
    xcj = [ctx.scratch(f"l0a_xcj{q}", [P, TC], BF16) for q in range(2)]
    zsj = [ctx.scratch(f"l0a_zsj{q}", [P, TC], BF16) for q in range(2)]
    drB = ctx.scratch("l0a_drB", [P, NT * TC], BF16)
    bbc_all = ctx.scratch("l0a_bbc", [P, NS * TC], BF16)
    cbc_all = ctx.scratch("l0a_cbc", [P, NS * TC], BF16)
    for ch in range(NCH):
        pre = f"l0aC{ch}"
        t0 = ch * TC
        first = (ch % CPB == 0)
        for j in range(NT):
            p.dma("sp", drB[:, j * TC:(j + 1) * TC], dr_d[j * P:(j + 1) * P, t0:t0 + TC],
                  reads=[("dr_d", ch, j)], writes=[("l0a_drB", j)], name=f"{pre}_lddr{j}")
        dr_rhs = [(drB[:, j * TC:(j + 1) * TC], ("l0a_drB", j)) for j in range(NT)]
        if first:
            p.emit("dve", lambda e: nc.vector.memset(state, 0.0),
                   reads=[], writes=[("l0a_state", n_, j_) for n_ in range(NS) for j_ in range(NT)],
                   name=f"{pre}_sz")
        for m in range(NT):
            q = m % 2
            pb, pbk = ctx.psum()
            ctx.mm(pb, pbk, [(wdt_t(j, m * P, (m + 1) * P), ("l0a_wa", "wdt", j)) for j in range(NT)],
                   dr_rhs, name=f"{pre}_mmdt{m}")
            raw = ctx.scratch(f"l0a_raw{q}", [P, TC], F32)
            na = ctx.scratch(f"l0a_na{q}", [P, TC], F32)
            dl = delta[:, m * TC:(m + 1) * TC]
            p.emit("dve", lambda e, pb=pb, raw=raw, m=m:
                   nc.vector.tensor_scalar_add(raw, pb, bdt[:, m:m + 1]),
                   reads=[pbk, bdt_k], writes=[("l0a_raw", q)], name=f"{pre}_bd{m}")
            p.emit("dve", lambda e, raw=raw, na=na:
                   nc.vector.scalar_tensor_tensor(na, raw, -1.0, raw, ALU.mult, ALU.min),
                   reads=[("l0a_raw", q)], writes=[("l0a_na", q)], name=f"{pre}_na{m}")
            p.emit("act", lambda e, na=na: nc.scalar.activation(na, na, AF.Exp),
                   reads=[("l0a_na", q)], writes=[("l0a_na", q)], name=f"{pre}_ex{m}")
            p.emit("act", lambda e, na=na: nc.scalar.activation(na, na, AF.Ln, bias=1.0),
                   reads=[("l0a_na", q)], writes=[("l0a_na", q)], name=f"{pre}_ln{m}")
            p.emit("dve", lambda e, raw=raw: nc.vector.tensor_scalar_max(raw, raw, 0.0),
                   reads=[("l0a_raw", q)], writes=[("l0a_raw", q)], name=f"{pre}_rl{m}")
            p.emit("dve", lambda e, raw=raw, na=na, dl=dl:
                   nc.vector.tensor_add(dl, raw, na),
                   reads=[("l0a_raw", q), ("l0a_na", q)], writes=[("l0a_delta", m)], name=f"{pre}_dl{m}")
        for n in range(NS):
            pb2, pbk2 = ctx.psum()
            p.emit("pe", lambda e, pb2=pb2, n=n, t0=t0:
                   nc.tensor.matmul(pb2, oh32[:, n * P:(n + 1) * P], bc_all[:, t0:t0 + TC], start=True, stop=True),
                   reads=["c_oh32", ("l0a_bc", ch)], writes=[pbk2], name=f"{pre}_bbc{n}")
            p.emit("act", lambda e, pb2=pb2, n=n: nc.scalar.copy(bbc_all[:, n * TC:(n + 1) * TC], pb2),
                   reads=[pbk2], writes=[("l0a_bbc", n)], name=f"{pre}_bev{n}")
            pb3, pbk3 = ctx.psum()
            p.emit("pe", lambda e, pb3=pb3, n=n, t0=t0:
                   nc.tensor.matmul(pb3, oh32[:, (16 + n) * P:(16 + n + 1) * P], bc_all[:, t0:t0 + TC], start=True, stop=True),
                   reads=["c_oh32", ("l0a_bc", ch)], writes=[pbk3], name=f"{pre}_cbc{n}")
            p.emit("act", lambda e, pb3=pb3, n=n: nc.scalar.copy(cbc_all[:, n * TC:(n + 1) * TC], pb3),
                   reads=[pbk3], writes=[("l0a_cbc", n)], name=f"{pre}_cev{n}")
        for j in range(NT):
            jq = j % 2
            ypb, ypbk = ctx.psum_fixed(4 + jq)
            xcjb, zsjb = xcj[jq], zsj[jq]
            p.dma("sp", xcjb, xc_d[j * P:(j + 1) * P, t0:t0 + TC],
                  reads=[("xc_d", ch, j)], writes=[("l0a_xcj", jq)], name=f"{pre}_ldxc{j}")
            p.dma("sp", zsjb, zs_d[j * P:(j + 1) * P, t0:t0 + TC],
                  reads=[("zs_d", ch, j)], writes=[("l0a_zsj", jq)], name=f"{pre}_ldzs{j}")
            def emit_au(n):
                q3 = n % 3
                a = ctx.scratch(f"l0a_a{q3}", [P, TC], F32)
                u = ctx.scratch(f"l0a_u{q3}", [P, TC], BF16)
                p.emit("act", lambda e, a=a, j=j, n=n:
                       nc.scalar.activation(a, delta[:, j * TC:(j + 1) * TC], AF.Exp, scale=acol[:, (j * NS + n):(j * NS + n) + 1]),
                       reads=[("l0a_delta", j), ("l0a_acol",)], writes=[("l0a_a", q3)], name=f"{pre}_a{j}_{n}")
                p.emit("pool", lambda e, u=u, j=j, n=n, xcjb=xcjb:
                       nc.gpsimd.tensor_mul(u, xcjb, bbc_all[:, n * TC:(n + 1) * TC]),
                       reads=[("l0a_xcj", jq), ("l0a_bbc", n)], writes=[("l0a_u", q3)], name=f"{pre}_u{j}_{n}")

            def emit_body(n):
                q = n % 2
                q3 = n % 3
                a = ctx.scratch(f"l0a_a{q3}", [P, TC], F32)
                u = ctx.scratch(f"l0a_u{q3}", [P, TC], BF16)
                s = ctx.scratch(f"l0a_s{q}", [P, TC], F32)
                sq = ctx.scratch(f"l0a_sq2{q}", [P, TC], BF16)
                st_col = state[:, (j * NS + n):(j * NS + n) + 1]
                p.emit("dve", lambda e, s=s, a=a, u=u, st_col=st_col:
                       nc.vector.tensor_tensor_scan(s, a, u, st_col, ALU.mult, ALU.add),
                       reads=[("l0a_a", q3), ("l0a_u", q3), ("l0a_state", n, j)], writes=[("l0a_s", q)], name=f"{pre}_sc{j}_{n}")
                p.emit("act", lambda e, s=s, st_col=st_col:
                       nc.scalar.copy(st_col, s[:, TC - 1:TC]),
                       reads=[("l0a_s", q)], writes=[("l0a_state", n, j)], name=f"{pre}_su{j}_{n}")
                seng = "pool" if q == 0 else "dve"
                sop = nc.gpsimd if q == 0 else nc.vector
                p.emit(seng, lambda e, sq=sq, s=s, n=n, sop=sop:
                       sop.tensor_mul(sq, s, cbc_all[:, n * TC:(n + 1) * TC]),
                       reads=[("l0a_s", q), ("l0a_cbc", n)], writes=[("l0a_sq2", q)], name=f"{pre}_sp{j}_{n}")
                p.emit("pe", lambda e, ypb=ypb, sq=sq, n=n:
                       nc.tensor.matmul(ypb, ctx.ident_bf, sq, start=(n == 0), stop=(n == NS - 1)),
                       reads=[("l0a_sq2", q), "c_identb"], writes=[ypbk], name=f"{pre}_ym{j}_{n}")

            for t in range(NS + 2):
                if t < NS:
                    emit_au(t)
                if t >= 2:
                    emit_body(t - 2)
            yg = ctx.scratch("l0a_ygs", [P, TC], F32)
            ygb = ctx.scratch(f"l0a_ygb{jq}", [P, TC], BF16)
            p.emit("dve", lambda e, yg=yg, ypb=ypb, j=j, xcjb=xcjb:
                   nc.vector.scalar_tensor_tensor(yg, xcjb, dcol[:, j:j + 1], ypb, ALU.mult, ALU.add),
                   reads=[ypbk, ("l0a_xcj", jq), dcol_k], writes=[("l0a_ygs",)], name=f"{pre}_yd{j}")
            p.emit("dve", lambda e, yg=yg, ygb=ygb, j=j, zsjb=zsjb:
                   nc.vector.tensor_mul(ygb, yg, zsjb),
                   reads=[("l0a_ygs",), ("l0a_zsj", jq)], writes=[("l0a_ygb", jq)], name=f"{pre}_yz{j}")
            p.dma("sp", yg_d[j * P:(j + 1) * P, t0:t0 + TC], ygb,
                  reads=[("l0a_ygb", jq)], writes=[("yg_d", ch, j)], name=f"{pre}_sty{j}")
    free_warena(wdt_keys)
    phase_reset()

    wout_t, wout_keys = load_warena("wout", W["W_out"], DI, DM)
    ygD = [ctx.scratch(f"l0a_ygD{par}", [P, NT * TC], BF16) for par in range(2)]
    xD = [ctx.scratch(f"l0a_xD{par}", [P, DT * TC], F32) for par in range(2)]
    for ch in range(NCH):
        par = ch % 2
        pre = f"l0aD{ch}"
        t0 = ch * TC
        for j in range(NT):
            p.dma("sp", ygD[par][:, j * TC:(j + 1) * TC], yg_d[j * P:(j + 1) * P, t0:t0 + TC],
                  reads=[("yg_d", ch, j)], writes=[("l0a_ygD", par, j)], name=f"{pre}_ldy{j}")
        for j in range(DT):
            p.dma("sp", xD[par][:, j * TC:(j + 1) * TC], xin_d[j * P:(j + 1) * P, t0:t0 + TC],
                  writes=[("l0a_xD", par, j)], name=f"{pre}_ldx{j}")
        yg_rhs = [(ygD[par][:, j * TC:(j + 1) * TC], ("l0a_ygD", par, j)) for j in range(NT)]
        for m in range(DT):
            pb, pbk = ctx.psum()
            ctx.mm(pb, pbk, [(wout_t(j, m * P, (m + 1) * P), ("l0a_wa", "wout", j)) for j in range(NT)],
                   yg_rhs, name=f"{pre}_mm{m}")
            xo = ctx.scratch(f"l0a_xo{par}_{m % 2}", [P, TC], F32)
            p.emit("dve", lambda e, pb=pb, xo=xo, m=m, par=par:
                   nc.vector.tensor_add(xo, pb, xD[par][:, m * TC:(m + 1) * TC]),
                   reads=[pbk, ("l0a_xD", par, m)], writes=[("l0a_xo", par, m % 2)], name=f"{pre}_res{m}")
            p.dma("sp", xout_d[m * P:(m + 1) * P, t0:t0 + TC], xo,
                  reads=[("l0a_xo", par, m % 2)], writes=[("xo", ch, m)], name=f"{pre}_st{m}")



def emit_l2_attn(ctx, xin_d, xout_d, W):
    nc, p = ctx.nc, ctx.p
    NS = 16
    KCONV = 4
    _, win_t, _ = ctx.load_w_bf16("l2a_win", W["W_in"], DM, 2048)
    _, wx_t, _ = ctx.load_w_bf16("l2a_wx", W["W_x"], DM, 32)
    _, wout_t, _ = ctx.load_w_bf16("l2a_wout", W["W_out"], DM, DM)
    bin_c, bin_k = ctx.vec_to_cols("l2a_bin", W["b_in"], 2048)
    bx_c, bx_k = ctx.vec_to_cols("l2a_bx", W["b_x"], 32)
    bout_c, bout_k = ctx.vec_to_cols("l2a_bout", W["b_out"], DM)
    dpr_c, dpr_k = ctx.vec_to_cols("l2a_dp", W["D_param"], DM)
    cb_c, cb_k = ctx.vec_to_cols("l2a_cb", W["conv_b"], DM)
    cwraw = ctx.scratch("l2a_cwr", [DT, P * KCONV], F32)
    p.dma("sp", cwraw, W["conv_w"].rearrange("(a b) k -> a (b k)", b=P), writes=[("l2a_cwr",)], name="l2a_cw_ld")
    cw = ctx.scratch("l2a_cw", [P, DT * KCONV], F32)
    for k in range(KCONV):
        pb, pbk = ctx.psum()
        ap_in = cwraw.rearrange("a (b k) -> a b k", k=KCONV)[:, :, k]
        p.emit("pe", lambda e, ap_in=ap_in, pb=pb: nc.tensor.transpose(pb[:, 0:DT], ap_in, ctx.ident[:DT, :DT]),
               reads=[("l2a_cwr",), "c_ident"], writes=[pbk], name=f"l2a_cwtr{k}")
        p.emit("act", lambda e, pb=pb, k=k: nc.scalar.copy(
            cw.rearrange("p (j q) -> p j q", q=KCONV)[:, :, k], pb[:, 0:DT]),
            reads=[pbk], writes=[("l2a_cw", k)], name=f"l2a_cwev{k}")
    cw_col = lambda j, k: cw[:, (j * KCONV + k):(j * KCONV + k) + 1]
    adc = ctx.scratch("l2a_adc", [P, DT * NS], F32)
    for j in range(DT):
        p.dma("sp", adc[:, j * NS:(j + 1) * NS], W["A_log"][j * P:(j + 1) * P, :],
              writes=[("l2a_adc_r", j)], name=f"l2a_al{j}")
    p.emit("act", lambda e: nc.scalar.activation(adc, adc, AF.Exp),
           reads=[("l2a_adc_r", j) for j in range(DT)], writes=[("l2a_adc_1",)], name="l2a_ae1")
    p.emit("act", lambda e: nc.scalar.activation(adc, adc, AF.Exp, scale=-1.0),
           reads=[("l2a_adc_1",)], writes=[("l2a_adc",)], name="l2a_ae2")
    state = ctx.scratch("l2a_state", [P, DT * NS], F32)

    HW = TC + 3
    xbuf = [ctx.scratch(f"l2a_xb{q}", [P, DT * TC], F32) for q in range(2)]
    xnb = [ctx.scratch(f"l2a_xn{q}", [P, DT * TC], BF16) for q in range(2)]
    xib = [ctx.scratch("l2a_xi", [P, DT * HW], BF16)] * 2
    gs = ctx.scratch("l2a_gs", [P, DT * TC], BF16)
    xc = ctx.scratch("l2a_xc", [P, DT * TC], BF16)
    ygb = ctx.scratch("l2a_ygb", [P, DT * TC], BF16)
    bcs = ctx.scratch("l2a_bcs", [32, TC], BF16)
    bbc = ctx.scratch("l2a_bbc", [P, NS * TC], BF16)
    cbc = ctx.scratch("l2a_cbc", [P, NS * TC], BF16)

    for ch in range(NCH):
        par = ch % 2
        pre = f"l2a{ch}"
        t0 = ch * TC
        first = (ch % CPB == 0)
        for j in range(DT):
            p.dma("sp", xbuf[par][:, j * TC:(j + 1) * TC], xin_d[j * P:(j + 1) * P, t0:t0 + TC],
                  writes=[("l2a_xb", par, j)], name=f"{pre}_ld{j}")
        emit_pre_ln(ctx, lambda j: (xbuf[par][:, j * TC:(j + 1) * TC], ("l2a_xb", par, j)),
                    lambda j: (xnb[par][:, j * TC:(j + 1) * TC], ("l2a_xn", par, j)),
                    name="l2aln")
        xn_rhs = [(xnb[par][:, j * TC:(j + 1) * TC], ("l2a_xn", par, j)) for j in range(DT)]
        for m in range(DT):
            hk = ("l2a_xih", m)
            if first:
                p.emit("dve", lambda e, m=m, par=par: nc.vector.memset(xib[par][:, m * HW:m * HW + 3], 0.0),
                       reads=[("l2a_xi", m)], writes=[hk], name=f"{pre}_hz{m}")
            else:
                p.emit("dve", lambda e, m=m, par=par:
                       nc.vector.tensor_copy(xib[par][:, m * HW:m * HW + 3], xib[par][:, (m + 1) * HW - 3:(m + 1) * HW]),
                       reads=[("l2a_xi", m)], writes=[hk], name=f"{pre}_hc{m}")
        for m in range(2 * DT):
            pb, pbk = ctx.psum()
            ctx.mm(pb, pbk, [(win_t(j, m * P, (m + 1) * P), ("l2a_win", j)) for j in range(DT)],
                   xn_rhs, name=f"{pre}_mm{m}")
            if m < DT:
                p.emit("act", lambda e, pb=pb, m=m, par=par:
                       nc.scalar.activation(xib[par][:, m * HW + 3:(m + 1) * HW], pb, AF.Identity, bias=bin_c[:, m:m + 1]),
                       reads=[pbk, bin_k, ("l2a_xih", m)], writes=[("l2a_xi", m)], name=f"{pre}_ev{m}")
            else:
                p.emit("act", lambda e, pb=pb, m=m:
                       nc.scalar.activation(gs[:, (m - DT) * TC:(m - DT + 1) * TC], pb, AF.Silu, bias=bin_c[:, m:m + 1]),
                       reads=[pbk, bin_k], writes=[("l2a_gs", m - DT)], name=f"{pre}_sg{m}")
        for m in range(DT):
            mq = 0
            acc = ctx.scratch("l2a_cacc", [P, TC], F32)
            xi = xib[par]
            base = m * HW
            p.emit("dve", lambda e, xi=xi, base=base, acc=acc, m=m:
                   nc.vector.tensor_scalar(acc, xi[:, base + 3:base + 3 + TC], cw_col(m, 3), cb_c[:, m:m + 1], ALU.mult, ALU.add),
                   reads=[("l2a_xi", m), ("l2a_xih", m), ("l2a_cw", 3), cb_k],
                   writes=[("l2a_cacc", mq)], name=f"{pre}_cv3_{m}")
            for k in range(KCONV - 1):
                p.emit("dve", lambda e, xi=xi, base=base, acc=acc, m=m, k=k:
                       nc.vector.scalar_tensor_tensor(acc, xi[:, base + k:base + k + TC], cw_col(m, k), acc, ALU.mult, ALU.add),
                       reads=[("l2a_xi", m), ("l2a_xih", m), ("l2a_cw", k), ("l2a_cacc", mq)],
                       writes=[("l2a_cacc", mq)], name=f"{pre}_cv{k}_{m}")
            p.emit("act", lambda e, acc=acc, m=m: nc.scalar.activation(xc[:, m * TC:(m + 1) * TC], acc, AF.Silu),
                   reads=[("l2a_cacc", mq)], writes=[("l2a_xc", m)], name=f"{pre}_cs{m}")
        xc_rhs = [(xc[:, j * TC:(j + 1) * TC], ("l2a_xc", j)) for j in range(DT)]
        pb, pbk = ctx.psum()
        ctx.mm(pb[0:32, 0:TC], pbk, [(wx_t(j, 0, 32), ("l2a_wx", j)) for j in range(DT)],
               xc_rhs, name=f"{pre}_mmbc")
        p.emit("dve", lambda e, pb=pb: nc.vector.tensor_scalar_add(bcs, pb[0:32, 0:TC], bx_c),
               reads=[pbk, bx_k], writes=[("l2a_bcs",)], name=f"{pre}_bcb")
        for n in range(NS):
            pb2, pbk2 = ctx.psum()
            p.emit("pe", lambda e, pb2=pb2, n=n:
                   nc.tensor.matmul(pb2, ctx.oh32[:, n * P:(n + 1) * P], bcs, start=True, stop=True),
                   reads=["c_oh32", ("l2a_bcs",)], writes=[pbk2], name=f"{pre}_bbc{n}")
            p.emit("act", lambda e, pb2=pb2, n=n: nc.scalar.copy(bbc[:, n * TC:(n + 1) * TC], pb2),
                   reads=[pbk2], writes=[("l2a_bbc", n)], name=f"{pre}_bev{n}")
            pb3, pbk3 = ctx.psum()
            p.emit("pe", lambda e, pb3=pb3, n=n:
                   nc.tensor.matmul(pb3, ctx.oh32[:, (16 + n) * P:(16 + n + 1) * P], bcs, start=True, stop=True),
                   reads=["c_oh32", ("l2a_bcs",)], writes=[pbk3], name=f"{pre}_cbc{n}")
            p.emit("act", lambda e, pb3=pb3, n=n: nc.scalar.copy(cbc[:, n * TC:(n + 1) * TC], pb3),
                   reads=[pbk3], writes=[("l2a_cbc", n)], name=f"{pre}_cev{n}")
        if first:
            p.emit("dve", lambda e: nc.vector.memset(state, 0.0),
                   writes=[("l2a_state", n_, j_) for n_ in range(NS) for j_ in range(DT)],
                   name=f"{pre}_sz")
        for j in range(DT):
            jq = j % 2
            ypb, ypbk = ctx.psum_fixed(4 + jq)
            def emit_u2(n):
                q3 = n % 3
                u = ctx.scratch(f"l2a_u{q3}", [P, TC], BF16)
                p.emit("pool", lambda e, u=u, j=j, n=n:
                       nc.gpsimd.tensor_mul(u, xc[:, j * TC:(j + 1) * TC], bbc[:, n * TC:(n + 1) * TC]),
                       reads=[("l2a_xc", j), ("l2a_bbc", n)], writes=[("l2a_u", q3)], name=f"{pre}_u{j}_{n}")

            def emit_body2(n):
                q = n % 2
                q3 = n % 3
                u = ctx.scratch(f"l2a_u{q3}", [P, TC], BF16)
                s = ctx.scratch(f"l2a_s{q}", [P, TC], F32)
                sq = ctx.scratch(f"l2a_sq2{q}", [P, TC], BF16)
                st_col = state[:, (j * NS + n):(j * NS + n) + 1]
                ad_b = adc[:, (j * NS + n):(j * NS + n) + 1].to_broadcast([P, TC])
                p.emit("dve", lambda e, s=s, u=u, st_col=st_col, ad_b=ad_b:
                       nc.vector.tensor_tensor_scan(s, ad_b, u, st_col, ALU.mult, ALU.add),
                       reads=[("l2a_adc",), ("l2a_u", q3), ("l2a_state", n, j)], writes=[("l2a_s", q)], name=f"{pre}_sc{j}_{n}")
                p.emit("act", lambda e, s=s, st_col=st_col:
                       nc.scalar.copy(st_col, s[:, TC - 1:TC]),
                       reads=[("l2a_s", q)], writes=[("l2a_state", n, j)], name=f"{pre}_su{j}_{n}")
                seng = "pool" if q == 0 else "dve"
                sop = nc.gpsimd if q == 0 else nc.vector
                p.emit(seng, lambda e, sq=sq, s=s, n=n, sop=sop:
                       sop.tensor_mul(sq, s, cbc[:, n * TC:(n + 1) * TC]),
                       reads=[("l2a_s", q), ("l2a_cbc", n)], writes=[("l2a_sq2", q)], name=f"{pre}_sp{j}_{n}")
                p.emit("pe", lambda e, ypb=ypb, sq=sq, n=n:
                       nc.tensor.matmul(ypb, ctx.ident_bf, sq, start=(n == 0), stop=(n == NS - 1)),
                       reads=[("l2a_sq2", q), "c_identb"], writes=[ypbk], name=f"{pre}_ym{j}_{n}")

            for t in range(NS + 2):
                if t < NS:
                    emit_u2(t)
                if t >= 2:
                    emit_body2(t - 2)
            yg = ctx.scratch("l2a_ygs", [P, TC], F32)
            p.emit("dve", lambda e, yg=yg, ypb=ypb, j=j:
                   nc.vector.scalar_tensor_tensor(yg, xc[:, j * TC:(j + 1) * TC], dpr_c[:, j:j + 1], ypb, ALU.mult, ALU.add),
                   reads=[ypbk, ("l2a_xc", j), dpr_k], writes=[("l2a_ygs",)], name=f"{pre}_yd{j}")
            p.emit("dve", lambda e, yg=yg, j=j:
                   nc.vector.tensor_mul(ygb[:, j * TC:(j + 1) * TC], yg, gs[:, j * TC:(j + 1) * TC]),
                   reads=[("l2a_ygs",), ("l2a_gs", j)], writes=[("l2a_ygb", j)], name=f"{pre}_yz{j}")
        yg_rhs = [(ygb[:, j * TC:(j + 1) * TC], ("l2a_ygb", j)) for j in range(DT)]
        for m in range(DT):
            pb, pbk = ctx.psum()
            ctx.mm(pb, pbk, [(wout_t(j, m * P, (m + 1) * P), ("l2a_wout", j)) for j in range(DT)],
                   yg_rhs, name=f"{pre}_mo{m}")
            xo = ctx.scratch("l2a_xo", [P, TC], F32)
            p.emit("dve", lambda e, pb=pb, xo=xo, m=m:
                   nc.vector.tensor_scalar_add(xo, pb, bout_c[:, m:m + 1]),
                   reads=[pbk, bout_k], writes=[("l2a_xo",)], name=f"{pre}_bo{m}")
            p.emit("dve", lambda e, xo=xo, m=m, par=par:
                   nc.vector.tensor_add(xo, xo, xbuf[par][:, m * TC:(m + 1) * TC]),
                   reads=[("l2a_xo",), ("l2a_xb", par, m)], writes=[("l2a_xo",)], name=f"{pre}_res{m}")
            p.dma("sp", xout_d[m * P:(m + 1) * P, t0:t0 + TC], xo,
                  reads=[("l2a_xo",)], writes=[("xo", ch, m)], name=f"{pre}_st{m}")



MOE_RUN_EXPERTS = 8
MOE_ACCUM_DVE = False
MOE_DEBUG_T2 = False


def emit_moe_ffn(ctx, xin_d, xout_d, W, kind):
    nc, p = ctx.nc, ctx.p
    NE = 8
    FF = 4096
    FT = FF // P
    pre0 = "l1f" if kind == "l1" else "l3f"
    act_fn = AF.Relu if kind == "l1" else AF.Gelu
    wg_name = "Wg" if kind == "l1" else "Wr"
    bg_name = "bg" if kind == "l1" else "br"

    xn_d = nc.dram_tensor(pre0 + "_xn", [DM, TOK], BF16, kind="Internal").ap()
    y_d = nc.dram_tensor(pre0 + "_y", [DM, TOK], F32, kind="Internal").ap()

    wgf = ctx.scratch(pre0 + "_wgf", [P, DT * NE], F32)
    for kt in range(DT):
        p.dma("sp", wgf[:, kt * NE:(kt + 1) * NE], W[wg_name][kt * P:(kt + 1) * P, :],
              writes=[(pre0 + "_wgf", kt)], name=f"{pre0}_ldwg{kt}")
    bgr = ctx.scratch(pre0 + "_bgr", [1, NE], F32)
    p.dma("sp", bgr, W[bg_name].rearrange("(a b) -> a b", b=NE), writes=[(pre0, "bgr")], name=pre0 + "_ldbg")
    bg_ps, bg_psk = ctx.bcast_row(bgr, (pre0, "bgr"), NE, name=pre0 + "_bgbc")
    bgbc = ctx.scratch(pre0 + "_bgbc", [P, NE], F32)
    p.emit("act", lambda e: nc.scalar.copy(bgbc, bg_ps), reads=[bg_psk], writes=[(pre0, "bgbc")], name=pre0 + "_bgev")
    gateT = ctx.scratch(pre0 + "_gT", [NE, TOK], BF16)
    bcols = []
    for e in range(NE):
        b1c, b1k = ctx.vec_to_cols(pre0 + f"_b1_{e}", W["b1"][e, :], FF)
        b2c, b2k = ctx.vec_to_cols(pre0 + f"_b2_{e}", W["b2"][e, :], DM)
        bcols.append((b1c, b1k, b2c, b2k))
    _moemark = ctx.arena_mark()

    xbuf = [ctx.scratch(f"{pre0}_xb{q}", [P, DT * TC], F32) for q in range(2)]
    xnb = [ctx.scratch(f"{pre0}_xn{q}", [P, DT * TC], BF16) for q in range(2)]
    xnf = [ctx.scratch(f"{pre0}_xf{q}", [P, DT * TC], F32) for q in range(2)]
    for ch in range(NCH):
        par = ch % 2
        pre = f"{pre0}P{ch}"
        t0 = ch * TC
        for j in range(DT):
            p.dma("sp", xbuf[par][:, j * TC:(j + 1) * TC], xin_d[j * P:(j + 1) * P, t0:t0 + TC],
                  writes=[(pre0 + "_xb", par, j)], name=f"{pre}_ld{j}")
        emit_pre_ln(ctx, lambda j: (xbuf[par][:, j * TC:(j + 1) * TC], (pre0 + "_xb", par, j)),
                    lambda j: (xnf[par][:, j * TC:(j + 1) * TC], (pre0 + "_xf", par, j)),
                    name=pre0 + "ln",
                    out2_tile=lambda j: (xnb[par][:, j * TC:(j + 1) * TC], (pre0 + "_xn", par, j)))
        for j in range(DT):
            p.dma("sp", xn_d[j * P:(j + 1) * P, t0:t0 + TC], xnb[par][:, j * TC:(j + 1) * TC],
                  reads=[(pre0 + "_xn", par, j)], writes=[("xn_d", ch, j)], name=f"{pre}_stn{j}")
        for tt in range(TC // P):
            tq = tt % 2
            pb, pbk = ctx.psum()
            for j in range(DT):
                p.emit("pe", lambda e, pb=pb, j=j, tt=tt, par=par:
                       nc.tensor.matmul(pb[:, 0:NE], xnf[par][:, j * TC + tt * P: j * TC + (tt + 1) * P],
                                        wgf[:, j * NE:(j + 1) * NE], start=(j == 0), stop=(j == DT - 1)),
                       reads=[(pre0 + "_xf", par, j), (pre0 + "_wgf", j)], writes=[pbk], name=f"{pre}_rmm{tt}_{j}")
            lg = ctx.scratch(pre0 + f"_lg{tq}", [P, NE], F32)
            p.emit("dve", lambda e, pb=pb, lg=lg: nc.vector.tensor_add(lg, pb[:, 0:NE], bgbc),
                   reads=[pbk, (pre0, "bgbc")], writes=[(pre0, "lg", tq)], name=f"{pre}_lg{tt}")
            nmx = ctx.scratch(pre0 + f"_nmx{tq}", [P, 1], F32)
            p.emit("dve", lambda e, lg=lg, nmx=nmx: nc.vector.reduce_max(nmx, lg, axis=AX.X, negate=True),
                   reads=[(pre0, "lg", tq)], writes=[(pre0, "nmx", tq)], name=f"{pre}_mx{tt}")
            ex = ctx.scratch(pre0 + f"_ex{tq}", [P, NE], F32)
            sm = ctx.scratch(pre0 + f"_sm{tq}", [P, 1], F32)
            p.emit("act", lambda e, lg=lg, ex=ex, sm=sm, nmx=nmx:
                   nc.scalar.activation(ex, lg, AF.Exp, bias=nmx, accum_out=sm),
                   reads=[(pre0, "lg", tq), (pre0, "nmx", tq)], writes=[(pre0, "ex", tq), (pre0, "sm", tq)], name=f"{pre}_ex{tt}")
            p.emit("dve", lambda e, sm=sm: nc.vector.reciprocal(sm, sm),
                   reads=[(pre0, "sm", tq)], writes=[(pre0, "sm", tq)], name=f"{pre}_rc{tt}")
            g = ctx.scratch(pre0 + f"_g{tq}", [P, NE], F32)
            p.emit("dve", lambda e, ex=ex, sm=sm, g=g: nc.vector.tensor_scalar_mul(g, ex, sm),
                   reads=[(pre0, "ex", tq), (pre0, "sm", tq)], writes=[(pre0, "g", tq)], name=f"{pre}_g{tt}")
            w_tm = ctx.scratch(pre0 + f"_wtm{tq}", [P, NE], BF16)
            m1 = ctx.scratch(pre0 + f"_m1{tq}", [P, 1], F32)
            if kind == "l3":
                p.emit("dve", lambda e, g=g, m1=m1: nc.vector.reduce_max(m1, g, axis=AX.X),
                       reads=[(pre0, "g", tq)], writes=[(pre0, "m1", tq)], name=f"{pre}_m1{tt}")
                p.emit("dve", lambda e, g=g, m1=m1, w_tm=w_tm:
                       nc.vector.tensor_scalar(w_tm, g, m1, m1, ALU.is_equal, ALU.mult),
                       reads=[(pre0, "g", tq), (pre0, "m1", tq)], writes=[(pre0, "wtm", tq)], name=f"{pre}_w{tt}")
            else:
                m2 = ctx.scratch(pre0 + f"_m2{tq}", [P, 1], F32)
                v2 = ctx.scratch(pre0 + f"_v2{tq}", [P, NE], F32)
                p1 = ctx.scratch(pre0 + f"_p1{tq}", [P, 1], F32)
                p2 = ctx.scratch(pre0 + f"_p2{tq}", [P, 1], F32)
                w1 = ctx.scratch(pre0 + f"_w1{tq}", [P, NE], F32)
                p.emit("dve", lambda e, g=g, m1=m1: nc.vector.reduce_max(m1, g, axis=AX.X),
                       reads=[(pre0, "g", tq)], writes=[(pre0, "m1", tq)], name=f"{pre}_m1{tt}")
                p.emit("dve", lambda e, g=g, m1=m1, v2=v2:
                       nc.vector.scalar_tensor_tensor(v2, g, m1, g, ALU.is_lt, ALU.mult),
                       reads=[(pre0, "g", tq), (pre0, "m1", tq)], writes=[(pre0, "v2", tq)], name=f"{pre}_v2{tt}")
                p.emit("dve", lambda e, v2=v2, m2=m2: nc.vector.reduce_max(m2, v2, axis=AX.X),
                       reads=[(pre0, "v2", tq)], writes=[(pre0, "m2", tq)], name=f"{pre}_m2{tt}")
                p.emit("dve", lambda e, m1=m1, m2=m2, p1=p1: nc.vector.tensor_sub(p1, m2, m1),
                       reads=[(pre0, "m1", tq), (pre0, "m2", tq)], writes=[(pre0, "p1", tq)], name=f"{pre}_d{tt}")
                p.emit("act", lambda e, p1=p1: nc.scalar.activation(p1, p1, AF.Sigmoid, scale=-1.0),
                       reads=[(pre0, "p1", tq)], writes=[(pre0, "p1", tq)], name=f"{pre}_sg{tt}")
                p.emit("dve", lambda e, p1=p1, p2=p2: nc.vector.tensor_scalar(p2, p1, -1.0, 1.0, ALU.mult, ALU.add),
                       reads=[(pre0, "p1", tq)], writes=[(pre0, "p2", tq)], name=f"{pre}_p2{tt}")
                p.emit("dve", lambda e, g=g, m1=m1, w1=w1, p1=p1:
                       nc.vector.tensor_scalar(w1, g, m1, p1, ALU.is_equal, ALU.mult),
                       reads=[(pre0, "g", tq), (pre0, "m1", tq), (pre0, "p1", tq)], writes=[(pre0, "w1", tq)], name=f"{pre}_w1{tt}")
                p.emit("dve", lambda e, g=g, m2=m2, w_tm=w_tm, p2=p2:
                       nc.vector.tensor_scalar(w_tm, g, m2, p2, ALU.is_equal, ALU.mult),
                       reads=[(pre0, "g", tq), (pre0, "m2", tq), (pre0, "p2", tq)], writes=[(pre0, "wtm", tq)], name=f"{pre}_w2{tt}")
                p.emit("dve", lambda e, w1=w1, w_tm=w_tm: nc.vector.tensor_add(w_tm, w_tm, w1),
                       reads=[(pre0, "w1", tq), (pre0, "wtm", tq)], writes=[(pre0, "wtm", tq)], name=f"{pre}_w{tt}")
            pbt, pbtk = ctx.psum_bf()
            p.emit("pe", lambda e, pbt=pbt, w_tm=w_tm:
                   nc.tensor.transpose(pbt[0:NE, 0:P], w_tm, ctx.ident_bf),
                   reads=[(pre0, "wtm", tq), "c_identb"], writes=[pbtk], name=f"{pre}_wt{tt}")
            p.emit("act", lambda e, pbt=pbt, t0=t0, tt=tt:
                   nc.scalar.copy(gateT[:, t0 + tt * P: t0 + (tt + 1) * P], pbt[0:NE, 0:P]),
                   reads=[pbtk], writes=[(pre0 + "_gT", ch, tt)], name=f"{pre}_wte{tt}")

    ctx.arena_to(_moemark)
    w1b = ctx.scratch(pre0 + "_w1b", [P, DT * FF], BF16)
    w2b = ctx.scratch(pre0 + "_w2b", [P, FT * DM], BF16)
    hb = ctx.scratch(pre0 + "_hb", [P, FT * TC], BF16)
    xnc = ctx.scratch(pre0 + "_xnc", [P, DT * TC], BF16)
    wbc = ctx.scratch(pre0 + "_wbc", [P, TC], BF16)
    if MOE_DEBUG_T2:
        dbg_d = nc.dram_tensor(pre0 + "_dbg", [MOE_RUN_EXPERTS * NCH * DT * P, TC], F32, kind="Internal").ap()
        dbg2_d = nc.dram_tensor(pre0 + "_dbg2", [MOE_RUN_EXPERTS * NCH * P, TC], F32, kind="Internal").ap()
        dbg3_d = nc.dram_tensor(pre0 + "_dbg3", [MOE_RUN_EXPERTS * NCH * P, TC], F32, kind="Internal").ap()
        dbg4_d = nc.dram_tensor(pre0 + "_dbg4", [MOE_RUN_EXPERTS * NCH * P, TC], F32, kind="Internal").ap()
        ctx._dbg5 = nc.dram_tensor(pre0 + "_dbg5", [MOE_RUN_EXPERTS * NCH * P, TC], F32, kind="Internal").ap()
        ctx._dbg6 = nc.dram_tensor(pre0 + "_dbg6", [MOE_RUN_EXPERTS * NCH * P, TC], BF16, kind="Internal").ap()
    for e in range(MOE_RUN_EXPERTS):
        epre = f"{pre0}E{e}"
        for kt in range(DT):
            p.dma("pool", w1b[:, kt * FF:(kt + 1) * FF], W["W1"][e * DM + kt * P:e * DM + (kt + 1) * P, :],
                  writes=[(pre0 + "_w1b", kt)], name=f"{epre}_lw1{kt}")
        for kt in range(FT):
            p.dma("pool", w2b[:, kt * DM:(kt + 1) * DM], W["W2"][e * FF + kt * P:e * FF + (kt + 1) * P, :],
                  writes=[(pre0 + "_w2b", kt)], name=f"{epre}_lw2{kt}")
        b1c, b1k, b2c, b2k = bcols[e]
        for ch in range(NCH):
            t0 = ch * TC
            cpre = f"{epre}c{ch}"
            for j in range(DT):
                p.dma("sp", xnc[:, j * TC:(j + 1) * TC], xn_d[j * P:(j + 1) * P, t0:t0 + TC],
                      reads=[("xn_d", ch, j)], writes=[(pre0 + "_xnc", j)], name=f"{cpre}_ldx{j}")
            xn_rhs = [(xnc[:, j * TC:(j + 1) * TC], (pre0 + "_xnc", j)) for j in range(DT)]
            if MOE_DEBUG_T2:
                row = (e * NCH + ch) * P
                dbg5_d = ctx._dbg5
                p.dma("pool", dbg5_d[row:row + P, :], w1b[:, 0:TC],
                      reads=[(pre0 + "_w1b", 0)], writes=[("dbg_w1", e, ch)], name=f"{cpre}_dbgw1")
                p.dma("sp", ctx._dbg6[row:row + P, :], xnc[:, 0:TC],
                      reads=[(pre0 + "_xnc", 0)], writes=[("dbg_xn", e, ch)], name=f"{cpre}_dbgxn")
            for fg in range(FT):
                pb, pbk = ctx.psum()
                ctx.mm(pb, pbk, [(w1b[:, j * FF + fg * P:j * FF + (fg + 1) * P], (pre0 + "_w1b", j)) for j in range(DT)],
                       xn_rhs, name=f"{cpre}_h{fg}")
                p.emit("act", lambda e_, pb=pb, fg=fg, b1c=b1c:
                       nc.scalar.activation(hb[:, fg * TC:(fg + 1) * TC], pb, act_fn, bias=b1c[:, fg:fg + 1]),
                       reads=[pbk, b1k], writes=[(pre0 + "_hb", fg)], name=f"{cpre}_ha{fg}")
            pbg, pbgk = ctx.psum()
            p.emit("pe", lambda e_, pbg=pbg, t0=t0, ee=e:
                   nc.tensor.matmul(pbg, ctx.oh8[:, ee * P:(ee + 1) * P], gateT[:, t0:t0 + TC], start=True, stop=True),
                   reads=["c_oh8"] + [(pre0 + "_gT", c_, tt) for c_ in range(NCH) for tt in range(TC // P)],
                   writes=[pbgk], name=f"{cpre}_gbc")
            p.emit("act", lambda e_, pbg=pbg: nc.scalar.copy(wbc, pbg),
                   reads=[pbgk], writes=[(pre0 + "_wbc",)], name=f"{cpre}_gbe")
            if MOE_DEBUG_T2:
                row = (e * NCH + ch) * P
                p.dma("pool", dbg2_d[row:row + P, :], wbc,
                      reads=[(pre0 + "_wbc",)], writes=[("dbg_wbc", e, ch)], name=f"{cpre}_dbgw")
                p.dma("pool", dbg4_d[row:row + P, :], hb[:, 0:TC],
                      reads=[(pre0 + "_hb", 0)], writes=[("dbg_h", e, ch)], name=f"{cpre}_dbgh")
            h_rhs = [(hb[:, j * TC:(j + 1) * TC], (pre0 + "_hb", j)) for j in range(FT)]
            for m in range(DT):
                mq = m % 2
                pb, pbk = ctx.psum()
                ctx.mm(pb, pbk, [(w2b[:, j * DM + m * P:j * DM + (m + 1) * P], (pre0 + "_w2b", j)) for j in range(FT)],
                       h_rhs, name=f"{cpre}_o{m}")
                t1 = ctx.scratch(pre0 + f"_t1{mq}", [P, TC], F32)
                t2 = ctx.scratch(pre0 + f"_t2{mq}", [P, TC], F32)
                p.emit("dve", lambda e_, pb=pb, t1=t1, m=m, b2c=b2c:
                       nc.vector.tensor_scalar_add(t1, pb, b2c[:, m:m + 1]),
                       reads=[pbk, b2k], writes=[(pre0, "t1", mq)], name=f"{cpre}_b2{m}")
                if MOE_DEBUG_T2 and m == 0:
                    row = (e * NCH + ch) * P
                    p.dma("sp", dbg3_d[row:row + P, :], t1,
                          reads=[(pre0, "t1", mq)], writes=[("dbg_t1", e, ch)], name=f"{cpre}_dbgt1")
                p.emit("dve", lambda e_, t1=t1, t2=t2:
                       nc.vector.tensor_mul(t2, t1, wbc),
                       reads=[(pre0, "t1", mq), (pre0 + "_wbc",)], writes=[(pre0, "t2", mq)], name=f"{cpre}_gg{m}")
                if MOE_DEBUG_T2:
                    row = ((e * NCH + ch) * DT + m) * P
                    p.dma("sp", dbg_d[row:row + P, :], t2,
                          reads=[(pre0, "t2", mq)], writes=[("dbg_t2", e, ch, m)], name=f"{cpre}_dbg{m}")
                if e == 0:
                    p.dma("pool", y_d[m * P:(m + 1) * P, t0:t0 + TC], t2,
                          reads=[(pre0, "t2", mq)], writes=[("y_d", ch, m)], name=f"{cpre}_sy{m}")
                elif MOE_ACCUM_DVE:
                    yt = ctx.scratch(pre0 + "_yt", [P, TC], F32)
                    p.dma("sp", yt, y_d[m * P:(m + 1) * P, t0:t0 + TC],
                          reads=[("y_d", ch, m)], writes=[(pre0, "yt")], name=f"{cpre}_ly{m}")
                    p.emit("dve", lambda e_, t2=t2, yt=yt: nc.vector.tensor_add(t2, t2, yt),
                           reads=[(pre0, "t2", mq), (pre0, "yt")], writes=[(pre0, "t2", mq)], name=f"{cpre}_ad{m}")
                    p.dma("pool", y_d[m * P:(m + 1) * P, t0:t0 + TC], t2,
                          reads=[(pre0, "t2", mq), ("y_d", ch, m)], writes=[("y_d", ch, m)], name=f"{cpre}_wy{m}")
                else:
                    p.dma("pool", y_d[m * P:(m + 1) * P, t0:t0 + TC], t2,
                          reads=[(pre0, "t2", mq), ("y_d", ch, m)], writes=[("y_d", ch, m)],
                          name=f"{cpre}_ay{m}", accum_op=ALU.add)

    ctx.arena_to(_moemark)
    xbuf = [ctx.scratch(f"{pre0}_xb{q}", [P, DT * TC], F32) for q in range(2)]
    for ch in range(NCH):
        par = ch % 2
        t0 = ch * TC
        fpre = f"{pre0}F{ch}"
        for j in range(DT):
            p.dma("sp", xbuf[par][:, j * TC:(j + 1) * TC], xin_d[j * P:(j + 1) * P, t0:t0 + TC],
                  writes=[(pre0 + "_xb", par, j)], name=f"{fpre}_ldx{j}")
            yb = ctx.scratch(pre0 + f"_yb{par}", [P, DT * TC], F32)
            p.dma("sp", yb[:, j * TC:(j + 1) * TC], y_d[j * P:(j + 1) * P, t0:t0 + TC],
                  reads=[("y_d", ch, j)], writes=[(pre0 + "_yb", par, j)], name=f"{fpre}_ldy{j}")
            xo = ctx.scratch(pre0 + f"_xo{par}_{j % 2}", [P, TC], F32)
            p.emit("dve", lambda e, j=j, par=par, yb=yb, xo=xo:
                   nc.vector.tensor_add(xo, xbuf[par][:, j * TC:(j + 1) * TC], yb[:, j * TC:(j + 1) * TC]),
                   reads=[(pre0 + "_xb", par, j), (pre0 + "_yb", par, j)], writes=[(pre0 + "_xo", par, j % 2)],
                   name=f"{fpre}_add{j}")
            p.dma("sp", xout_d[j * P:(j + 1) * P, t0:t0 + TC], xo,
                  reads=[(pre0 + "_xo", par, j % 2)], writes=[("xo", ch, j)], name=f"{fpre}_st{j}")


def emit_l1_ffn(ctx, xin_d, xout_d, W):
    emit_moe_ffn(ctx, xin_d, xout_d, W, "l1")


def emit_l3_ffn(ctx, xin_d, xout_d, W):
    emit_moe_ffn(ctx, xin_d, xout_d, W, "l3")



def emit_l1_attn(ctx, xin_d, xout_d, W):
    nc, p = ctx.nc, ctx.p
    NH = 16
    HD = 64
    _, wq_t, _ = ctx.load_w_bf16_T("l1a_wq", W["Wq"], DM, DM)
    _, wk_t, _ = ctx.load_w_bf16_T("l1a_wk", W["Wk"], DM, DM)
    _, wv_t, _ = ctx.load_w_bf16_T("l1a_wv", W["Wv"], DM, DM)
    _, wf_t, _ = ctx.load_w_bf16_T("l1a_wf", W["Wfc"], DM, DM)

    def vop(eng):
        return nc.vector if eng == "dve" else nc.gpsimd

    xbuf = ctx.scratch("l1a_xb", [P, DT * TC], F32)
    xnb = ctx.scratch("l1a_xn", [P, DT * TC], BF16)
    qf = ctx.scratch("l1a_qf", [P, DT * TC], BF16)
    kf = ctx.scratch("l1a_kf", [P, DT * TC], BF16)
    vf = ctx.scratch("l1a_vf", [P, DT * TC], BF16)
    of = ctx.scratch("l1a_of", [P, DT * TC], BF16)
    qtms = [ctx.scratch(f"l1a_qtm{q}", [P, DM], BF16) for q in range(2)]
    ktms = [ctx.scratch(f"l1a_ktm{q}", [P, DM], BF16) for q in range(2)]
    vtms = [ctx.scratch(f"l1a_vtm{q}", [P, DM], BF16) for q in range(2)]
    otms = [ctx.scratch(f"l1a_otm{q}", [P, DM], BF16) for q in range(2)]
    mbufs = [ctx.scratch(f"l1a_mb{q}", [P, DM], BF16) for q in range(2)]
    mbuf2s = [ctx.scratch(f"l1a_mb2{q}", [P, DM], BF16) for q in range(2)]
    sbufs = [ctx.scratch(f"l1a_sb{q}", [P, NH * NH], F32) for q in range(2)]
    abufs = [ctx.scratch(f"l1a_ab{q}", [P, NH * NH], BF16) for q in range(2)]
    nmxs = [ctx.scratch(f"l1a_nmx{q}", [P, NH], F32) for q in range(2)]
    ssums = [ctx.scratch(f"l1a_ssum{q}", [P, NH], F32) for q in range(2)]
    tmpbs = [ctx.scratch(f"l1a_tmp{q}", [P, DM], BF16) for q in range(2)]
    tmpb2s = [ctx.scratch(f"l1a_tmp2{q}", [P, DM], BF16) for q in range(2)]

    s3 = lambda ap: ap.rearrange("p (h d) -> p h d", d=HD)
    a3 = lambda ap: ap.rearrange("p (h t) -> p h t", t=NH)

    for ch in range(NCH):
        pre = f"l1a{ch}"
        t0 = ch * TC
        for j in range(DT):
            p.dma("sp", xbuf[:, j * TC:(j + 1) * TC], xin_d[j * P:(j + 1) * P, t0:t0 + TC],
                  writes=[("l1a_xb", j)], name=f"{pre}_ld{j}")
        emit_pre_ln(ctx, lambda j: (xbuf[:, j * TC:(j + 1) * TC], ("l1a_xb", j)),
                    lambda j: (xnb[:, j * TC:(j + 1) * TC], ("l1a_xn", j)),
                    name="l1aln")
        xn_rhs = [(xnb[:, j * TC:(j + 1) * TC], ("l1a_xn", j)) for j in range(DT)]
        for which, wt, wname, dst, dk in [(0, wq_t, "l1a_wq", qf, "l1a_qf"),
                                          (1, wk_t, "l1a_wk", kf, "l1a_kf"),
                                          (2, wv_t, "l1a_wv", vf, "l1a_vf")]:
            for m in range(DT):
                pb, pbk = ctx.psum()
                ctx.mm(pb, pbk, [(wt(j, m * P, (m + 1) * P), (wname, j, m)) for j in range(DT)],
                       xn_rhs, name=f"{pre}_p{which}_{m}")
                p.emit("act", lambda e, pb=pb, dst=dst, m=m: nc.scalar.copy(dst[:, m * TC:(m + 1) * TC], pb),
                       reads=[pbk], writes=[(dk, m)], name=f"{pre}_e{which}_{m}")
        for tt in range(TC // P):
            ts0 = tt * P
            tq = tt % 2
            tpre = f"{pre}t{tt}"
            qtm, ktm, vtm, otm = qtms[tq], ktms[tq], vtms[tq], otms[tq]
            sbuf_, abuf = sbufs[tq], abufs[tq]
            nmx, ssum = nmxs[tq], ssums[tq]
            for (srcb, srck, dstb, dstk) in [(qf, "l1a_qf", qtm, ("l1a_qtm", tq)),
                                             (kf, "l1a_kf", ktm, ("l1a_ktm", tq)),
                                             (vf, "l1a_vf", vtm, ("l1a_vtm", tq))]:
                pbt, pbtk = ctx.psum_bf()
                for j in range(DT):
                    p.emit("pe", lambda e, srcb=srcb, j=j, pbt=pbt, ts0=ts0:
                           nc.tensor.transpose(pbt[:, j * P:(j + 1) * P], srcb[:, j * TC + ts0:j * TC + ts0 + P], ctx.ident_bf),
                           reads=[(srck, j), "c_identb", dstk], writes=[pbtk], name=f"{tpre}_tr{j}")
                p.emit("act", lambda e, pbt=pbt, dstb=dstb: nc.scalar.copy(dstb, pbt),
                       reads=[pbtk], writes=[dstk], name=f"{tpre}_te")
            ecnt = {"dve": 0, "pool": 0}
            for th in range(NH):
                eng = "dve" if th % 2 == 0 else "pool"
                uq = ecnt[eng] % 2
                ecnt[eng] += 1
                mb = mbufs[uq] if eng == "dve" else mbuf2s[uq]
                mk = ("l1a_mb", uq) if eng == "dve" else ("l1a_mb2", uq)
                krep = ktm[:, th * HD:(th + 1) * HD].unsqueeze(1).to_broadcast([P, NH, HD])
                p.emit(eng, lambda e, krep=krep, eng=eng, mb=mb, qtm=qtm:
                       vop(eng).tensor_tensor(s3(mb), s3(qtm), krep, ALU.mult),
                       reads=[("l1a_qtm", tq), ("l1a_ktm", tq)], writes=[mk], name=f"{tpre}_qk{th}")
                sout = a3(sbuf_)[:, :, th]
                p.emit("dve", lambda e, sout=sout, mb=mb:
                       nc.vector.reduce_sum(sout, s3(mb), axis=AX.X),
                       reads=[mk], writes=[("l1a_sb", tq, th)], name=f"{tpre}_rs{th}")
            p.emit("dve", lambda e, nmx=nmx, sbuf_=sbuf_: nc.vector.reduce_max(nmx, a3(sbuf_), axis=AX.X, negate=True),
                   reads=[("l1a_sb", tq, th_) for th_ in range(NH)], writes=[("l1a_nmx", tq)], name=f"{tpre}_mx")
            p.emit("dve", lambda e, nmx=nmx, sbuf_=sbuf_: nc.vector.tensor_add(a3(sbuf_), a3(sbuf_), nmx.unsqueeze(2).to_broadcast([P, NH, NH])),
                   reads=[("l1a_nmx", tq)] + [("l1a_sb", tq, th_) for th_ in range(NH)], writes=[("l1a_sb2", tq)], name=f"{tpre}_sub")
            p.emit("act", lambda e, sbuf_=sbuf_: nc.scalar.activation(sbuf_, sbuf_, AF.Exp, scale=0.125),
                   reads=[("l1a_sb2", tq)], writes=[("l1a_sb3", tq)], name=f"{tpre}_exp")
            p.emit("dve", lambda e, ssum=ssum, sbuf_=sbuf_: nc.vector.reduce_sum(ssum, a3(sbuf_), axis=AX.X),
                   reads=[("l1a_sb3", tq)], writes=[("l1a_ssum", tq)], name=f"{tpre}_sm")
            p.emit("dve", lambda e, ssum=ssum: nc.vector.reciprocal(ssum, ssum),
                   reads=[("l1a_ssum", tq)], writes=[("l1a_ssum", tq)], name=f"{tpre}_rc")
            p.emit("dve", lambda e, abuf=abuf, sbuf_=sbuf_, ssum=ssum: nc.vector.tensor_mul(a3(abuf), a3(sbuf_), ssum.unsqueeze(2).to_broadcast([P, NH, NH])),
                   reads=[("l1a_sb3", tq), ("l1a_ssum", tq)], writes=[("l1a_ab", tq)], name=f"{tpre}_nm")
            po0, po0k = ctx.psum()
            po1, po1k = ctx.psum()
            ecnt = {"dve": 0, "pool": 0}
            for th in range(NH):
                eng = "dve" if th % 2 == 0 else "pool"
                uq = ecnt[eng] % 2
                ecnt[eng] += 1
                tb = tmpbs[uq] if eng == "dve" else tmpb2s[uq]
                tk = ("l1a_tmp", uq) if eng == "dve" else ("l1a_tmp2", uq)
                arep = a3(abuf)[:, :, th].unsqueeze(2).to_broadcast([P, NH, HD])
                vrep = vtm[:, th * HD:(th + 1) * HD].unsqueeze(1).to_broadcast([P, NH, HD])
                p.emit(eng, lambda e, arep=arep, vrep=vrep, eng=eng, tb=tb:
                       vop(eng).tensor_tensor(s3(tb), arep, vrep, ALU.mult),
                       reads=[("l1a_ab", tq), ("l1a_vtm", tq)], writes=[tk], name=f"{tpre}_av{th}")
                p.emit("pe", lambda e, th=th, po0=po0, tb=tb: nc.tensor.matmul(po0, ctx.ident_bf, tb[:, 0:TC], start=(th == 0), stop=(th == NH - 1)),
                       reads=[tk, "c_identb"], writes=[po0k], name=f"{tpre}_oa{th}")
                p.emit("pe", lambda e, th=th, po1=po1, tb=tb: nc.tensor.matmul(po1, ctx.ident_bf, tb[:, TC:DM], start=(th == 0), stop=(th == NH - 1)),
                       reads=[tk, "c_identb"], writes=[po1k], name=f"{tpre}_ob{th}")
            p.emit("act", lambda e, po0=po0, otm=otm: nc.scalar.copy(otm[:, 0:TC], po0),
                   reads=[po0k], writes=[("l1a_otm", tq, 0)], name=f"{tpre}_oe0")
            p.emit("act", lambda e, po1=po1, otm=otm: nc.scalar.copy(otm[:, TC:DM], po1),
                   reads=[po1k], writes=[("l1a_otm", tq, 1)], name=f"{tpre}_oe1")
            pbt, pbtk = ctx.psum_bf()
            for j in range(DT):
                p.emit("pe", lambda e, j=j, pbt=pbt, otm=otm:
                       nc.tensor.transpose(pbt[:, j * P:(j + 1) * P], otm[:, j * P:(j + 1) * P], ctx.ident_bf),
                       reads=[("l1a_otm", tq, 0), ("l1a_otm", tq, 1), "c_identb", ("l1a_of", tt)], writes=[pbtk], name=f"{tpre}_otr{j}")
            dst = of.rearrange("p (j t) -> p j t", t=TC)[:, :, ts0:ts0 + P]
            p.emit("act", lambda e, pbt=pbt, dst=dst: nc.scalar.copy(dst, pbt.rearrange("p (j t) -> p j t", t=P)),
                   reads=[pbtk], writes=[("l1a_of", tt)], name=f"{tpre}_ofe")
        of_rhs = [(of[:, j * TC:(j + 1) * TC], ("l1a_of", tt_) ) for j in range(DT) for tt_ in [0]]
        of_rhs = [(of[:, j * TC:(j + 1) * TC], ("l1a_of",)) for j in range(DT)]
        p.emit("dve", lambda e: nc.vector.engine_nop(),
               reads=[("l1a_of", tt_) for tt_ in range(TC // P)], writes=[("l1a_of",)], name=f"{pre}_ofsync")
        for m in range(DT):
            pb, pbk = ctx.psum()
            ctx.mm(pb, pbk, [(wf_t(j, m * P, (m + 1) * P), ("l1a_wf", j, m)) for j in range(DT)],
                   of_rhs, name=f"{pre}_fc{m}")
            xo = ctx.scratch(f"l1a_xo{m % 2}", [P, TC], F32)
            p.emit("dve", lambda e, pb=pb, m=m, xo=xo:
                   nc.vector.tensor_add(xo, pb, xbuf[:, m * TC:(m + 1) * TC]),
                   reads=[pbk, ("l1a_xb", m)], writes=[("l1a_xo", m % 2)], name=f"{pre}_res{m}")
            p.dma("sp", xout_d[m * P:(m + 1) * P, t0:t0 + TC], xo,
                  reads=[("l1a_xo", m % 2)], writes=[("xo", ch, m)], name=f"{pre}_st{m}")



def emit_l0_ffn(ctx, xin_d, xout_d, W):
    nc, p = ctx.nc, ctx.p
    NH, HD = 16, 64
    FF = 2048
    FT = FF // P
    NR = 8
    CP = SEQ // TC
    QT = SEQ // P

    qkvb, qkvb_k = ctx.vec_to_cols("l0f_qkvb", W["qkv_b"], 3072)
    outb, outb_k = ctx.vec_to_cols("l0f_outb", W["out_b"], DM)
    n1w, n1w_k = ctx.vec_to_cols("l0f_n1w", W["norm1_w"], DM)
    n1b, n1b_k = ctx.vec_to_cols("l0f_n1b", W["norm1_b"], DM)
    n2w, n2w_k = ctx.vec_to_cols("l0f_n2w", W["norm2_w"], DM)
    n2b, n2b_k = ctx.vec_to_cols("l0f_n2b", W["norm2_b"], DM)
    sb1c, sb1_k = ctx.vec_to_cols("l0f_sb1", W["sb1"][0, :], FF)
    sb2c, sb2_k = ctx.vec_to_cols("l0f_sb2", W["sb2"][0, :], DM)
    rbcols = []
    for e in range(NR):
        r1, r1k = ctx.vec_to_cols(f"l0f_rb1_{e}", W["rb1"][e, :], FF)
        r2, r2k = ctx.vec_to_cols(f"l0f_rb2_{e}", W["rb2"][e, :], DM)
        rbcols.append((r1, r1k, r2, r2k))
    craw = ctx.scratch("l0f_craw", [NR, DM], F32)
    p.dma("sp", craw, W["centroid"], writes=[("l0f_craw",)], name="l0f_cld")
    centT = ctx.scratch("l0f_centT", [P, DT * NR], F32)
    for j in range(DT):
        pb, pbk = ctx.psum()
        p.emit("pe", lambda e, pb=pb, j=j:
               nc.tensor.transpose(pb[:, 0:NR], craw[:, j * P:(j + 1) * P], ctx.ident[:NR, :NR]),
               reads=[("l0f_craw",), "c_ident"], writes=[pbk], name=f"l0f_ctr{j}")
        p.emit("act", lambda e, pb=pb, j=j: nc.scalar.copy(centT[:, j * NR:(j + 1) * NR], pb[:, 0:NR]),
               reads=[pbk], writes=[("l0f_centT", j)], name=f"l0f_cev{j}")
    rbr = ctx.scratch("l0f_rbr", [1, NR], F32)
    p.dma("sp", rbr, W["route_bias"].rearrange("(a b) -> a b", b=NR), writes=[("l0f_rbr",)], name="l0f_rbld")
    rb_ps, rb_psk = ctx.bcast_row(rbr, ("l0f_rbr",), NR, name="l0f_rbbc")
    rbbc = ctx.scratch("l0f_rbbc", [P, NR], F32)
    p.emit("act", lambda e: nc.scalar.copy(rbbc, rb_ps), reads=[rb_psk], writes=[("l0f_rbbc",)], name="l0f_rbev")
    gateT = ctx.scratch("l0f_gT", [NR, TOK], BF16)

    WA = 8 * 3072 + 8 * 1024
    warena = ctx.scratch("l0f_warena", [P, WA], BF16)
    for kt in range(DT):
        p.dma("pool", warena[:, kt * 3072:(kt + 1) * 3072], W["qkv_w"][kt * P:(kt + 1) * P, :],
              reads=[("l0f_wguard",)], writes=[("l0f_wqkv", kt)], name=f"l0f_wq{kt}")
        p.dma("pool", warena[:, 8 * 3072 + kt * 1024: 8 * 3072 + (kt + 1) * 1024],
              W["out_w"][kt * P:(kt + 1) * P, :],
              reads=[("l0f_wguard",)], writes=[("l0f_wout", kt)], name=f"l0f_wo{kt}")
    qkv_t = lambda kt, m0, m1: warena[:, kt * 3072 + m0: kt * 3072 + m1]
    out_t = lambda kt, m0, m1: warena[:, 8 * 3072 + kt * 1024 + m0: 8 * 3072 + kt * 1024 + m1]
    rw1_t = lambda kt, m0, m1: warena[:, kt * 2048 + m0: kt * 2048 + m1]
    rw2_t = lambda kt, m0, m1: warena[:, 8 * 2048 + kt * 1024 + m0: 8 * 2048 + kt * 1024 + m1]

    v32_d = nc.dram_tensor("l0f_v32", [DM, TOK], F32, kind="Internal").ap()
    q_d = nc.dram_tensor("l0f_q", [DM, TOK], BF16, kind="Internal").ap()
    o_d = nc.dram_tensor("l0f_o", [DM, TOK], BF16, kind="Internal").ap()
    x1_d = nc.dram_tensor("l0f_x1", [DM, TOK], F32, kind="Internal").ap()
    x1b_d = nc.dram_tensor("l0f_x1b", [DM, TOK], BF16, kind="Internal").ap()
    y2_d = nc.dram_tensor("l0f_y2", [DM, TOK], F32, kind="Internal").ap()

    _mark = ctx.arena_mark()

    for b in range(NB):
        bt = b * SEQ
        kfm = ctx.scratch("l0f_kfm", [P, DT * SEQ], BF16)
        vtb = ctx.scratch("l0f_vt", [P, NH * QT * HD], BF16)
        vt_tile = lambda h, kt: vtb[:, (h * QT + kt) * HD:(h * QT + kt + 1) * HD]
        _mark2 = ctx.arena_mark()
        xbuf = ctx.scratch("l0f_xb", [P, DT * TC], F32)
        vb16 = ctx.scratch("l0f_vb", [P, DT * TC], BF16)
        v32 = xbuf
        vtmps = [ctx.scratch(f"l0f_vtmp{q}", [P, TC], BF16) for q in range(2)]

        for cc in range(CP):
            pre = f"l0fQ{b}_{cc}"
            t0 = bt + cc * TC
            for j in range(DT):
                p.dma("sp", xbuf[:, j * TC:(j + 1) * TC], xin_d[j * P:(j + 1) * P, t0:t0 + TC],
                      writes=[("l0f_xb", j)], name=f"{pre}_ld{j}")
            emit_pre_ln(ctx, lambda j: (xbuf[:, j * TC:(j + 1) * TC], ("l0f_xb", j)),
                        lambda j: (v32[:, j * TC:(j + 1) * TC], ("l0f_xb", j)),
                        name="l0fln",
                        out2_tile=lambda j: (vb16[:, j * TC:(j + 1) * TC], ("l0f_vb", j)))
            for j in range(DT):
                p.dma("sp", v32_d[j * P:(j + 1) * P, t0:t0 + TC], v32[:, j * TC:(j + 1) * TC],
                      reads=[("l0f_xb", j)], writes=[("v32_d", b, cc, j)], name=f"{pre}_sv{j}")
            vb_rhs = [(vb16[:, j * TC:(j + 1) * TC], ("l0f_vb", j)) for j in range(DT)]
            for m in range(24):
                pb, pbk = ctx.psum()
                ctx.mm(pb, pbk, [(qkv_t(j, m * P, (m + 1) * P), ("l0f_wqkv", j)) for j in range(DT)],
                       vb_rhs, name=f"{pre}_mm{m}")
                mq = m % 2
                vtmp = vtmps[mq]
                if m < 8:
                    p.emit("act", lambda e, pb=pb, m=m, vtmp=vtmp: nc.scalar.activation(vtmp, pb, AF.Identity, bias=qkvb[:, m:m + 1]),
                           reads=[pbk, qkvb_k, ("l0f_vtmp", mq)], writes=[("l0f_vtmp", mq)], name=f"{pre}_qe{m}")
                    p.dma("sp", q_d[m * P:(m + 1) * P, t0:t0 + TC], vtmp,
                          reads=[("l0f_vtmp", mq)], writes=[("q_d", b, cc, m)], name=f"{pre}_qs{m}")
                elif m < 16:
                    p.emit("act", lambda e, pb=pb, m=m, cc=cc:
                           nc.scalar.activation(kfm[:, (m - 8) * SEQ + cc * TC:(m - 8) * SEQ + (cc + 1) * TC], pb, AF.Identity, bias=qkvb[:, m:m + 1]),
                           reads=[pbk, qkvb_k], writes=[("l0f_kfm", m - 8, cc)], name=f"{pre}_ke{m}")
                else:
                    p.emit("act", lambda e, pb=pb, m=m, vtmp=vtmp: nc.scalar.activation(vtmp, pb, AF.Identity, bias=qkvb[:, m:m + 1]),
                           reads=[pbk, qkvb_k, ("l0f_vtmp", mq)], writes=[("l0f_vtmp", mq)], name=f"{pre}_ve{m}")
                    mj = m - 16
                    for hh in range(2):
                        h = 2 * mj + hh
                        pbt, pbtk = ctx.psum_bf()
                        for ts in range(4):
                            p.emit("pe", lambda e, hh=hh, ts=ts, pbt=pbt, vtmp=vtmp:
                                   nc.tensor.transpose(pbt[:, ts * HD:(ts + 1) * HD], vtmp[hh * HD:(hh + 1) * HD, ts * P:(ts + 1) * P],
                                                       ctx.ident_bf[hh * HD:(hh + 1) * HD, hh * HD:(hh + 1) * HD]),
                                   reads=[("l0f_vtmp", mq), "c_identb", ("l0f_vt", h, cc)], writes=[pbtk], name=f"{pre}_vt{mj}_{hh}_{ts}")
                        dst = vtb.rearrange("p (h k d) -> p h k d", k=QT, d=HD)[:, h, cc * 4:(cc + 1) * 4, :]
                        p.emit("act", lambda e, pbt=pbt, dst=dst:
                               nc.scalar.copy(dst, pbt[:, 0:4 * HD].rearrange("p (k d) -> p k d", d=HD)),
                               reads=[pbtk], writes=[("l0f_vt", h, cc)], name=f"{pre}_vte{mj}_{hh}")
        ctx.arena_to(_mark2)
        qtiles = [ctx.scratch(f"l0f_qt{q}", [P, DT * P], BF16) for q in range(2)]
        pbufs = [ctx.scratch(f"l0f_pbuf{q}", [P, SEQ], BF16) for q in range(2)]
        ptbs = [ctx.scratch(f"l0f_ptb{q}", [P, QT * P], BF16) for q in range(2)]
        osbs = [ctx.scratch(f"l0f_osb{q}", [P, DT * P], BF16) for q in range(2)]
        nmx4s = [ctx.scratch(f"l0f_nmx{q}", [P, 8], F32) for q in range(2)]
        bias8s = [ctx.scratch(f"l0f_b8{q}", [P, 1], F32) for q in range(2)]
        rcps = [ctx.scratch(f"l0f_rcp{q}", [P, 1], F32) for q in range(2)]
        for qi in range(QT):
            apre = f"l0fA{b}_{qi}"
            kl = (qi + 1) * P
            nkb = ceil_div(kl, 512)
            qq = qi % 2
            qtile = qtiles[qq]
            osb = osbs[qq]
            p.dma("sp", qtile.rearrange("p (a t) -> p a t", t=P),
                  q_d[:, bt + qi * P: bt + (qi + 1) * P].rearrange("(a b) t -> b a t", b=P),
                  reads=[("q_d", b, qi // 4, m_) for m_ in range(8)] + [("l0f_qt", qq)],
                  writes=[("l0f_qt", qq)], name=f"{apre}_lq")
            def emit_score(h):
                hpre = f"{apre}h{h}"
                hq = h % 2
                pbuf = pbufs[hq]
                nmx4, bias8, rcp = nmx4s[hq], bias8s[hq], rcps[hq]
                hb2 = (h % 2) * HD
                qap = qtile[hb2:hb2 + HD, (h // 2) * P:(h // 2 + 1) * P]
                kap = lambda k0, k1, h=h: kfm[(h % 2) * HD:(h % 2) * HD + HD, (h // 2) * SEQ + k0:(h // 2) * SEQ + k1]
                sbanks = []
                for kb in range(nkb):
                    k0 = kb * 512
                    k1 = min(kl, k0 + 512)
                    pb, pbk = ctx.psum()
                    sbanks.append((pb, pbk, k0, k1))
                    p.emit("pe", lambda e, pb=pb, qap=qap, k0=k0, k1=k1, h=h:
                           nc.tensor.matmul(pb[:, 0:k1 - k0], qap, kap(k0, k1, h), start=True, stop=True),
                           reads=[("l0f_qt", qq)] + [("l0f_kfm", h // 2, cc_) for cc_ in range(CP)],
                           writes=[pbk], name=f"{hpre}_s{kb}")
                pbL, pbLk, k0L, k1L = sbanks[-1]
                wL = k1L - k0L
                p.emit("dve", lambda e, pbL=pbL, wL=wL:
                       nc.vector.tensor_add(pbL[:, wL - P:wL], pbL[:, wL - P:wL], ctx.cmask),
                       reads=[pbLk, "c_cmask"], writes=[pbLk], name=f"{hpre}_msk")
                for kb, (pb, pbk, k0, k1) in enumerate(sbanks):
                    p.emit("dve", lambda e, pb=pb, kb=kb, k0=k0, k1=k1, nmx4=nmx4:
                           nc.vector.reduce_max(nmx4[:, kb:kb + 1], pb[:, 0:k1 - k0], axis=AX.X, negate=True),
                           reads=[pbk], writes=[("l0f_nmx", hq, kb)], name=f"{hpre}_mx{kb}")
                p.emit("dve", lambda e, nkb=nkb, nmx4=nmx4, bias8=bias8:
                       nc.vector.tensor_reduce(bias8, nmx4[:, 0:nkb], axis=AX.X, op=ALU.min),
                       reads=[("l0f_nmx", hq, kb_) for kb_ in range(nkb)], writes=[("l0f_b8a", hq)], name=f"{hpre}_mn")
                p.emit("dve", lambda e, bias8=bias8: nc.vector.tensor_scalar_mul(bias8, bias8, 0.125),
                       reads=[("l0f_b8a", hq)], writes=[("l0f_b8", hq)], name=f"{hpre}_bs")
                for kb, (pb, pbk, k0, k1) in enumerate(sbanks):
                    p.emit("act", lambda e, pb=pb, kb=kb, k0=k0, k1=k1, pbuf=pbuf, bias8=bias8, nmx4=nmx4:
                           nc.scalar.activation(pbuf[:, k0:k1], pb[:, 0:k1 - k0], AF.Exp, scale=0.125,
                                                bias=bias8, accum_out=nmx4[:, 4 + kb:5 + kb]),
                           reads=[pbk, ("l0f_b8", hq)], writes=[("l0f_pb", hq, kb), ("l0f_sm", hq, kb)], name=f"{hpre}_ex{kb}")
                p.emit("dve", lambda e, nkb=nkb, rcp=rcp, nmx4=nmx4: nc.vector.reduce_sum(rcp, nmx4[:, 4:4 + nkb], axis=AX.X),
                       reads=[("l0f_sm", hq, kb_) for kb_ in range(nkb)], writes=[("l0f_rcp", hq)], name=f"{hpre}_sm")
                p.emit("dve", lambda e, rcp=rcp: nc.vector.reciprocal(rcp, rcp),
                       reads=[("l0f_rcp", hq)], writes=[("l0f_rcp", hq)], name=f"{hpre}_rc")
                p.emit("dve", lambda e, kl=kl, pbuf=pbuf, rcp=rcp: nc.vector.tensor_scalar_mul(pbuf[:, 0:kl], pbuf[:, 0:kl], rcp),
                       reads=[("l0f_rcp", hq)] + [("l0f_pb", hq, kb_) for kb_ in range(nkb)],
                       writes=[("l0f_pbn", hq)], name=f"{hpre}_nm")

            def emit_out(h):
                hpre = f"{apre}h{h}"
                hq = h % 2
                pbuf, ptb = pbufs[hq], ptbs[hq]
                hb2 = (h % 2) * HD
                ntr = qi + 1
                for g in range(ceil_div(ntr, 8)):
                    pbt, pbtk = ctx.psum_bf()
                    cnt = min(8, ntr - g * 8)
                    for u in range(cnt):
                        kt = g * 8 + u
                        p.emit("pe", lambda e, kt=kt, u=u, pbt=pbt, pbuf=pbuf:
                               nc.tensor.transpose(pbt[:, u * P:(u + 1) * P], pbuf[:, kt * P:(kt + 1) * P], ctx.ident_bf),
                               reads=[("l0f_pbn", hq), "c_identb", ("l0f_ptb", hq, g)], writes=[pbtk], name=f"{hpre}_pt{kt}")
                    p.emit("act", lambda e, pbt=pbt, g=g, cnt=cnt, ptb=ptb:
                           nc.scalar.copy(ptb[:, g * 8 * P: g * 8 * P + cnt * P], pbt[:, 0:cnt * P]),
                           reads=[pbtk], writes=[("l0f_ptb", hq, g)], name=f"{hpre}_pte{g}")
                po, pok = ctx.psum()
                poap = po[hb2:hb2 + HD, 0:P]
                tpos = (0, hb2)
                for kt in range(ntr):
                    p.emit("pe", lambda e, kt=kt, poap=poap, h=h, ntr=ntr, tpos=tpos, ptb=ptb:
                           nc.tensor.matmul(poap, vt_tile(h, kt), ptb[:, kt * P:(kt + 1) * P],
                                            start=(kt == 0), stop=(kt == ntr - 1), tile_position=tpos),
                           reads=[("l0f_ptb", hq, kt // 8), ("l0f_vt", h, kt // 4)], writes=[pok], name=f"{hpre}_pv{kt}")
                p.emit("act", lambda e, poap=poap, h=h, hb2=hb2, osb=osb:
                       nc.scalar.copy(osb[hb2:hb2 + HD, (h // 2) * P:(h // 2 + 1) * P], poap),
                       reads=[pok], writes=[("l0f_osb", qq, h)], name=f"{hpre}_oe")

            for h in range(NH):
                emit_score(h)
                if h > 0:
                    emit_out(h - 1)
            emit_out(NH - 1)
            p.dma("sp", o_d[:, bt + qi * P: bt + (qi + 1) * P].rearrange("(a b) t -> b a t", b=P),
                  osb.rearrange("p (a t) -> p a t", t=P),
                  reads=[("l0f_osb", qq, h_) for h_ in range(NH)], writes=[("o_d", b, qi), ("l0f_osb", qq)], name=f"{apre}_so")
        ctx.arena_to(_mark)

        woutb = ctx.scratch("l0f_woutb", [P, DT * DM], BF16)
        for kt in range(DT):
            p.dma("pool", woutb[:, kt * 1024:(kt + 1) * 1024], W["out_w"][kt * P:(kt + 1) * P, :],
                  writes=[("l0f_wout", kt)], name=f"l0fP{b}_wo{kt}")
        out_t = lambda kt, m0, m1: woutb[:, kt * 1024 + m0: kt * 1024 + m1]
        obuf = ctx.scratch("l0f_ob", [P, DT * TC], BF16)
        v32b = ctx.scratch("l0f_v32b", [P, DT * TC], F32)
        s1 = ctx.scratch("l0f_s1", [P, DT * TC], F32)
        x1f = ctx.scratch("l0f_x1f", [P, DT * TC], F32)
        x1b = ctx.scratch("l0f_x1b", [P, DT * TC], BF16)
        for cc in range(CP):
            pre = f"l0fP{b}_{cc}"
            t0 = bt + cc * TC
            for j in range(DT):
                p.dma("sp", obuf[:, j * TC:(j + 1) * TC],
                      o_d[j * P:(j + 1) * P, t0:t0 + TC],
                      reads=[("o_d", b, qi_) for qi_ in range(cc * 4, cc * 4 + 4)],
                      writes=[("l0f_ob", j)], name=f"{pre}_lo{j}")
                p.dma("sp", v32b[:, j * TC:(j + 1) * TC], v32_d[j * P:(j + 1) * P, t0:t0 + TC],
                      reads=[("v32_d", b, cc, j)], writes=[("l0f_v32b", j)], name=f"{pre}_lv{j}")
            o_rhs = [(obuf[:, j * TC:(j + 1) * TC], ("l0f_ob", j)) for j in range(DT)]
            for m in range(DT):
                pb, pbk = ctx.psum()
                ctx.mm(pb, pbk, [(out_t(j, m * P, (m + 1) * P), ("l0f_wout", j)) for j in range(DT)],
                       o_rhs, name=f"{pre}_om{m}")
                p.emit("dve", lambda e, pb=pb, m=m:
                       nc.vector.tensor_scalar_add(s1[:, m * TC:(m + 1) * TC], pb, outb[:, m:m + 1]),
                       reads=[pbk, outb_k], writes=[("l0f_s1", m)], name=f"{pre}_ob{m}")
                p.emit("dve", lambda e, m=m:
                       nc.vector.tensor_add(s1[:, m * TC:(m + 1) * TC], s1[:, m * TC:(m + 1) * TC], v32b[:, m * TC:(m + 1) * TC]),
                       reads=[("l0f_s1", m), ("l0f_v32b", m)], writes=[("l0f_s1", m)], name=f"{pre}_sv{m}")
            emit_pre_ln(ctx, lambda j: (s1[:, j * TC:(j + 1) * TC], ("l0f_s1", j)),
                        lambda j: (x1f[:, j * TC:(j + 1) * TC], ("l0f_x1f", j)),
                        name="l0fln1", gamma_beta=(n1w, n1w_k, n1b, n1b_k),
                        out2_tile=lambda j: (x1b[:, j * TC:(j + 1) * TC], ("l0f_x1b", j)))
            for j in range(DT):
                p.dma("sp", x1_d[j * P:(j + 1) * P, t0:t0 + TC], x1f[:, j * TC:(j + 1) * TC],
                      reads=[("l0f_x1f", j)], writes=[("x1_d", b * CP + cc, j)], name=f"{pre}_sx{j}")
                p.dma("sp", x1b_d[j * P:(j + 1) * P, t0:t0 + TC], x1b[:, j * TC:(j + 1) * TC],
                      reads=[("l0f_x1b", j)], writes=[("x1b_d", b * CP + cc, j)], name=f"{pre}_sxb{j}")
            for tt in range(TC // P):
                rpre = f"{pre}r{tt}"
                pb, pbk = ctx.psum()
                for j in range(DT):
                    p.emit("pe", lambda e, pb=pb, j=j, tt=tt:
                           nc.tensor.matmul(pb[:, 0:NR], x1f[:, j * TC + tt * P: j * TC + (tt + 1) * P],
                                            centT[:, j * NR:(j + 1) * NR], start=(j == 0), stop=(j == DT - 1)),
                           reads=[("l0f_x1f", j), ("l0f_centT", j)], writes=[pbk], name=f"{rpre}_mm{j}")
                sim = ctx.scratch("l0f_sim", [P, NR], F32)
                v2 = ctx.scratch("l0f_v2", [P, NR], F32)
                w_tm = ctx.scratch("l0f_wtm", [P, NR], BF16)
                nmr = ctx.scratch("l0f_nmr", [P, 1], F32)
                smr = ctx.scratch("l0f_smr", [P, 1], F32)
                m1c = ctx.scratch("l0f_m1", [P, 1], F32)
                m2c = ctx.scratch("l0f_m2", [P, 1], F32)
                tq = ctx.scratch("l0f_tq", [P, NR], F32)
                mk_ = ctx.scratch("l0f_mk", [P, NR], F32)
                p.emit("dve", lambda e, pb=pb: nc.vector.reduce_max(nmr, pb[:, 0:NR], axis=AX.X, negate=True),
                       reads=[pbk], writes=[("l0f_nmr",)], name=f"{rpre}_mx")
                p.emit("act", lambda e, pb=pb: nc.scalar.activation(sim, pb[:, 0:NR], AF.Exp, bias=nmr, accum_out=smr),
                       reads=[pbk, ("l0f_nmr",)], writes=[("l0f_sim",), ("l0f_smr",)], name=f"{rpre}_ex")
                p.emit("dve", lambda e: nc.vector.reciprocal(smr, smr),
                       reads=[("l0f_smr",)], writes=[("l0f_smr",)], name=f"{rpre}_rc")
                p.emit("dve", lambda e: nc.vector.tensor_scalar_mul(sim, sim, smr),
                       reads=[("l0f_sim",), ("l0f_smr",)], writes=[("l0f_sim",)], name=f"{rpre}_nm")
                p.emit("dve", lambda e: nc.vector.tensor_add(v2, sim, rbbc),
                       reads=[("l0f_sim",), ("l0f_rbbc",)], writes=[("l0f_v2",)], name=f"{rpre}_v2")
                p.emit("dve", lambda e: nc.vector.reduce_max(m1c, v2, axis=AX.X),
                       reads=[("l0f_v2",)], writes=[("l0f_m1",)], name=f"{rpre}_m1")
                p.emit("dve", lambda e: nc.vector.tensor_scalar_sub(tq, v2, m1c),
                       reads=[("l0f_v2",), ("l0f_m1",)], writes=[("l0f_tq",)], name=f"{rpre}_t")
                p.emit("dve", lambda e: nc.vector.tensor_single_scalar(mk_, tq, 0.0, ALU.is_equal),
                       reads=[("l0f_tq",)], writes=[("l0f_mk",)], name=f"{rpre}_k0")
                p.emit("dve", lambda e: nc.vector.scalar_tensor_tensor(tq, mk_, -1e30, tq, ALU.mult, ALU.add),
                       reads=[("l0f_tq",), ("l0f_mk",)], writes=[("l0f_tq",)], name=f"{rpre}_k1")
                p.emit("dve", lambda e: nc.vector.reduce_max(m2c, tq, axis=AX.X),
                       reads=[("l0f_tq",)], writes=[("l0f_m2",)], name=f"{rpre}_m2")
                p.emit("dve", lambda e: nc.vector.tensor_add(m2c, m2c, m1c),
                       reads=[("l0f_m2",), ("l0f_m1",)], writes=[("l0f_m2",)], name=f"{rpre}_m2b")
                p.emit("dve", lambda e: nc.vector.tensor_scalar(mk_, v2, m2c, None, ALU.is_ge),
                       reads=[("l0f_v2",), ("l0f_m2",)], writes=[("l0f_mk",)], name=f"{rpre}_kp")
                p.emit("dve", lambda e: nc.vector.tensor_mul(tq, sim, mk_),
                       reads=[("l0f_sim",), ("l0f_mk",)], writes=[("l0f_tq",)], name=f"{rpre}_gp")
                p.emit("dve", lambda e: nc.vector.reduce_sum(smr, tq, axis=AX.X),
                       reads=[("l0f_tq",)], writes=[("l0f_smr",)], name=f"{rpre}_gs")
                p.emit("dve", lambda e: nc.vector.reciprocal(smr, smr),
                       reads=[("l0f_smr",)], writes=[("l0f_smr",)], name=f"{rpre}_gr")
                p.emit("dve", lambda e: nc.vector.tensor_scalar_mul(w_tm, tq, smr),
                       reads=[("l0f_tq",), ("l0f_smr",)], writes=[("l0f_wtm",)], name=f"{rpre}_gw")
                pbt, pbtk = ctx.psum_bf()
                p.emit("pe", lambda e, pbt=pbt: nc.tensor.transpose(pbt[0:NR, 0:P], w_tm, ctx.ident_bf),
                       reads=[("l0f_wtm",), "c_identb"], writes=[pbtk], name=f"{rpre}_wt")
                p.emit("act", lambda e, pbt=pbt, t0=t0, tt=tt:
                       nc.scalar.copy(gateT[:, t0 + tt * P: t0 + (tt + 1) * P], pbt[0:NR, 0:P]),
                       reads=[pbtk], writes=[("l0f_gT", b * CP + cc, tt)], name=f"{rpre}_wte")
        ctx.arena_to(_mark)

    p.emit("dve", lambda e: nc.vector.engine_nop(),
           writes=[("l0f_wqkv", kt) for kt in range(DT)] + [("l0f_wout", kt) for kt in range(DT)] + [("l0f_wguard",)],
           name="l0f_wfree")
    hsh = ctx.scratch("l0f_hsh", [P, FT * TC], BF16)
    x1c = ctx.scratch("l0f_x1c", [P, DT * TC], BF16)
    x1fcs = [ctx.scratch(f"l0f_x1fc{q}", [P, TC], F32) for q in range(2)]
    wbc = ctx.scratch("l0f_wbc", [P, TC], BF16)
    t1ss = [ctx.scratch(f"l0f_t1{q}", [P, TC], F32) for q in range(2)]
    t2ss = [ctx.scratch(f"l0f_t2{q}", [P, TC], F32) for q in range(2)]
    sw1b = ctx.scratch("l0f_sw1", [P, DT * FF], BF16)
    sw2b = ctx.scratch("l0f_sw2", [P, FT * DM], BF16)
    for kt in range(DT):
        p.dma("pool", sw1b[:, kt * FF:(kt + 1) * FF], W["sw1"][kt * P:(kt + 1) * P, :],
              writes=[("l0f_sw1", kt)], name=f"l0f_sw1_{kt}")
    for kt in range(FT):
        p.dma("pool", sw2b[:, kt * DM:(kt + 1) * DM], W["sw2"][kt * P:(kt + 1) * P, :],
              writes=[("l0f_sw2", kt)], name=f"l0f_sw2_{kt}")
    for e in range(-1, NR):
        epre = f"l0fE{e}"
        if e >= 0:
            for kt in range(DT):
                p.dma("pool", rw1_t(kt, 0, FF), W["rw1"][e * DM + kt * P:e * DM + (kt + 1) * P, :],
                      reads=[("l0f_wguard",)], writes=[("l0f_rw1", kt)], name=f"{epre}_lw1{kt}")
            for kt in range(FT):
                p.dma("pool", rw2_t(kt, 0, DM), W["rw2"][e * FF + kt * P:e * FF + (kt + 1) * P, :],
                      reads=[("l0f_wguard",)], writes=[("l0f_rw2", kt)], name=f"{epre}_lw2{kt}")
        for ch in range(NCH):
            t0 = ch * TC
            cpre = f"{epre}c{ch}"
            for j in range(DT):
                p.dma("sp", x1c[:, j * TC:(j + 1) * TC], x1b_d[j * P:(j + 1) * P, t0:t0 + TC],
                      reads=[("x1b_d", ch, j)], writes=[("l0f_x1c", j)], name=f"{cpre}_lx{j}")
            x1_rhs = [(x1c[:, j * TC:(j + 1) * TC], ("l0f_x1c", j)) for j in range(DT)]
            if e >= 0:
                w1t, w1keys = rw1_t, [("l0f_rw1", j) for j in range(DT)]
                w2t, w2keys = rw2_t, [("l0f_rw2", j) for j in range(FT)]
                b1c, b1k, b2c, b2k = rbcols[e]
                act_f = AF.Gelu
            else:
                w1t, w1keys = (lambda kt, m0, m1: sw1b[:, kt * FF + m0:kt * FF + m1]), [("l0f_sw1", j) for j in range(DT)]
                w2t, w2keys = (lambda kt, m0, m1: sw2b[:, kt * DM + m0:kt * DM + m1]), [("l0f_sw2", j) for j in range(FT)]
                b1c, b1k, b2c, b2k = sb1c, sb1_k, sb2c, sb2_k
                act_f = AF.Gelu
            for fg in range(FT):
                pb, pbk = ctx.psum()
                ctx.mm(pb, pbk, [(w1t(j, fg * P, (fg + 1) * P), w1keys[j]) for j in range(DT)],
                       x1_rhs, name=f"{cpre}_h{fg}")
                p.emit("act", lambda e_, pb=pb, fg=fg, b1c=b1c:
                       nc.scalar.activation(hsh[:, fg * TC:(fg + 1) * TC], pb, act_f, bias=b1c[:, fg:fg + 1]),
                       reads=[pbk, b1k], writes=[("l0f_hsh", fg)], name=f"{cpre}_ha{fg}")
            if e >= 0:
                pbg, pbgk = ctx.psum()
                p.emit("pe", lambda e_, pbg=pbg, t0=t0, ee=e:
                       nc.tensor.matmul(pbg, ctx.oh8[:, ee * P:(ee + 1) * P], gateT[:, t0:t0 + TC], start=True, stop=True),
                       reads=["c_oh8"] + [("l0f_gT", c_, tt_) for c_ in range(NCH) for tt_ in range(TC // P)],
                       writes=[pbgk], name=f"{cpre}_gbc")
                p.emit("act", lambda e_, pbg=pbg: nc.scalar.copy(wbc, pbg),
                       reads=[pbgk], writes=[("l0f_wbc",)], name=f"{cpre}_gbe")
            h_rhs = [(hsh[:, j * TC:(j + 1) * TC], ("l0f_hsh", j)) for j in range(FT)]
            for m in range(DT):
                mq = m % 2
                x1fc, t1s, t2s = x1fcs[mq], t1ss[mq], t2ss[mq]
                pb, pbk = ctx.psum()
                ctx.mm(pb, pbk, [(w2t(j, m * P, (m + 1) * P), w2keys[j]) for j in range(FT)],
                       h_rhs, name=f"{cpre}_o{m}")
                if e < 0:
                    p.dma("sp", x1fc, x1_d[m * P:(m + 1) * P, t0:t0 + TC],
                          reads=[("x1_d", ch, m)], writes=[("l0f_x1fc", mq)], name=f"{cpre}_lf{m}")
                    p.emit("dve", lambda e_, pb=pb, m=m, b2c=b2c, t1s=t1s:
                           nc.vector.tensor_scalar_add(t1s, pb, b2c[:, m:m + 1]),
                           reads=[pbk, b2k], writes=[("l0f_t1", mq)], name=f"{cpre}_b2{m}")
                    p.emit("dve", lambda e_, t1s=t1s, t2s=t2s, x1fc=x1fc:
                           nc.vector.tensor_add(t2s, t1s, x1fc),
                           reads=[("l0f_t1", mq), ("l0f_x1fc", mq)], writes=[("l0f_t2", mq)], name=f"{cpre}_sx{m}")
                    p.dma("pool", y2_d[m * P:(m + 1) * P, t0:t0 + TC], t2s,
                          reads=[("l0f_t2", mq)], writes=[("y2_d", ch, m)], name=f"{cpre}_sy{m}")
                else:
                    p.emit("dve", lambda e_, pb=pb, m=m, b2c=b2c, t1s=t1s:
                           nc.vector.tensor_scalar_add(t1s, pb, b2c[:, m:m + 1]),
                           reads=[pbk, b2k], writes=[("l0f_t1", mq)], name=f"{cpre}_b2{m}")
                    p.emit("dve", lambda e_, t1s=t1s, t2s=t2s:
                           nc.vector.tensor_mul(t2s, t1s, wbc),
                           reads=[("l0f_t1", mq), ("l0f_wbc",)], writes=[("l0f_t2", mq)], name=f"{cpre}_gg{m}")
                    p.dma("pool", y2_d[m * P:(m + 1) * P, t0:t0 + TC], t2s,
                          reads=[("l0f_t2", mq), ("y2_d", ch, m)], writes=[("y2_d", ch, m)],
                          name=f"{cpre}_ay{m}", accum_op=ALU.add)
    ctx.arena_to(_mark)

    y2b = ctx.scratch("l0f_y2b", [P, DT * TC], F32)
    xfb = ctx.scratch("l0f_xfb", [P, DT * TC], F32)
    o2s = [ctx.scratch(f"l0f_o2{q}", [P, TC], F32) for q in range(2)]
    for ch in range(NCH):
        t0 = ch * TC
        fpre = f"l0fF{ch}"
        for j in range(DT):
            p.dma("sp", y2b[:, j * TC:(j + 1) * TC], y2_d[j * P:(j + 1) * P, t0:t0 + TC],
                  reads=[("y2_d", ch, j)], writes=[("l0f_y2b", j)], name=f"{fpre}_ly{j}")
            p.dma("sp", xfb[:, j * TC:(j + 1) * TC], xin_d[j * P:(j + 1) * P, t0:t0 + TC],
                  writes=[("l0f_xfb", j)], name=f"{fpre}_lx{j}")
        emit_pre_ln(ctx, lambda j: (y2b[:, j * TC:(j + 1) * TC], ("l0f_y2b", j)),
                    lambda j: (y2b[:, j * TC:(j + 1) * TC], ("l0f_y2b", j)),
                    name="l0fln2", gamma_beta=(n2w, n2w_k, n2b, n2b_k))
        for j in range(DT):
            o2 = o2s[j % 2]
            p.emit("dve", lambda e, j=j, o2=o2:
                   nc.vector.tensor_add(o2, y2b[:, j * TC:(j + 1) * TC], xfb[:, j * TC:(j + 1) * TC]),
                   reads=[("l0f_y2b", j), ("l0f_xfb", j)], writes=[("l0f_o2", j % 2)], name=f"{fpre}_add{j}")
            p.dma("sp", xout_d[j * P:(j + 1) * P, t0:t0 + TC], o2,
                  reads=[("l0f_o2", j % 2)], writes=[("xo", ch, j)], name=f"{fpre}_st{j}")



L3A_DEBUG = False


def emit_l3_attn(ctx, xin_d, xout_d, W):
    nc, p = ctx.nc, ctx.p
    if L3A_DEBUG:
        dbg_d = nc.dram_tensor("l3a_dbg", [NCH * DM, TC], F32, kind="Internal").ap()
        dbg2_d = nc.dram_tensor("l3a_dbg2", [NCH, TC], F32, kind="Internal").ap()
        dbg4_d = nc.dram_tensor("l3a_dbg4", [NCH, TC], F32, kind="Internal").ap()
        dbg5_d = nc.dram_tensor("l3a_dbg5", [NCH * P, TC], F32, kind="Internal").ap()
        dbg6_d = nc.dram_tensor("l3a_dbg6", [NCH, TC], F32, kind="Internal").ap()
        dbg3_d = nc.dram_tensor("l3a_dbg3", [NCH * DM, TC], F32, kind="Internal").ap()
    NHd = 16
    HD = 64
    NSt = 128
    TF = 32
    SEL = 28
    NSEL = 12
    CONVD = 1280
    CT10 = CONVD // P
    DIP = 2336
    HIN = 1056
    KCONV = 4
    EPS = 1e-5

    _, wip_t, _ = ctx.load_w_bf16_T("l3a_wip", W["in_proj_w"], DIP, DM)
    _, whp_t, _ = ctx.load_w_bf16_T("l3a_whp", W["h_proj_w"], 40, HIN)
    _, wop_t, _ = ctx.load_w_bf16_T("l3a_wop", W["out_proj_w"], DM, DM)
    nw_c, nw_k = ctx.vec_to_cols("l3a_nw", W["norm_w"], DM)
    cb_c, cb_k = ctx.vec_to_cols("l3a_cb", W["conv_b"], CONVD)
    cwraw = ctx.scratch("l3a_cwr", [CT10, P * KCONV], F32)
    p.dma("sp", cwraw, W["conv_w"].rearrange("(a b) k -> a (b k)", b=P), writes=[("l3a_cwr",)], name="l3a_cwld")
    cw = ctx.scratch("l3a_cw", [P, CT10 * KCONV], F32)
    for k in range(KCONV):
        pb, pbk = ctx.psum()
        ap_in = cwraw.rearrange("a (b k) -> a b k", k=KCONV)[:, :, k]
        p.emit("pe", lambda e, ap_in=ap_in, pb=pb: nc.tensor.transpose(pb[:, 0:CT10], ap_in, ctx.ident[:CT10, :CT10]),
               reads=[("l3a_cwr",), "c_ident"], writes=[pbk], name=f"l3a_cwtr{k}")
        p.emit("act", lambda e, pb=pb, k=k: nc.scalar.copy(
            cw.rearrange("p (j q) -> p j q", q=KCONV)[:, :, k], pb[:, 0:CT10]),
            reads=[pbk], writes=[("l3a_cw", k)], name=f"l3a_cwev{k}")
    cw_col = lambda j, k: cw[:, (j * KCONV + k):(j * KCONV + k) + 1]
    def row_bcast(tag, dram_ap, n):
        r = ctx.scratch(f"l3a_{tag}r", [1, n], F32)
        p.dma("sp", r, dram_ap.rearrange("(a b) -> a b", b=n), writes=[("l3a", tag, "r")], name=f"l3a_{tag}ld")
        ps_, psk_ = ctx.bcast_row(r, ("l3a", tag, "r"), n, name=f"l3a_{tag}bc")
        out = ctx.scratch(f"l3a_{tag}", [P, n], F32)
        p.emit("act", lambda e, ps_=ps_, out=out: nc.scalar.copy(out, ps_),
               reads=[psk_], writes=[("l3a", tag)], name=f"l3a_{tag}ev")
        return out, ("l3a", tag)
    gam_bc, gam_k = row_bcast("gam", W["gamma"], NHd)
    dtb_bc, dtb_k = row_bcast("dtb", W["dt_bias"], 1)
    dpar_bc, dpar_k = row_bcast("dp", W["D"], NHd)
    alog_bc, alog_k = row_bcast("alog", W["A_log"], NHd)
    negA = ctx.scratch("l3a_negA", [P, NHd], F32)
    p.emit("act", lambda e: nc.scalar.activation(negA, alog_bc, AF.Exp),
           reads=[alog_k], writes=[("l3a_negA1",)], name="l3a_nA1")
    p.emit("dve", lambda e: nc.vector.tensor_scalar_mul(negA, negA, -1.0),
           reads=[("l3a_negA1",)], writes=[("l3a_negA",)], name="l3a_nA2")
    invd_np = (1.0 / np.arange(1, SEQ + 1, dtype=np.float64)).astype(np.float32).reshape(1, SEQ)
    invd_d = nc.inline_tensor(invd_np, name="l3a_invd_d")
    invd = ctx.scratch("l3a_invd", [1, TC], F32)
    tsep_d = nc.dram_tensor("l3a_tsep", [NHd, TC], F32, kind="Internal").ap()
    trow1 = ctx.scratch("l3a_trow", [1, TC], F32)
    tri_np = np.where(np.arange(P)[None, :] >= np.arange(P)[:, None], 1.0, 0.0).astype(np.float32)
    tri_d = nc.inline_tensor(tri_np, name="l3a_tri_d")
    tri01 = ctx.scratch("l3a_tri", [P, P], BF16)
    p.dma("pool", tri01, tri_d.ap(), writes=[("l3a_tri",)], name="l3a_trild")

    cscar = ctx.scratch("l3a_cscar", [P, DT], F32)
    sT = ctx.scratch("l3a_sT", [P, NHd * HD], F32)
    sTb16 = ctx.scratch("l3a_sTb", [P, NHd * HD], BF16)
    zero1 = ctx.scratch("l3a_zero", [P, 1], F32)
    p.emit("dve", lambda e: nc.vector.memset(zero1, 0.0), writes=[("l3a_zero",)], name="l3a_z0")

    HW = TC + 3
    xbuf = ctx.scratch("l3a_xb", [P, DT * TC], F32)
    xnb = ctx.scratch("l3a_xn", [P, DT * TC], BF16)
    cst = ctx.scratch("l3a_cst", [P, TC], F32)
    srb = ctx.scratch("l3a_sr", [P, DT * TC], BF16)
    zb = ctx.scratch("l3a_zb", [P, DT * TC], BF16)
    xib = ctx.scratch("l3a_xi", [P, CT10 * HW], BF16)
    xcb = ctx.scratch("l3a_xc", [P, DT * TC], BF16)
    bcb = ctx.scratch("l3a_bc", [P, 2 * TC], BF16)
    dtaf = ctx.scratch("l3a_dtaf", [32, TC], F32)
    dtat = ctx.scratch("l3a_dtat", [P, 4 * 32], F32)
    hbs = ctx.scratch("l3a_hbs", [40, TC], F32)
    hbt = ctx.scratch("l3a_hbt", [P, 4 * 40], F32)
    dttm = ctx.scratch("l3a_dttm", [P, 4 * NHd], F32)
    tfm = ctx.scratch("l3a_tfm", [NHd, TC], F32)
    tcolb = ctx.scratch("l3a_tcol", [P, 4 * NHd], F32)
    trbs = [ctx.scratch(f"l3a_trb{q}", [P, TC], F32) for q in range(2)]
    GT_OFF = [0, TC, TC + (TC - P), TC + (TC - P) + (TC - 2 * P)]
    gtb = ctx.scratch("l3a_gt", [P, GT_OFF[3] + (TC - 3 * P)], BF16)
    gt_sl = lambda ii: gtb[:, GT_OFF[ii]:GT_OFF[ii] + (TC - ii * P)]
    xtb = ctx.scratch("l3a_xt", [P, NHd * 4 * HD], BF16)
    btb = ctx.scratch("l3a_bt", [P, 4 * P], BF16)
    cdec1 = ctx.scratch("l3a_cdec", [P, TC], BF16)
    yg = ctx.scratch("l3a_yg", [P, DT * TC], BF16)
    m8a = ctx.scratch("l3a_m8a", [P, 8], F32)
    m8b = ctx.scratch("l3a_m8b", [P, 8], F32)
    hb2t = ctx.scratch("l3a_hb2", [P, SEL], F32)
    t28 = ctx.scratch("l3a_t28", [P, SEL], F32)
    dcols = [ctx.scratch(f"l3a_dcol{q}", [P, 1], F32) for q in range(2)]
    e3cs = [ctx.scratch(f"l3a_e3{q}", [P, 1], F32) for q in range(2)]
    xws = [ctx.scratch(f"l3a_xw{q}", [P, HD], BF16) for q in range(2)]
    stmp1 = ctx.scratch("l3a_stmp", [P, HD], F32)
    rows3 = ctx.scratch("l3a_rows", [1, TC], F32)

    xt_tile = lambda h, ii: xtb[:, (h * 4 + ii) * HD:(h * 4 + ii + 1) * HD]

    for ch in range(NCH):
        pre = f"l3a{ch}"
        t0 = ch * TC
        b = ch // CPB
        first = (ch % CPB == 0)
        cloc = ch % CPB
        for j in range(DT):
            p.dma("sp", xbuf[:, j * TC:(j + 1) * TC], xin_d[j * P:(j + 1) * P, t0:t0 + TC],
                  writes=[("l3a_xb", j)], name=f"{pre}_ld{j}")
        emit_pre_ln(ctx, lambda j: (xbuf[:, j * TC:(j + 1) * TC], ("l3a_xb", j)),
                    lambda j: (xbuf[:, j * TC:(j + 1) * TC], ("l3a_xb", j)),
                    name="l3aln",
                    out2_tile=lambda j: (xnb[:, j * TC:(j + 1) * TC], ("l3a_xn", j)))
        p.dma("sp", invd, invd_d.ap()[:, cloc * TC:(cloc + 1) * TC],
              reads=[("l3a_invd",)], writes=[("l3a_invd",)], name=f"{pre}_ivld")
        pbI, pbIk = ctx.psum()
        p.emit("pe", lambda e, pbI=pbI:
               nc.tensor.matmul(pbI, ctx.ones_row, invd, start=True, stop=True),
               reads=[("l3a_invd",), "c_ones"], writes=[pbIk], name=f"{pre}_ibc")

        for j in range(DT):
            if first:
                p.emit("dve", lambda e, j=j: nc.vector.memset(cscar[:, j:j + 1], 0.0),
                       reads=[("l3a_cscar", j)], writes=[("l3a_cscar", j)], name=f"{pre}_cz{j}")
            p.emit("dve", lambda e, j=j:
                   nc.vector.tensor_tensor_scan(cst, xbuf[:, j * TC:(j + 1) * TC],
                                                zero1.to_broadcast([P, TC]), cscar[:, j:j + 1], ALU.add, ALU.add),
                   reads=[("l3a_xb", j), ("l3a_zero",), ("l3a_cscar", j), ("l3a_cst",)], writes=[("l3a_cst",)], name=f"{pre}_cs{j}")
            p.emit("dve", lambda e, j=j: nc.vector.tensor_copy(cscar[:, j:j + 1], cst[:, TC - 1:TC]),
                   reads=[("l3a_cst",)], writes=[("l3a_cscar", j)], name=f"{pre}_cc{j}")
            p.emit("dve", lambda e, j=j, pbI=pbI:
                   nc.vector.tensor_mul(cst, cst, pbI),
                   reads=[("l3a_cst",), pbIk, ("l3a_cscar", j)], writes=[("l3a_cst",)], name=f"{pre}_sm{j}")
            p.emit("dve", lambda e, j=j:
                   nc.vector.tensor_sub(srb[:, j * TC:(j + 1) * TC], xbuf[:, j * TC:(j + 1) * TC], cst),
                   reads=[("l3a_xb", j), ("l3a_cst",), ("l3a_srB",)], writes=[("l3a_sr", j), ("l3a_srB",)], name=f"{pre}_sr{j}")
        xn_rhs = [(xnb[:, j * TC:(j + 1) * TC], ("l3a_xn", j)) for j in range(DT)]
        for m in range(18):
            pb, pbk = ctx.psum()
            ctx.mm(pb, pbk, [(wip_t(j, m * P, (m + 1) * P), ("l3a_wip", j, m)) for j in range(DT)],
                   xn_rhs, name=f"{pre}_ip{m}")
            if m < DT:
                p.emit("act", lambda e, pb=pb, m=m: nc.scalar.activation(zb[:, m * TC:(m + 1) * TC], pb, AF.Silu),
                       reads=[pbk], writes=[("l3a_zb", m)], name=f"{pre}_ze{m}")
            else:
                mj = m - DT
                hk = ("l3a_xih", mj)
                if first:
                    p.emit("dve", lambda e, mj=mj: nc.vector.memset(xib[:, mj * HW:mj * HW + 3], 0.0),
                           reads=[("l3a_xi", mj)], writes=[hk], name=f"{pre}_hz{mj}")
                else:
                    p.emit("dve", lambda e, mj=mj:
                           nc.vector.tensor_copy(xib[:, mj * HW:mj * HW + 3], xib[:, (mj + 1) * HW - 3:(mj + 1) * HW]),
                           reads=[("l3a_xi", mj)], writes=[hk], name=f"{pre}_hc{mj}")
                p.emit("act", lambda e, pb=pb, mj=mj: nc.scalar.copy(xib[:, mj * HW + 3:(mj + 1) * HW], pb),
                       reads=[pbk, hk], writes=[("l3a_xi", mj)], name=f"{pre}_xe{mj}")
        pb, pbk = ctx.psum()
        ctx.mm(pb[0:32, 0:TC], pbk, [(wip_t(j, 18 * P, 18 * P + 32), ("l3a_wip", j, 18)) for j in range(DT)],
               xn_rhs, name=f"{pre}_ipdt")
        p.emit("act", lambda e, pb=pb: nc.scalar.copy(dtaf, pb[0:32, 0:TC]),
               reads=[pbk], writes=[("l3a_dtaf",)], name=f"{pre}_dte")
        for mj in range(CT10):
            mq = 0
            acc = ctx.scratch("l3a_cacc", [P, TC], F32)
            base = mj * HW
            p.emit("dve", lambda e, base=base, acc=acc, mj=mj:
                   nc.vector.tensor_scalar(acc, xib[:, base + 3:base + 3 + TC], cw_col(mj, 3), cb_c[:, mj:mj + 1], ALU.mult, ALU.add),
                   reads=[("l3a_xi", mj), ("l3a_xih", mj), ("l3a_cw", 3), cb_k], writes=[("l3a_cacc", mq)], name=f"{pre}_cv3_{mj}")
            for k in range(KCONV - 1):
                p.emit("dve", lambda e, base=base, acc=acc, mj=mj, k=k:
                       nc.vector.scalar_tensor_tensor(acc, xib[:, base + k:base + k + TC], cw_col(mj, k), acc, ALU.mult, ALU.add),
                       reads=[("l3a_xi", mj), ("l3a_xih", mj), ("l3a_cw", k), ("l3a_cacc", mq)],
                       writes=[("l3a_cacc", mq)], name=f"{pre}_cv{k}_{mj}")
            if mj < DT:
                p.emit("act", lambda e, acc=acc, mj=mj: nc.scalar.activation(xcb[:, mj * TC:(mj + 1) * TC], acc, AF.Silu),
                       reads=[("l3a_cacc", mq)], writes=[("l3a_xc", mj)], name=f"{pre}_cs{mj}")
            else:
                p.emit("act", lambda e, acc=acc, mj=mj:
                       nc.scalar.activation(bcb[:, (mj - DT) * TC:(mj - DT + 1) * TC], acc, AF.Silu),
                       reads=[("l3a_cacc", mq)], writes=[("l3a_bcb", mj - DT)], name=f"{pre}_cbs{mj}")
        pbh, pbhk = ctx.psum()
        dtab = ctx.scratch("l3a_dtab", [32, TC], BF16)
        p.emit("dve", lambda e: nc.vector.tensor_copy(dtab, dtaf),
               reads=[("l3a_dtaf",)], writes=[("l3a_dtab",)], name=f"{pre}_dtc")
        hp_lhs = [(whp_t(ct, 0, 40) if ct < 8 else whp_t(ct, 0, 40)[0:32, :], ("l3a_whp", ct)) for ct in range(9)]
        hp_rhs = [(srb[:, j * TC:(j + 1) * TC], ("l3a_sr", j)) for j in range(DT)] + [(dtab, ("l3a_dtab",))]
        for i_, ((la, lk), (ra, rk)) in enumerate(zip(hp_lhs, hp_rhs)):
            p.emit("pe", lambda e, la=la, ra=ra, i_=i_, pbh=pbh:
                   nc.tensor.matmul(pbh[0:40, 0:TC], la, ra, start=(i_ == 0), stop=(i_ == 8)),
                   reads=[lk, rk, ("l3a_srB",)], writes=[pbhk], name=f"{pre}_hp{i_}")
        p.emit("act", lambda e, pbh=pbh: nc.scalar.copy(hbs, pbh[0:40, 0:TC]),
               reads=[pbhk], writes=[("l3a_hbs",)], name=f"{pre}_hbe")
        for tt in range(4):
            pbt, pbtk = ctx.psum()
            p.emit("pe", lambda e, tt=tt, pbt=pbt:
                   nc.tensor.transpose(pbt[:, 0:40], hbs[:, tt * P:(tt + 1) * P], ctx.ident[0:40, 0:40]),
                   reads=[("l3a_hbs",), "c_ident"], writes=[pbtk], name=f"{pre}_htr{tt}")
            p.emit("act", lambda e, tt=tt, pbt=pbt: nc.scalar.copy(hbt[:, tt * 40:(tt + 1) * 40], pbt[:, 0:40]),
                   reads=[pbtk], writes=[("l3a_hbt", tt)], name=f"{pre}_hte{tt}")
            pbt2, pbt2k = ctx.psum()
            p.emit("pe", lambda e, tt=tt, pbt2=pbt2:
                   nc.tensor.transpose(pbt2[:, 0:32], dtaf[:, tt * P:(tt + 1) * P], ctx.ident[0:32, 0:32]),
                   reads=[("l3a_dtaf",), "c_ident"], writes=[pbt2k], name=f"{pre}_dtr{tt}")
            p.emit("act", lambda e, tt=tt, pbt2=pbt2: nc.scalar.copy(dtat[:, tt * 32:(tt + 1) * 32], pbt2[:, 0:32]),
                   reads=[pbt2k], writes=[("l3a_dtat", tt)], name=f"{pre}_dte{tt}")
        for tt in range(4):
            tpre = f"{pre}s{tt}"
            hsl = hbt[:, tt * 40: tt * 40 + SEL]
            p.emit("dve", lambda e, hsl=hsl: nc.vector.max(out=m8a, in_=hsl),
                   reads=[("l3a_hbt", tt)], writes=[("l3a_m8a",)], name=f"{tpre}_m8a")
            p.emit("dve", lambda e, hsl=hsl: nc.vector.match_replace(out=hb2t, in_to_replace=m8a, in_values=hsl, imm_value=-1e30),
                   reads=[("l3a_hbt", tt), ("l3a_m8a",)], writes=[("l3a_hb2",)], name=f"{tpre}_mr")
            p.emit("dve", lambda e: nc.vector.max(out=m8b, in_=hb2t),
                   reads=[("l3a_hb2",)], writes=[("l3a_m8b",)], name=f"{tpre}_m8b")
            dtt = dttm[:, tt * NHd:(tt + 1) * NHd]
            for jj in range(NSEL):
                mv = m8a[:, jj:jj + 1] if jj < 8 else m8b[:, jj - 8:jj - 7]
                mvk = ("l3a_m8a",) if jj < 8 else ("l3a_m8b",)
                p.emit("dve", lambda e, mv=mv, hsl=hsl, tt=tt:
                       nc.vector.scalar_tensor_tensor(t28, hsl, mv, dtat[:, tt * 32:tt * 32 + SEL], ALU.is_equal, ALU.mult),
                       reads=[("l3a_hbt", tt), mvk, ("l3a_dtat", tt)], writes=[("l3a_t28",)], name=f"{tpre}_g{jj}")
                p.emit("dve", lambda e, dtt=dtt, jj=jj: nc.vector.reduce_sum(dtt[:, jj:jj + 1], t28, axis=AX.X),
                       reads=[("l3a_t28",)], writes=[("l3a_dttm", tt, jj)], name=f"{tpre}_r{jj}")
            p.emit("dve", lambda e, dtt=dtt, tt=tt:
                   nc.vector.tensor_copy(dtt[:, NSEL:NHd], dtat[:, tt * 32 + SEL:tt * 32 + 32]),
                   reads=[("l3a_dtat", tt)], writes=[("l3a_dttm", tt, "sh")], name=f"{tpre}_sh")
            p.emit("dve", lambda e, dtt=dtt: nc.vector.tensor_scalar_add(dtt, dtt, dtb_bc[:, 0:1]),
                   reads=[("l3a_dttm", tt, jj_) for jj_ in range(NSEL)] + [("l3a_dttm", tt, "sh"), dtb_k],
                   writes=[("l3a_dttm", tt, "b")], name=f"{tpre}_db")
            sbt = ctx.scratch("l3a_sbt", [P, NSEL], F32)
            p.emit("dve", lambda e, tt=tt, sbt=sbt:
                   nc.vector.tensor_mul(sbt, hbt[:, tt * 40 + SEL:tt * 40 + 40], gam_bc[:, 0:NSEL]),
                   reads=[("l3a_hbt", tt), gam_k], writes=[("l3a_sbt",)], name=f"{tpre}_sb")
            p.emit("dve", lambda e, dtt=dtt, sbt=sbt:
                   nc.vector.tensor_add(dtt[:, 0:NSEL], dtt[:, 0:NSEL], sbt),
                   reads=[("l3a_dttm", tt, "b"), ("l3a_sbt",)], writes=[("l3a_dttm", tt, "b2")], name=f"{tpre}_sba")
            na = ctx.scratch("l3a_na", [P, NHd], F32)
            p.emit("dve", lambda e, dtt=dtt, na=na:
                   nc.vector.scalar_tensor_tensor(na, dtt, -1.0, dtt, ALU.mult, ALU.min),
                   reads=[("l3a_dttm", tt, "b2")], writes=[("l3a_na",)], name=f"{tpre}_na")
            p.emit("act", lambda e, na=na: nc.scalar.activation(na, na, AF.Exp),
                   reads=[("l3a_na",)], writes=[("l3a_na",)], name=f"{tpre}_ex")
            p.emit("act", lambda e, na=na: nc.scalar.activation(na, na, AF.Ln, bias=1.0),
                   reads=[("l3a_na",)], writes=[("l3a_na",)], name=f"{tpre}_ln")
            p.emit("dve", lambda e, dtt=dtt: nc.vector.tensor_scalar_max(dtt, dtt, 0.0),
                   reads=[("l3a_dttm", tt, "b2")], writes=[("l3a_dttm", tt, "rl")], name=f"{tpre}_rl")
            p.emit("dve", lambda e, dtt=dtt, na=na: nc.vector.tensor_add(dtt, dtt, na),
                   reads=[("l3a_dttm", tt, "rl"), ("l3a_na",)], writes=[("l3a_dttm", tt)], name=f"{tpre}_sp")
            pbt, pbtk = ctx.psum()
            p.emit("pe", lambda e, dtt=dtt, pbt=pbt:
                   nc.tensor.transpose(pbt[0:NHd, 0:P], dtt, ctx.ident),
                   reads=[("l3a_dttm", tt), "c_ident"], writes=[pbtk], name=f"{tpre}_ttr")
            p.emit("act", lambda e, tt=tt, pbt=pbt: nc.scalar.copy(tfm[:, tt * P:(tt + 1) * P], pbt[0:NHd, 0:P]),
                   reads=[pbtk], writes=[("l3a_tfm", tt)], name=f"{tpre}_tte")
        p.emit("dve", lambda e:
               nc.vector.tensor_tensor_scan(tfm, tfm, zero1[0:NHd, :].to_broadcast([NHd, TC]), 0.0, ALU.add, ALU.add),
               reads=[("l3a_tfm", tt_) for tt_ in range(4)] + [("l3a_zero",)], writes=[("l3a_T",)], name=f"{pre}_Tcs")
        for tt in range(4):
            pbt, pbtk = ctx.psum()
            p.emit("pe", lambda e, tt=tt, pbt=pbt:
                   nc.tensor.transpose(pbt[:, 0:NHd], tfm[:, tt * P:(tt + 1) * P], ctx.ident[0:NHd, 0:NHd]),
                   reads=[("l3a_T",), "c_ident"], writes=[pbtk], name=f"{pre}_Ttr{tt}")
            p.emit("act", lambda e, tt=tt, pbt=pbt: nc.scalar.copy(tcolb[:, tt * NHd:(tt + 1) * NHd], pbt[:, 0:NHd]),
                   reads=[pbtk], writes=[("l3a_tcol", tt)], name=f"{pre}_Tte{tt}")
        p.dma("sp", tsep_d, tfm, reads=[("l3a_T",), ("l3a_tsep",)], writes=[("l3a_tsep",)], name=f"{pre}_tst")
        for ii in range(4):
            wv = TC - ii * P
            pb, pbk = ctx.psum()
            p.emit("pe", lambda e, ii=ii, wv=wv, pb=pb:
                   nc.tensor.matmul(pb[:, 0:wv], bcb[:, ii * P:(ii + 1) * P], bcb[:, TC + ii * P:2 * TC], start=True, stop=True),
                   reads=[("l3a_bcb", 0), ("l3a_bcb", 1)], writes=[pbk], name=f"{pre}_g{ii}")
            p.emit("act", lambda e, ii=ii, wv=wv, pb=pb: nc.scalar.copy(gt_sl(ii), pb[:, 0:wv]),
                   reads=[pbk], writes=[("l3a_gt", ii)], name=f"{pre}_ge{ii}")
            pbt, pbtk = ctx.psum_bf()
            p.emit("pe", lambda e, ii=ii, pbt=pbt:
                   nc.tensor.transpose(pbt[:, 0:P], bcb[:, ii * P:(ii + 1) * P], ctx.ident_bf),
                   reads=[("l3a_bcb", 0), "c_identb"], writes=[pbtk], name=f"{pre}_btr{ii}")
            p.emit("act", lambda e, ii=ii, pbt=pbt: nc.scalar.copy(btb[:, ii * P:(ii + 1) * P], pbt[:, 0:P]),
                   reads=[pbtk], writes=[("l3a_btb", ii)], name=f"{pre}_bte{ii}")
        for jt in range(DT):
            for hh in range(2):
                h = 2 * jt + hh
                pbt, pbtk = ctx.psum_bf()
                for ii in range(4):
                    p.emit("pe", lambda e, jt=jt, hh=hh, ii=ii, pbt=pbt:
                           nc.tensor.transpose(pbt[:, ii * HD:(ii + 1) * HD],
                                               xcb[hh * HD:(hh + 1) * HD, jt * TC + ii * P:jt * TC + (ii + 1) * P],
                                               ctx.ident_bf[hh * HD:(hh + 1) * HD, hh * HD:(hh + 1) * HD]),
                           reads=[("l3a_xc", jt), "c_identb", ("l3a_xt", h)], writes=[pbtk], name=f"{pre}_xtr{h}_{ii}")
                p.emit("act", lambda e, h=h, pbt=pbt: nc.scalar.copy(xtb[:, h * 4 * HD:(h + 1) * 4 * HD], pbt[:, 0:4 * HD]),
                       reads=[pbtk], writes=[("l3a_xt", h)], name=f"{pre}_xte{h}")
        if first:
            p.emit("dve", lambda e: nc.vector.memset(sT, 0.0),
                   reads=[("l3a_sT", h_) for h_ in range(NHd)],
                   writes=[("l3a_sT", h_) for h_ in range(NHd)], name=f"{pre}_s0")
            p.emit("dve", lambda e: nc.vector.memset(sTb16, 0.0),
                   reads=[("l3a_sTb", h_) for h_ in range(NHd)],
                   writes=[("l3a_sTb", h_) for h_ in range(NHd)], name=f"{pre}_s0b")
        for h in range(NHd):
            hq = h % 2
            hpre = f"{pre}h{h}"
            nA = negA[:, h:h + 1]
            trb = trbs[hq]
            trow = trow1
            p.dma("sp", trow, tsep_d[h:h + 1, :], reads=[("l3a_tsep",), ("l3a_trow",)],
                  writes=[("l3a_trow",)], name=f"{hpre}_trl")
            pbr, pbrk = ctx.psum()
            p.emit("pe", lambda e, pbr=pbr, trow=trow:
                   nc.tensor.matmul(pbr, ctx.ones_row, trow, start=True, stop=True),
                   reads=[("l3a_trow",), "c_ones"], writes=[pbrk], name=f"{hpre}_trb")
            p.emit("act", lambda e, pbr=pbr, trb=trb: nc.scalar.copy(trb, pbr),
                   reads=[pbrk], writes=[("l3a_trb", hq)], name=f"{hpre}_trbe")
            ypb, ypbk = ctx.psum_fixed(4 + hq)
            hb2 = (h % 2) * HD
            ysl = ypb[hb2:hb2 + HD, 0:TC]
            cdec = cdec1
            drow = ctx.scratch("l3a_z1_1", [P, TC], F32)
            p.emit("act", lambda e, nA=nA, drow=drow, trb=trb: nc.scalar.activation(drow, trb, AF.Exp, scale=nA),
                   reads=[("l3a_trb", hq), ("l3a_negA",)], writes=[("l3a_z1", 1)], name=f"{hpre}_dr")
            p.emit("dve", lambda e, drow=drow, cdec=cdec: nc.vector.tensor_mul(cdec, bcb[:, TC:2 * TC], drow),
                   reads=[("l3a_bcb", 1), ("l3a_z1", 1)], writes=[("l3a_cdec",)], name=f"{hpre}_cd")
            p.emit("pe", lambda e, h=h, ysl=ysl, hb2=hb2, cdec=cdec:
                   nc.tensor.matmul(ysl, sTb16[:, h * HD:(h + 1) * HD], cdec, start=True, stop=False, tile_position=(0, hb2)),
                   reads=[("l3a_sTb", h), ("l3a_cdec",)], writes=[ypbk], name=f"{hpre}_yi")
            for ii in range(4):
                iq = ii % 2
                wv = TC - ii * P
                z1 = ctx.scratch(f"l3a_z1_{iq}", [P, TC], F32)
                e2 = ctx.scratch(f"l3a_e2_{iq}", [P, TC], F32)
                mt = ctx.scratch(f"l3a_mt_{iq}", [P, TC], BF16)
                p.emit("dve", lambda e, ii=ii, wv=wv, z1=z1, h=h, trb=trb:
                       nc.vector.tensor_scalar(z1[:, 0:wv], trb[:, ii * P:TC], tcolb[:, ii * NHd + h:ii * NHd + h + 1], 0.0, ALU.subtract, ALU.max),
                       reads=[("l3a_trb", hq), ("l3a_tcol", ii)], writes=[("l3a_z1", iq)], name=f"{hpre}_z{ii}")
                p.emit("act", lambda e, wv=wv, z1=z1, e2=e2, nA=nA:
                       nc.scalar.activation(e2[:, 0:wv], z1[:, 0:wv], AF.Exp, scale=nA),
                       reads=[("l3a_z1", iq), ("l3a_negA",)], writes=[("l3a_e2", iq)], name=f"{hpre}_e{ii}")
                p.emit("dve", lambda e, wv=wv, e2=e2, mt=mt, ii=ii, h=h:
                       nc.vector.scalar_tensor_tensor(mt[:, 0:wv], e2[:, 0:wv], dttm[:, ii * NHd + h:ii * NHd + h + 1],
                                                      gt_sl(ii), ALU.mult, ALU.mult),
                       reads=[("l3a_e2", iq), ("l3a_dttm", ii), ("l3a_gt", ii)], writes=[("l3a_mt", iq)], name=f"{hpre}_m{ii}")
                p.emit("dve", lambda e, mt=mt:
                       nc.vector.tensor_mul(mt[:, 0:P], mt[:, 0:P], tri01),
                       reads=[("l3a_mt", iq), ("l3a_tri",)], writes=[("l3a_mt", iq)], name=f"{hpre}_tm{ii}")
                p.emit("pe", lambda e, ii=ii, wv=wv, h=h, ysl=ypb, hb2=hb2, mt=mt:
                       nc.tensor.matmul(ypb[hb2:hb2 + HD, ii * P:TC], xt_tile(h, ii), mt[:, 0:wv],
                                        start=False, stop=(ii == 3), tile_position=(0, hb2)),
                       reads=[("l3a_mt", iq), ("l3a_xt", h)], writes=[ypbk], name=f"{hpre}_ym{ii}")
            p.emit("dve", lambda e, h=h, hb2=hb2, ypb=ypb:
                   nc.vector.scalar_tensor_tensor(yg[hb2 + (h // 2) * 0:0, :] if False else yg.rearrange("p (j t) -> p j t", t=TC)[hb2:hb2 + HD, h // 2, :],
                                                  xcb.rearrange("p (j t) -> p j t", t=TC)[hb2:hb2 + HD, h // 2, :],
                                                  dpar_bc[hb2:hb2 + HD, h:h + 1], ypb[hb2:hb2 + HD, 0:TC], ALU.mult, ALU.add),
                   reads=[ypbk, ("l3a_xc", h // 2), dpar_k, ("l3a_ygB",)], writes=[("l3a_yg", h), ("l3a_ygB",)], name=f"{hpre}_ye")
            spb, spbk = ctx.psum()
            for ii in range(4):
                iq = ii % 2
                dcol = dcols[iq]
                xw = xws[iq]
                p.emit("dve", lambda e, ii=ii, h=h, dcol=dcol, trb=trb:
                       nc.vector.tensor_scalar(dcol, trb[:, TC - 1:TC], tcolb[:, ii * NHd + h:ii * NHd + h + 1], None, ALU.subtract),
                       reads=[("l3a_tcol", ii), ("l3a_trb", hq)], writes=[("l3a_dcol", iq)], name=f"{hpre}_w{ii}a")
                p.emit("act", lambda e, nA=nA, dcol=dcol: nc.scalar.activation(dcol, dcol, AF.Exp, scale=nA),
                       reads=[("l3a_dcol", iq), ("l3a_negA",)], writes=[("l3a_dcol", iq)], name=f"{hpre}_w{ii}b")
                p.emit("dve", lambda e, ii=ii, h=h, dcol=dcol:
                       nc.vector.tensor_mul(dcol, dcol, dttm[:, ii * NHd + h:ii * NHd + h + 1]),
                       reads=[("l3a_dcol", iq), ("l3a_dttm", ii)], writes=[("l3a_dcol", iq)], name=f"{hpre}_w{ii}c")
                p.emit("dve", lambda e, ii=ii, h=h, dcol=dcol, xw=xw:
                       nc.vector.tensor_scalar_mul(xw, xt_tile(h, ii), dcol),
                       reads=[("l3a_xt", h), ("l3a_dcol", iq)], writes=[("l3a_xw", iq)], name=f"{hpre}_w{ii}d")
                p.emit("pe", lambda e, ii=ii, h=h, spb=spb, xw=xw:
                       nc.tensor.matmul(spb[:, 0:HD], btb[:, ii * P:(ii + 1) * P], xw,
                                        start=(ii == 0), stop=(ii == 3)),
                       reads=[("l3a_btb", ii), ("l3a_xw", iq)], writes=[spbk], name=f"{hpre}_sm{ii}")
            e3c = e3cs[hq]
            stmp = stmp1
            p.emit("act", lambda e, h=h, nA=nA, e3c=e3c, trb=trb:
                   nc.scalar.activation(e3c, trb[:, TC - 1:TC], AF.Exp, scale=nA),
                   reads=[("l3a_trb", hq), ("l3a_negA",)], writes=[("l3a_e3", hq)], name=f"{hpre}_de")
            p.emit("dve", lambda e, h=h, e3c=e3c, stmp=stmp:
                   nc.vector.tensor_scalar_mul(stmp, sT[:, h * HD:(h + 1) * HD], e3c),
                   reads=[("l3a_sT", h), ("l3a_e3", hq)], writes=[("l3a_stmp",)], name=f"{hpre}_sd")
            p.emit("dve", lambda e, h=h, spb=spb, stmp=stmp:
                   nc.vector.tensor_add(sT[:, h * HD:(h + 1) * HD], stmp, spb[:, 0:HD]),
                   reads=[("l3a_stmp",), spbk], writes=[("l3a_sT", h)], name=f"{hpre}_su")
            p.emit("dve", lambda e, h=h:
                   nc.vector.tensor_copy(sTb16[:, h * HD:(h + 1) * HD], sT[:, h * HD:(h + 1) * HD]),
                   reads=[("l3a_sT", h)], writes=[("l3a_sTb", h)], name=f"{hpre}_sb")
        sq8 = srb
        for j in range(DT):
            p.emit("dve", lambda e, j=j:
                   nc.vector.tensor_mul(yg[:, j * TC:(j + 1) * TC], yg[:, j * TC:(j + 1) * TC], zb[:, j * TC:(j + 1) * TC]),
                   reads=[("l3a_yg", 2 * j), ("l3a_yg", 2 * j + 1), ("l3a_zb", j), ("l3a_ygB",)],
                   writes=[("l3a_yz", j), ("l3a_ygB",)], name=f"{pre}_yz{j}")
            p.emit("act", lambda e, j=j: nc.scalar.activation(sq8[:, j * TC:(j + 1) * TC], yg[:, j * TC:(j + 1) * TC], AF.Square),
                   reads=[("l3a_yz", j), ("l3a_ygB",), ("l3a_srB",)], writes=[("l3a_sq8", j), ("l3a_sr", j), ("l3a_srB",)], name=f"{pre}_sq{j}")
            if L3A_DEBUG:
                p.dma("pool", dbg_d[(ch * DT + j) * P:(ch * DT + j + 1) * P, :], yg[:, j * TC:(j + 1) * TC],
                      reads=[("l3a_yz", j), ("l3a_ygB",)], writes=[("l3a_dbgw", ch, j)], name=f"{pre}_dbg{j}")
        pbS, pbSk = ctx.psum()
        ctx.mm(pbS[0:1, 0:TC], pbSk, [(ctx.ones_col_bf, "c_onesb")] * DT,
               [(sq8[:, j * TC:(j + 1) * TC], ("l3a_sq8", j)) for j in range(DT)], name=f"{pre}_ssq",
               extra_reads=[("l3a_srB",)])
        ms_row = rows3[:, 0:TC]
        p.emit("dve", lambda e, ms_row=ms_row, pbS=pbS: nc.vector.tensor_scalar_mul(
                   ms_row, pbS[0:1, 0:TC], 1.0 / DM),
               reads=[pbSk], writes=[("l3a_ms",)], name=f"{pre}_ms")
        if L3A_DEBUG:
            p.dma("pool", dbg4_d[ch:ch + 1, :], ms_row, reads=[("l3a_ms",)],
                  writes=[("l3a_dbg4", ch)], name=f"{pre}_dbgms")
            msact = ctx.scratch("l3a_trow", [1, TC], F32)
            p.emit("act", lambda e, pbS=pbS, msact=msact: nc.scalar.copy(msact, pbS[0:1, 0:TC]),
                   reads=[pbSk, ("l3a_trow",)], writes=[("l3a_trow",)], name=f"{pre}_dbgac")
            p.dma("pool", dbg6_d[ch:ch + 1, :], msact, reads=[("l3a_trow",)],
                  writes=[("l3a_dbg6", ch)], name=f"{pre}_dbgm6")
            p.dma("pool", dbg5_d[ch * P:(ch + 1) * P, :], sq8[:, 0:TC],
                  reads=[("l3a_sq8", 0), ("l3a_srB",)], writes=[("l3a_dbg5", ch)], name=f"{pre}_dbgsq")
        epsc, epsk = ctx.constcol(EPS)
        p.emit("act", lambda e, ms_row=ms_row: nc.scalar.activation(
                   ms_row, ms_row, AF.Sqrt, bias=epsc[0:1, :]),
               reads=[("l3a_ms",), epsk], writes=[("l3a_ms2",), ("l3a_ms",)], name=f"{pre}_msq")
        p.emit("dve", lambda e, ms_row=ms_row: nc.vector.reciprocal(ms_row, ms_row),
               reads=[("l3a_ms2",)], writes=[("l3a_ms3",)], name=f"{pre}_mrc")
        if L3A_DEBUG:
            p.dma("pool", dbg2_d[ch:ch + 1, :], ms_row,
                  reads=[("l3a_ms3",)], writes=[("l3a_dbg2", ch)], name=f"{pre}_dbgm")
        rb_, rbk_ = ctx.bcast_row(ms_row, ("l3a_ms3",), TC, name=f"{pre}_rbc")
        for j in range(DT):
            t1 = ctx.scratch("l3a_cst", [P, TC], F32)
            p.emit("dve", lambda e, j=j, t1=t1, rb_=rb_:
                   nc.vector.tensor_mul(t1, yg[:, j * TC:(j + 1) * TC], rb_),
                   reads=[("l3a_yz", j), ("l3a_ygB",), rbk_, ("l3a_cst",)], writes=[("l3a_cst",)], name=f"{pre}_on{j}")
            p.emit("dve", lambda e, j=j, t1=t1:
                   nc.vector.tensor_scalar_mul(xcb[:, j * TC:(j + 1) * TC], t1, nw_c[:, j:j + 1]),
                   reads=[("l3a_cst",), nw_k, ("l3a_xc", j)], writes=[("l3a_xc", j)], name=f"{pre}_ow{j}")
        if L3A_DEBUG:
            for j in range(DT):
                p.dma("pool", dbg3_d[(ch * DT + j) * P:(ch * DT + j + 1) * P, :], xcb[:, j * TC:(j + 1) * TC],
                      reads=[("l3a_xc", j)], writes=[("l3a_dbg3", ch, j)], name=f"{pre}_dbn{j}")
        on_rhs = [(xcb[:, j * TC:(j + 1) * TC], ("l3a_xc", j)) for j in range(DT)]
        for m in range(DT):
            pb, pbk = ctx.psum()
            ctx.mm(pb, pbk, [(wop_t(j, m * P, (m + 1) * P), ("l3a_wop", j, m)) for j in range(DT)],
                   on_rhs, name=f"{pre}_op{m}")
            xres = ctx.scratch(f"l3a_z1_{m % 2}", [P, TC], F32)
            p.dma("sp", xres, xin_d[m * P:(m + 1) * P, t0:t0 + TC],
                  reads=[("l3a_z1", m % 2)], writes=[("l3a_z1", m % 2)], name=f"{pre}_lr{m}")
            xo = ctx.scratch(f"l3a_e2_{m % 2}", [P, TC], F32)
            p.emit("dve", lambda e, pb=pb, m=m, xo=xo, xres=xres:
                   nc.vector.tensor_add(xo, pb, xres),
                   reads=[pbk, ("l3a_z1", m % 2), ("l3a_e2", m % 2)], writes=[("l3a_e2", m % 2)], name=f"{pre}_res{m}")
            p.dma("sp", xout_d[m * P:(m + 1) * P, t0:t0 + TC], xo,
                  reads=[("l3a_e2", m % 2)], writes=[("xo", ch, m), ("l3a_e2", m % 2)], name=f"{pre}_st{m}")



SUBLAYERS = {
    "L2_ffn": (emit_l2_ffn, ["W_r", "W_k", "W_v", "mu_r", "mu_k"]),
    "L0_attn": (emit_l0_attn, ["W_in", "conv_w", "conv_b", "W_xproj", "W_dt",
                               "b_dt", "A_log", "D", "W_out"]),
    "L2_attn": (emit_l2_attn, ["W_in", "b_in", "conv_w", "conv_b", "W_x", "b_x",
                               "A_log", "D_param", "W_out", "b_out"]),
    "L1_ffn": (emit_l1_ffn, ["W1", "b1", "W2", "b2", "Wg", "bg"]),
    "L1_attn": (emit_l1_attn, ["Wq", "Wk", "Wv", "Wfc"]),
    "L3_attn": (emit_l3_attn, ["in_proj_w", "h_proj_w", "conv_w", "conv_b", "dt_bias",
                               "gamma", "A_log", "D", "norm_w", "out_proj_w"]),
    "L0_ffn": (emit_l0_ffn, ["qkv_w", "qkv_b", "out_w", "out_b", "norm1_w", "norm1_b",
                             "sw1", "sb1", "sw2", "sb2", "rw1", "rb1", "rw2", "rb2",
                             "centroid", "route_bias", "norm2_w", "norm2_b"]),
    "L3_ffn": (emit_l3_ffn, ["Wr", "br", "W1", "b1", "W2", "b2"]),
}

WEIGHT_SHAPES = {
    "L2_ffn": {"W_r": [DM, DM], "W_k": [DM, DM], "W_v": [DM, DM],
               "mu_r": [DM], "mu_k": [DM]},
    "L0_attn": {"W_in": [DM, 4096], "conv_w": [2048, 4], "conv_b": [2048],
                "W_xproj": [2048, 2080], "W_dt": [2048, 2048], "b_dt": [2048],
                "A_log": [2048, 16], "D": [2048], "W_out": [2048, DM]},
    "L2_attn": {"W_in": [DM, 2048], "b_in": [2048], "conv_w": [DM, 4], "conv_b": [DM],
                "W_x": [DM, 32], "b_x": [32], "A_log": [DM, 16], "D_param": [DM],
                "W_out": [DM, DM], "b_out": [DM]},
    "L1_ffn": {"W1": [8 * DM, 4096], "b1": [8, 4096], "W2": [8 * 4096, DM],
               "b2": [8, DM], "Wg": [DM, 8], "bg": [8]},
    "L1_attn": {"Wq": [DM, DM], "Wk": [DM, DM], "Wv": [DM, DM], "Wfc": [DM, DM]},
    "L3_attn": {"in_proj_w": [2336, DM], "h_proj_w": [40, 1056], "conv_w": [1280, 4],
                "conv_b": [1280], "dt_bias": [1], "gamma": [16], "A_log": [16],
                "D": [16], "norm_w": [DM], "out_proj_w": [DM, DM]},
    "L0_ffn": {"qkv_w": [DM, 3072], "qkv_b": [3072], "out_w": [DM, DM], "out_b": [DM],
               "norm1_w": [DM], "norm1_b": [DM], "sw1": [DM, 2048], "sb1": [1, 2048],
               "sw2": [2048, DM], "sb2": [1, DM], "rw1": [8 * DM, 2048], "rb1": [8, 2048],
               "rw2": [8 * 2048, DM], "rb2": [8, DM], "centroid": [8, DM],
               "route_bias": [8], "norm2_w": [DM], "norm2_b": [DM]},
    "L3_ffn": {"Wr": [DM, 8], "br": [8], "W1": [8 * DM, 4096], "b1": [8, 4096],
               "W2": [8 * 4096, DM], "b2": [8, DM]},
}


def build_sub_nc(sub_name):
    nc = bass.Bass("TRN2", target_bir_lowering=False)
    p = Prog(nc)
    ctx = Ctx(nc, p)
    xin = nc.dram_tensor("xin", [DM, TOK], F32, kind="ExternalInput").ap()
    xout = nc.dram_tensor("xout", [DM, TOK], F32, kind="ExternalOutput").ap()
    emit, wnames = SUBLAYERS[sub_name]
    shapes = WEIGHT_SHAPES[sub_name]
    W = {w: nc.dram_tensor("w_" + w, list(shapes[w]), F32, kind="ExternalInput").ap()
         for w in wnames}
    emit(ctx, xin, xout, W)
    p.build()
    return nc


def run_sub(sub_name, x_fm_shards, weights, cores=None):
    nc = build_sub_nc(sub_name)
    _, wnames = SUBLAYERS[sub_name]
    shapes = WEIGHT_SHAPES[sub_name]
    wprep = {("w_" + w): np.ascontiguousarray(
                 np.asarray(weights[w], np.float32).reshape(shapes[w]))
             for w in wnames}
    in_maps = []
    for s in x_fm_shards:
        m = {"xin": s}
        m.update(wprep)
        in_maps.append(m)
    if cores is None:
        cores = list(range(len(x_fm_shards)))
    res = run_bass_kernel_spmd(nc, in_maps, core_ids=cores)
    return [np.asarray(r["xout"], np.float32) for r in res.results]



SUB_ORDER = ["L0_attn", "L0_ffn", "L1_attn", "L1_ffn",
             "L2_attn", "L2_ffn", "L3_attn", "L3_ffn"]

N_CORES = 8


def build_full_nc():
    nc = bass.Bass("TRN2", target_bir_lowering=False)
    p = Prog(nc)
    ctx = Ctx(nc, p)
    xin = nc.dram_tensor("xin", [DM, TOK], F32, kind="ExternalInput").ap()
    xout = nc.dram_tensor("xout", [DM, TOK], F32, kind="ExternalOutput").ap()
    acts = [nc.dram_tensor(f"act{i}", [DM, TOK], F32, kind="Internal").ap()
            for i in range(len(SUB_ORDER) - 1)]
    bufs = [xin] + acts + [xout]
    for i, sub in enumerate(SUB_ORDER):
        if i > 0:
            ctx.arena_reset()
        emit, wnames = SUBLAYERS[sub]
        shapes = WEIGHT_SHAPES[sub]
        W = {w: nc.dram_tensor(f"w_{sub}_{w}", list(shapes[w]),
                               F32, kind="ExternalInput").ap()
             for w in wnames}
        emit(ctx, bufs[i], bufs[i + 1], W)
    p.build()
    return nc


def prep_full_weights(inputs):
    wprep = {}
    for sub in SUB_ORDER:
        _, wnames = SUBLAYERS[sub]
        shapes = WEIGHT_SHAPES[sub]
        for w in wnames:
            wprep[f"w_{sub}_{w}"] = np.ascontiguousarray(
                np.asarray(inputs[f"{sub}_{w}"], np.float32).reshape(shapes[w]))
    return wprep


def kernel(**inputs):
    x = np.asarray(inputs["L0_attn_x"], np.float32)
    B, S, D = x.shape
    bpc = B // N_CORES
    shards = [np.ascontiguousarray(x[c * bpc:(c + 1) * bpc].reshape(bpc * S, D).T)
              for c in range(N_CORES)]
    nc = build_full_nc()
    wprep = prep_full_weights(inputs)
    in_maps = []
    for s in shards:
        m = {"xin": s}
        m.update(wprep)
        in_maps.append(m)
    res = run_bass_kernel_spmd(nc, in_maps, core_ids=list(range(N_CORES)))
    outs = [np.asarray(r["xout"], np.float32) for r in res.results]
    return np.concatenate(
        [s.T.reshape(bpc, S, D) for s in outs], axis=0).astype(np.float32)

